# version 12
# baseline (speedup 1.0000x reference)
import numpy as np
from contextlib import ExitStack
import concourse.bass as bass
import concourse.mybir as mybir
from concourse.bass_utils import run_bass_kernel_spmd

F32 = mybir.dt.float32
BF16 = mybir.dt.bfloat16
I32 = mybir.dt.int32
AX = mybir.AxisListType.X
OP = mybir.AluOpType
ACT = mybir.ActivationFunctionType

NCORES = 8
L = 16384
T = L // NCORES
D = 1024
E = 2048
S = 128
F_IN = 2 * E + S
I_FF = 2816
NI = I_FF // 128
LN_EPS = 1e-5
NT = T // 128
KS = 512
NSUP = L // KS
NPASS = 4
QP = T // NPASS
INV_L = 1.0 / L


class Rec:
    def __init__(self, name):
        self.name = name
        self.items = []

    def wait(self, sem, val=None):
        v = sem.n if val is None else val
        assert v <= sem.n, f"{self.name}: wait {sem.name}>={v} > recorded {sem.n}"
        if v > 0:
            self.items.append(("w", sem, v))

    def op(self, fn, sem=None, amt=1):
        self.items.append(("o", fn, sem, amt if sem else 0))
        if sem is not None:
            sem.n += amt
            return sem.n
        return None


class Sem:
    def __init__(self, stack, nc, name):
        self.h = stack.enter_context(nc.semaphore(name))
        self.name = name
        self.n = 0


def build_program(debug=False):
    nc = bass.Bass("TRN2", target_bir_lowering=False, debug=False)

    dx = nc.dram_tensor("x", [T, D], F32, kind="ExternalInput").ap()
    d_ln_g = nc.dram_tensor("ln_g", [1, D], F32, kind="ExternalInput").ap()
    d_ln_b = nc.dram_tensor("ln_b", [1, D], F32, kind="ExternalInput").ap()
    d_Win = nc.dram_tensor("W_in", [D, F_IN], F32, kind="ExternalInput").ap()
    d_bin = nc.dram_tensor("b_in", [1, F_IN], F32, kind="ExternalInput").ap()
    d_Wout = nc.dram_tensor("W_out", [E, D], F32, kind="ExternalInput").ap()
    d_bout = nc.dram_tensor("b_out", [1, D], F32, kind="ExternalInput").ap()
    d_gq = nc.dram_tensor("gq", [S, 1], F32, kind="ExternalInput").ap()
    d_bq = nc.dram_tensor("bq", [S, 1], F32, kind="ExternalInput").ap()
    d_gk = nc.dram_tensor("gk", [S, 1], F32, kind="ExternalInput").ap()
    d_bk = nc.dram_tensor("bk", [S, 1], F32, kind="ExternalInput").ap()
    d_clip_x = nc.dram_tensor("clip_x", [1, 1], F32, kind="ExternalInput").ap()
    d_clip_d = nc.dram_tensor("clip_d", [1, 1], F32, kind="ExternalInput").ap()
    d_wg = nc.dram_tensor("w_gate", [I_FF, D], F32, kind="ExternalInput").ap()
    d_wu = nc.dram_tensor("w_up", [I_FF, D], F32, kind="ExternalInput").ap()
    d_wd = nc.dram_tensor("w_down", [D, I_FF], F32, kind="ExternalInput").ap()
    d_id = nc.dram_tensor("ident", [128, 128], F32, kind="ExternalInput").ap()
    d_out = nc.dram_tensor("out", [T, D], F32, kind="ExternalOutput").ap()
    d_dbg = nc.dram_tensor("dbg", [16 * 128, 2048], F32, kind="ExternalOutput").ap() if debug else None
    sDBG = [None]

    zg_in = nc.dram_tensor("zg_in", [S, T], BF16).ap()
    zg_out = nc.dram_tensor("zg_out", [S * NCORES, T], BF16, addr_space="Shared").ap()
    vg_in = nc.dram_tensor("vg_in", [T, E], BF16).ap()
    vg_outA = nc.dram_tensor("vg_outA", [L // 2, E], BF16, addr_space="Shared").ap()
    vg_outB = nc.dram_tensor("vg_outB", [L // 2, E], BF16, addr_space="Shared").ap()
    u_hbm = nc.dram_tensor("u_hbm", [T, E], BF16).ap()
    uav_hbm = nc.dram_tensor("uav_hbm", [T, E], F32).ap()
    x2_hbm = nc.dram_tensor("x2_hbm", [T, D], F32).ap()
    vh_hbm = nc.dram_tensor("vh_hbm", [I_FF, T], F32).ap()
    mx_in = nc.dram_tensor("mx_in", [128, 1], F32).ap()
    mx_out = nc.dram_tensor("mx_out", [128, 1], F32).ap()
    mh_in = nc.dram_tensor("mh_in", [128, 1], F32).ap()
    mh_out = nc.dram_tensor("mh_out", [128, 1], F32).ap()
    sc_hbm = nc.dram_tensor("sc_hbm", [1, 8], F32).ap()
    sd_hbm = nc.dram_tensor("sd_hbm", [1, D], F32).ap()

    RG = [list(range(NCORES))]
    SY, VE, SC, TE, GP = Rec("sync"), Rec("vector"), Rec("scalar"), Rec("tensor"), Rec("gpsimd")
    RECS = [SY, VE, SC, TE, GP]

    with ExitStack() as st0:
        def sem(name):
            return Sem(st0, nc, name)

        def sb(stk, name, shape, dtype):
            return stk.enter_context(nc.sbuf_tensor(name, shape, dtype))

        sT = sem("sT"); sV = sem("sV"); sS = sem("sS"); sCC = sem("sCC")
        sBC = sem("sBC")
        sXT = [sem("sXT0"), sem("sXT1"), sem("sXT2")]
        sWI = sem("sWI")
        sTRXN = sem("sTRXN")
        sUAVT = [sem("sUAVT0"), sem("sUAVT1")]
        sTRXQ = sem("sTRXQ")
        sTRKW = [sem("sTRKW0"), sem("sTRKW1")]
        sTRKD = sem("sTRKD")
        sSTZ = sem("sSTZ")
        sSTV = sem("sSTV")
        sSTU = sem("sSTU")
        sX2ST = [sem("sX2ST0"), sem("sX2ST1")]
        sSD = sem("sSD")
        sVL = [sem("sVL0"), sem("sVL1")]
        sGZ = sem("sGZ")
        sUL = [sem("sUL0"), sem("sUL1")]
        sWO = [sem("sWO0"), sem("sWO1")]
        sUT = [sem("sUT0"), sem("sUT1"), sem("sUT2")]
        sUL3 = [sem("sUL3_0"), sem("sUL3_1"), sem("sUL3_2")]
        sXR = [sem("sXR0"), sem("sXR1"), sem("sXR2")]
        sXQ = [sem("sXQ0"), sem("sXQ1"), sem("sXQ2")]
        sWS = [sem("sWS0"), sem("sWS1"), sem("sWS2"), sem("sWS3")]
        sWD = sem("sWD")
        sVH = [sem("sVH0"), sem("sVH1"), sem("sVH2")]
        sVHL = [sem("sVHL0"), sem("sVHL1"), sem("sVHL2"), sem("sVHL3")]
        sX2R = [sem("sX2R0"), sem("sX2R1"), sem("sX2R2")]
        sSCB = sem("sSCB")
        sOUTs = [sem("sOUT0"), sem("sOUT1"), sem("sOUT2")]
        sDBG[0] = sem("sDBG")
        ALL_SEMS = [sT, sV, sS, sCC, sBC, sWI, sTRXN, sTRXQ, sTRKD, sSTZ, sSTV,
                    sSTU, sSD, sGZ, sSCB, sWD] + sXT + sUAVT + sTRKW + sX2ST + \
                   sVL + sUL + sWO + sUT + sUL3 + sXR + sXQ + sWS + sVH + sVHL + sX2R + sOUTs + [sDBG[0]]

        ps_all = st0.enter_context(nc.psum_tensor("ps_all", [128, 4096], F32))
        ps = [ps_all[0:128, 512 * b:512 * (b + 1)] for b in range(8)]
        bankfree = {}

        def te_bank(b):
            if b in bankfree:
                TE.wait(*bankfree[b])

        ident = sb(st0, "ident_sb", [128, 128], F32)
        cols2 = sb(st0, "cols2", [128, 48], F32)
        tr_state = {"n": 0}
        stats = sb(st0, "stats", [128, 64], F32)
        cols = sb(st0, "cols", [128, 16], F32)
        one1 = sb(st0, "one1", [1, 16], F32)
        row128 = sb(st0, "row128", [1, 128], F32)
        scrS = sb(st0, "scrS", [128, 1024], F32)

        C_GQ, C_BQ, C_GK, C_BK, C_BZ, C_MX, C_MH, C_RSX, C_RSH, C_SX, C_SH = range(11)
        O_CLX, O_CLD, O_M, O_SIG, O_SX, O_RSX, O_T0, O_SH, O_RSH = range(9)

        def dma(rec, out_ap, in_ap, semx, transpose=False):
            rec.op(lambda e, o=out_ap, i=in_ap, tr=transpose:
                   e.dma_start(out=o, in_=i, transpose=tr), semx, 16)
            return semx.n

        def fill(n, bank=0):
            te_bank(bank)
            for _ in range(n):
                TE.op(lambda e, b=bank:
                      e.matmul(ps_all[0:128, 512 * b:512 * b + 128],
                               ident[:], ident[:], start=True, stop=True), None)

        def dbg_tap(k, src_ap, width, wait_sem=None, wait_val=None):
            if d_dbg is None:
                return
            if wait_sem is not None:
                GP.wait(wait_sem, wait_val)
            dma(GP, d_dbg[128 * k:128 * (k + 1), 0:width], src_ap, sDBG[0])

        def te_transpose(dst_bf, src_f32, nblk, src_step=128, dst_step=128, banks=(6, 7)):
            m = None
            for j in range(nblk):
                b = banks[tr_state["n"] % len(banks)]
                tr_state["n"] += 1
                te_bank(b)
                TE.op(lambda e, jj=j, bb=b, s=src_f32, st_=src_step:
                      e.transpose(ps_all[0:128, 512 * bb:512 * bb + 128],
                                  s[0:128, st_ * jj:st_ * jj + 128], ident[:]), sT)
                mt = sT.n
                SC.wait(sT, mt)
                SC.wait(sS, sS.n)
                m = SC.op(lambda e, jj=j, bb=b, d=dst_bf, st_=dst_step:
                          e.copy(out=d[0:128, st_ * jj:st_ * jj + 128], in_=ps_all[0:128, 512 * bb:512 * bb + 128]), sS)
                bankfree[b] = (sS, m)
            return m

        def barrier(exclude=()):
            snap = [(s, s.n) for s in ALL_SEMS if s.n > 0 and s not in exclude]
            for rec in RECS:
                for s, v in snap:
                    rec.wait(s, v)

        def ln_rstd(src_col, dst_col):
            t0, t1, t2, t3 = (stats[:, c:c + 1] for c in (60, 61, 62, 63))
            VE.wait(sV, sV.n)
            mvar = VE.op(lambda e, s=src_col: e.tensor_copy(out=t0, in_=s), sV)
            SC.wait(sV, mvar)
            SC.wait(sS, sS.n)
            msq = SC.op(lambda e: e.activation(out=t1, in_=t0, func=ACT.Sqrt), sS)
            VE.wait(sS, msq)
            VE.op(lambda e: e.reciprocal(out=t2, in_=t1), sV)
            VE.wait(sV, sV.n)
            VE.op(lambda e: e.tensor_tensor(out=t3, in0=t2, in1=t2, op=OP.mult), sV)
            VE.wait(sV, sV.n)
            VE.op(lambda e, s=src_col: e.tensor_tensor(out=t1, in0=t3, in1=s, op=OP.mult), sV)
            VE.wait(sV, sV.n)
            VE.op(lambda e: e.tensor_scalar(out=t1, in0=t1, scalar1=3.0, scalar2=-0.5,
                                            op0=OP.subtract, op1=OP.mult), sV)
            VE.wait(sV, sV.n)
            return VE.op(lambda e, d=dst_col: e.tensor_tensor(out=d, in0=t2, in1=t1, op=OP.mult), sV)

        def ln_stats(a_ap, mu_col, rstd_col):
            s0, s1, s2, s3 = (stats[:, c:c + 1] for c in (8, 9, 10, 11))
            VE.wait(sV, sV.n)
            VE.op(lambda e, a=a_ap: e.reduce_sum(out=s0, in_=a, axis=AX), sV)
            SC.wait(sV, sV.n)
            SC.wait(sS, sS.n)
            msq = SC.op(lambda e, a=a_ap: e.activation(out=scrS[:], in_=a, func=ACT.Square,
                                                       accum_out=s1), sS)
            VE.wait(sV, sV.n)
            VE.op(lambda e, m=mu_col: e.tensor_scalar_mul(m, s0, 1.0 / D), sV)
            VE.wait(sS, msq)
            VE.wait(sV, sV.n)
            VE.op(lambda e: e.tensor_scalar_mul(s2, s1, 1.0 / D), sV)
            VE.wait(sV, sV.n)
            VE.op(lambda e, m=mu_col: e.tensor_tensor(out=s3, in0=m, in1=m, op=OP.mult), sV)
            VE.wait(sV, sV.n)
            VE.op(lambda e: e.tensor_scalar(out=s2, in0=s2, scalar1=s3, scalar2=LN_EPS,
                                            op0=OP.subtract, op1=OP.add), sV)
            return ln_rstd(s2, rstd_col)

        def recip_nr(src, dst):
            t = one1[0:1, O_T0:O_T0 + 1]
            VE.wait(sV, sV.n)
            VE.op(lambda e, s=src, d=dst: e.reciprocal(out=d, in_=s), sV)
            VE.wait(sV, sV.n)
            VE.op(lambda e, s=src, d=dst: e.tensor_tensor(out=t, in0=s, in1=d, op=OP.mult), sV)
            VE.wait(sV, sV.n)
            VE.op(lambda e: e.tensor_scalar(out=t, in0=t, scalar1=2.0, scalar2=-1.0,
                                            op0=OP.subtract, op1=OP.mult), sV)
            VE.wait(sV, sV.n)
            return VE.op(lambda e, d=dst: e.tensor_tensor(out=d, in0=d, in1=t, op=OP.mult), sV)

        def rw_from_max(mx_col, rw_col):
            r0, tt = stats[:, 17:18], stats[:, 18:19]
            VE.wait(sV, sV.n)
            VE.op(lambda e, m=mx_col: e.reciprocal(out=r0, in_=m), sV)
            VE.wait(sV, sV.n)
            VE.op(lambda e, m=mx_col: e.tensor_tensor(out=tt, in0=m, in1=r0, op=OP.mult), sV)
            VE.wait(sV, sV.n)
            VE.op(lambda e: e.tensor_scalar(out=tt, in0=tt, scalar1=2.0, scalar2=-1.0,
                                            op0=OP.subtract, op1=OP.mult), sV)
            VE.wait(sV, sV.n)
            VE.op(lambda e: e.tensor_tensor(out=r0, in0=r0, in1=tt, op=OP.mult), sV)
            VE.wait(sV, sV.n)
            return VE.op(lambda e, d=rw_col: e.tensor_scalar_mul(d, r0, 127.0), sV)

        def quant(in_f32, rcol, itile, f32tmp, out_bf, pre_sc_waits=(), pre_ve_waits=()):
            for w_ in pre_ve_waits:
                VE.wait(*w_)
            VE.wait(sV, sV.n)
            m1 = VE.op(lambda e, s=in_f32, d=f32tmp, r=rcol:
                       e.tensor_scalar(out=d, in0=s, scalar1=r, scalar2=-128.0,
                                       op0=OP.mult, op1=OP.max), sV)
            VE.wait(sV, sV.n)
            m2 = VE.op(lambda e, s=f32tmp, d=itile:
                       e.tensor_scalar(out=d, in0=s, scalar1=127.0, scalar2=None,
                                       op0=OP.min), sV)
            for w_ in pre_sc_waits:
                SC.wait(*w_)
            SC.wait(sV, m2)
            SC.wait(sS, sS.n)
            return SC.op(lambda e, s=itile, d=out_bf: e.copy(out=d, in_=s), sS)

        stA = ExitStack()
        scr1 = sb(stA, "scr1", [128, 1024], F32)
        q_fm = sb(stA, "q_fm", [128, T], BF16)
        k_fm = sb(stA, "k_fm", [128, L], BF16)

        stP12 = ExitStack()
        xt = [sb(stP12, f"xt{i}", [128, D], F32) for i in range(3)]
        lng_bc = sb(stP12, "lng_bc", [128, D], F32)
        lnb_bc = sb(stP12, "lnb_bc", [128, D], F32)
        xn3 = sb(stP12, "xn3", [128, D], F32)
        xn_fm = sb(stP12, "xn_fm", [128, 8 * T], BF16)
        wi_st = sb(stP12, "wi_st", [128, F_IN], F32)
        wi_bf = sb(stP12, "wi_bf", [128, 8 * F_IN], BF16)
        z_bf = sb(stP12, "z_bf", [128, T], BF16)
        uv_sub = [sb(stP12, f"uvsub{i}", [128, 512], BF16) for i in range(2)]
        zch = sb(stP12, "zch", [128, T], BF16)
        ones_r = sb(stP12, "ones_r", [1, T], BF16)
        binr = sb(stP12, "binr", [1, F_IN], BF16)

        dma(SY, lng_bc[:], d_ln_g.partition_broadcast(128), sBC)
        dma(SY, lnb_bc[:], d_ln_b.partition_broadcast(128), sBC)
        dma(GP, binr[0:1, :], d_bin, sBC)
        dma(SY, cols[:, C_GQ:C_GQ + 1], d_gq, sBC)
        dma(SY, cols[:, C_BQ:C_BQ + 1], d_bq, sBC)
        dma(SY, cols[:, C_GK:C_GK + 1], d_gk, sBC)
        dma(SY, cols[:, C_BK:C_BK + 1], d_bk, sBC)
        dma(SY, cols[:, C_BZ:C_BZ + 1], d_bin[0:1, 2 * E:F_IN].rearrange("o p -> p o"), sBC)
        dma(SY, ident[:], d_id, sBC)
        dma(SY, one1[0:1, O_CLX:O_CLX + 1], d_clip_x, sBC)
        dma(SY, one1[0:1, O_CLD:O_CLD + 1], d_clip_d, sBC)
        m_bc = sBC.n
        VE.wait(sV, sV.n)
        m_ones = VE.op(lambda e: e.memset(ones_r[0:1, :], 1.0), sV)
        TE.wait(sBC, m_bc)

        m_sumA = {}
        m_sqA = {}
        m_wicv = {}
        dma(GP, wi_st[:], d_Win[0:128, :], sWI)
        for i in range(2):
            dma(SY, xt[i][:], dx[128 * i:128 * (i + 1), :], sXT[i])
        for i in range(NT):
            sl = i % 3
            if i + 2 < NT:
                s2 = (i + 2) % 3
                if i >= 1:
                    SY.wait(sV, m_sumA[i - 1])
                    SY.wait(sS, m_sqA[i - 1])
                dma(SY, xt[s2][:], dx[128 * (i + 2):128 * (i + 3), :], sXT[s2])
            VE.wait(sXT[sl], sXT[sl].n)
            VE.wait(sV, sV.n)
            m_sumA[i] = VE.op(lambda e, s=xt[sl], ii=i:
                              e.reduce_sum(out=stats[:, ii:ii + 1], in_=s[:], axis=AX), sV)
            SC.wait(sXT[sl], sXT[sl].n)
            SC.wait(sS, sS.n)
            m_sqA[i] = SC.op(lambda e, s=xt[sl], ii=i:
                             e.activation(out=scrS[:], in_=s[:], func=ACT.Square,
                                          accum_out=stats[:, 16 + ii:17 + ii]), sS)
            if i % 2 == 1 and i // 2 < 8:
                d_ = i // 2
                SC.wait(sWI, sWI.n)
                SC.wait(sS, sS.n)
                m_wicv[d_] = SC.op(lambda e, dd=d_:
                                   e.copy(out=wi_bf[0:128, F_IN * dd:F_IN * (dd + 1)], in_=wi_st[:]), sS)
                if d_ + 1 < 8:
                    GP.wait(sS, m_wicv[d_])
                    dma(GP, wi_st[:], d_Win[128 * (d_ + 1):128 * (d_ + 2), :], sWI)
            TE.wait(sS, m_sqA[i])
            fill(14, bank=0)
        VE.wait(sS, m_sqA[NT - 1])
        VE.wait(sV, sV.n)
        VE.op(lambda e: e.tensor_scalar_mul(stats[:, 32:48], stats[:, 0:16], 1.0 / D), sV)
        VE.wait(sV, sV.n)
        VE.op(lambda e: e.tensor_scalar_mul(stats[:, 16:32], stats[:, 16:32], 1.0 / D), sV)
        VE.wait(sV, sV.n)
        VE.op(lambda e: e.tensor_tensor(out=stats[:, 0:16], in0=stats[:, 32:48], in1=stats[:, 32:48], op=OP.mult), sV)
        VE.wait(sV, sV.n)
        VE.op(lambda e: e.tensor_tensor(out=stats[:, 16:32], in0=stats[:, 16:32], in1=stats[:, 0:16], op=OP.subtract), sV)
        VE.wait(sV, sV.n)
        mvB = VE.op(lambda e: e.tensor_scalar_add(stats[:, 16:32], stats[:, 16:32], LN_EPS), sV)
        SC.wait(sV, mvB)
        SC.wait(sS, sS.n)
        msB = SC.op(lambda e: e.activation(out=stats[:, 0:16], in_=stats[:, 16:32], func=ACT.Sqrt), sS)
        VE.wait(sS, msB)
        VE.op(lambda e: e.reciprocal(out=stats[:, 48:64], in_=stats[:, 0:16]), sV)
        VE.wait(sV, sV.n)
        VE.op(lambda e: e.tensor_tensor(out=stats[:, 0:16], in0=stats[:, 48:64], in1=stats[:, 48:64], op=OP.mult), sV)
        VE.wait(sV, sV.n)
        VE.op(lambda e: e.tensor_tensor(out=stats[:, 0:16], in0=stats[:, 0:16], in1=stats[:, 16:32], op=OP.mult), sV)
        VE.wait(sV, sV.n)
        VE.op(lambda e: e.tensor_scalar(out=stats[:, 0:16], in0=stats[:, 0:16], scalar1=3.0, scalar2=-0.5,
                                        op0=OP.subtract, op1=OP.mult), sV)
        VE.wait(sV, sV.n)
        VE.op(lambda e: e.tensor_tensor(out=stats[:, 16:32], in0=stats[:, 48:64], in1=stats[:, 0:16], op=OP.mult), sV)
        m_rstdB = sV.n
        m_xn = {}
        m_scr1 = {}
        m_xntr = {}
        m_xntr_te = {}
        for i in range(2):
            SY.wait(sV, m_sumA[NT - 1])
            SY.wait(sS, m_sqA[NT - 1])
            dma(SY, xt[i][:], dx[128 * i:128 * (i + 1), :], sXT[i])
        for i in range(NT):
            sl = i % 3
            if i + 2 < NT:
                s2 = (i + 2) % 3
                if i >= 1:
                    SY.wait(sV, m_scr1[i - 1])
                SY.wait(sS, m_sqA[NT - 1])
                dma(SY, xt[s2][:], dx[128 * (i + 2):128 * (i + 3), :], sXT[s2])
            VE.wait(sXT[sl], sXT[sl].n)
            VE.wait(sBC, m_bc)
            VE.wait(sV, max(sV.n, m_rstdB))
            m_scr1[i] = VE.op(lambda e, s=xt[sl], ii=i:
                              e.scalar_tensor_tensor(out=scr1[:], in0=s[:], scalar=stats[:, 32 + ii:33 + ii],
                                                     in1=lng_bc[:], op0=OP.subtract, op1=OP.mult), sV)
            VE.wait(sV, sV.n)
            if i >= 2:
                VE.wait(sT, m_xntr_te[i - 2])
            xnbuf = scrS if i % 2 == 0 else xn3
            if i % 2 == 0:
                VE.wait(sS, m_sqA[NT - 1])
            m_xn[i] = VE.op(lambda e, d=xnbuf, ii=i:
                            e.scalar_tensor_tensor(out=d[:], in0=scr1[:], scalar=stats[:, 16 + ii:17 + ii],
                                                   in1=lnb_bc[:], op0=OP.mult, op1=OP.add), sV)
            TE.wait(sV, m_xn[i])
            m_xntr[i] = te_transpose(xn_fm[:, 128 * i:], xnbuf[:], 8,
                                     src_step=128, dst_step=T, banks=(0, 1, 2, 3, 4, 5, 6, 7))
            m_xntr_te[i] = sT.n
            fill(6, bank=(i % 2) * 4)
        m_xn_tr = m_xntr[NT - 1]
        dbg_tap(0, xn3[:], D, sV, m_xn[NT - 1])

        m_zev = {}
        for n in range(4):
            TE.wait(sS, m_xntr[4 * n + 3])
            TE.wait(sS, m_wicv[7])
            te_bank(n)
            for d_ in range(8):
                TE.op(lambda e, dd=d_, nn=n:
                      e.matmul(ps[nn],
                               wi_bf[0:128, F_IN * dd + 2 * E:F_IN * dd + 2 * E + 128],
                               xn_fm[0:128, dd * T + 512 * nn:dd * T + 512 * (nn + 1)],
                               start=(dd == 0), stop=(dd == 7)),
                      sT if d_ == 7 else None)
            m_zmm = sT.n
            SC.wait(sT, m_zmm)
            SC.wait(sBC, m_bc)
            SC.wait(sS, sS.n)
            m_zev[n] = SC.op(lambda e, nn=n:
                             e.activation(out=z_bf[:, 512 * nn:512 * (nn + 1)], in_=ps[nn],
                                          func=ACT.Silu, bias=cols[:, C_BZ:C_BZ + 1]), sS)
            bankfree[n] = (sS, m_zev[n])
        VE.wait(sS, m_zev[3])
        VE.wait(sBC, m_bc)
        VE.wait(sV, sV.n)
        m_q = VE.op(lambda e: e.tensor_scalar(out=q_fm[:], in0=z_bf[:], scalar1=cols[:, C_GQ:C_GQ + 1],
                                              scalar2=cols[:, C_BQ:C_BQ + 1], op0=OP.mult, op1=OP.add), sV)
        SY.wait(sS, m_zev[3])
        dma(SY, zg_in, z_bf[:], sSTZ)
        dbg_tap(1, z_bf[:], T, sS, m_zev[3])
        dbg_tap(2, q_fm[:], T, sV, m_q)

        GP.wait(sSTZ, sSTZ.n)
        GP.op(lambda e: e.collective_compute("AllGather", OP.bypass, replica_groups=RG,
                                             ins=[zg_in], outs=[zg_out]), sCC)
        GP.wait(sCC, sCC.n)

        ev = 0
        m_uvadd = {}
        m_uvev = {}
        m_substore = {}
        for w in range(2):
            col0 = E if w == 0 else 0
            for t in range(NT):
                if w == 0 and t == NT // 2:
                    GP.wait(sSTV, sSTV.n)
                    GP.op(lambda e: e.collective_compute("AllGather", OP.bypass, replica_groups=RG,
                                                         ins=[vg_in[0:T // 2, :]], outs=[vg_outA]), sCC)
                    m_vgA = sCC.n
                for n in range(4):
                    b = ev % 8
                    te_bank(b)
                    if ev == 0:
                        TE.wait(sV, m_ones)
                        TE.wait(sBC, m_bc)
                    for d_ in range(9):
                        if d_ < 8:
                            TE.op(lambda e, dd=d_, tt=t, c0=col0, nn=n, bb=b:
                                  e.matmul(ps[bb],
                                           xn_fm[0:128, dd * T + 128 * tt:dd * T + 128 * (tt + 1)],
                                           wi_bf[0:128, F_IN * dd + c0 + 512 * nn:F_IN * dd + c0 + 512 * (nn + 1)],
                                           start=(dd == 0), stop=False),
                                  None)
                        else:
                            TE.op(lambda e, tt=t, c0=col0, nn=n, bb=b:
                                  e.matmul(ps[bb],
                                           ones_r[0:1, 128 * tt:128 * (tt + 1)],
                                           binr[0:1, c0 + 512 * nn:c0 + 512 * (nn + 1)],
                                           start=False, stop=True),
                                  sT)
                    m_mm = sT.n
                    SC.wait(sT, m_mm)
                    if ev >= 2:
                        SC.wait(*m_substore[ev - 2])
                    SC.wait(sS, sS.n)
                    m_uvev[ev] = SC.op(
                        lambda e, esl=ev % 2, bb=b:
                        e.activation(out=uv_sub[esl][:], in_=ps[bb], func=ACT.Silu), sS)
                    bankfree[b] = (sS, m_uvev[ev])
                    SY.wait(sS, m_uvev[ev])
                    dst, stx = (vg_in, sSTV) if w == 0 else (u_hbm, sSTU)
                    dma(SY, dst[128 * t:128 * (t + 1), 512 * n:512 * (n + 1)], uv_sub[ev % 2][:], stx)
                    m_substore[ev] = (stx, stx.n)
                    ev += 1
            if w == 0:
                GP.wait(sSTV, sSTV.n)
                GP.op(lambda e: e.collective_compute("AllGather", OP.bypass, replica_groups=RG,
                                                     ins=[vg_in[T // 2:T, :]], outs=[vg_outB]), sCC)
                m_vg = sCC.n

        m_kch = {}
        for r in range(NCORES):
            if r >= 1:
                GP.wait(sV, m_kch[r - 1])
            dma(GP, zch[:], zg_out[S * r:S * (r + 1), :], sGZ)
            VE.wait(sGZ, sGZ.n)
            VE.wait(sBC, m_bc)
            VE.wait(sV, sV.n)
            m_kch[r] = VE.op(lambda e, rr=r:
                             e.tensor_scalar(out=k_fm[:, T * rr:T * (rr + 1)], in0=zch[:],
                                             scalar1=cols[:, C_GK:C_GK + 1],
                                             scalar2=cols[:, C_BK:C_BK + 1],
                                             op0=OP.mult, op1=OP.add), sV)
            TE.wait(sV, m_kch[r])
            fill(14, bank=0)
        m_k = m_kch[NCORES - 1]
        dbg_tap(3, k_fm[:, 0:T], T, sV, m_k)
        fill(70, bank=0)
        TE.wait(sCC, m_vgA)
        fill(70, bank=0)

        stP12.close()
        barrier(exclude=(sCC,))
        stP3 = ExitStack()
        v_sup = [sb(stP3, f"vsup{i}", [128, 4 * E], BF16) for i in range(4)]
        sc_sb = [sb(stP3, f"scsb{i}", [128, 4 * 512], BF16) for i in range(4)]
        rl_tmp = sb(stP3, "rltmp", [128, 512], F32)
        av_acc = sb(stP3, "av_acc", [128, 4 * E], F32)
        u_t = [sb(stP3, f"ut{i}", [128, E], BF16) for i in range(3)]
        uav_f = [sb(stP3, f"uavf{i}", [128, E], F32) for i in range(2)]

        NPAIR = NSUP // 2
        m_avpair = {}
        m_sq = {}
        m_vld = {}
        m_uav = {}
        grp_box = [0]

        def emit_scores_kb(p, s_, kb):
            sl4 = s_ % 4
            idx = p * NSUP + s_
            if kb == 0:
                if idx >= 4:
                    pp, ss = (p, s_ - 4) if s_ >= 4 else (p - 1, NSUP - 4 + s_)
                    GP.wait(sT, m_avpair[(pp, ss // 2)])
                r_, o_ = divmod(KS * s_, T)
                if o_ < T // 2:
                    vsrc = vg_outA[(T // 2) * r_ + o_:(T // 2) * r_ + o_ + KS, :]
                    GP.wait(sCC, m_vgA)
                else:
                    vsrc = vg_outB[(T // 2) * r_ + o_ - T // 2:(T // 2) * r_ + o_ - T // 2 + KS, :]
                    GP.wait(sCC, m_vg)
                dma(GP, v_sup[s_ % 4][:], vsrc, sVL[s_ % 2])
                m_vld[(p, s_)] = sVL[s_ % 2].n
            if idx == 0 and kb == 0:
                TE.wait(sV, m_k)
            b = kb % 2
            te_bank(b)
            TE.op(lambda e, pp=p, ss=s_, kk=kb, bb=b:
                  e.matmul(ps[bb],
                           k_fm[0:128, KS * ss + 128 * kk:KS * ss + 128 * (kk + 1)],
                           q_fm[0:128, QP * pp:QP * (pp + 1)],
                           start=True, stop=True), sT)
            m_scmm = sT.n
            SC.wait(sT, m_scmm)
            SC.wait(sS, sS.n)
            m_rl = SC.op(lambda e, bb=b:
                         e.activation(out=rl_tmp[:], in_=ps[bb], func=ACT.Relu, scale=INV_L), sS)
            bankfree[b] = (sS, m_rl)
            if kb == 0 and idx >= 4:
                pp, ss = (p, s_ - 4) if s_ >= 4 else (p - 1, NSUP - 4 + s_)
                SC.wait(sT, m_avpair[(pp, ss // 2)])
            SC.wait(sS, sS.n)
            m_sq[(p, s_, kb)] = SC.op(
                lambda e, so=sl4, kk=kb:
                e.activation(out=sc_sb[so][0:128, 512 * kk:512 * (kk + 1)],
                             in_=rl_tmp[:], func=ACT.Square), sS)

        def emit_avpair(p, pr):
            s0, s1 = 2 * pr, 2 * pr + 1
            nxt = []
            if pr + 1 < NPAIR:
                nxt = [(2 * (pr + 1) + i // 4, i % 4) for i in range(8)]
            elif p + 1 < NPASS:
                nxt = [(0, None)] * 0
            TE.wait(sS, m_sq[(p, s1, 3)])
            TE.wait(sVL[s0 % 2], m_vld[(p, s0)])
            TE.wait(sVL[s1 % 2], m_vld[(p, s1)])
            gi = 0
            for qb in range(4):
                for g in range(2):
                    r = grp_box[0] % 3
                    grp_box[0] += 1
                    b0, b1 = 2 + 2 * r, 3 + 2 * r
                    te_bank(b0)
                    te_bank(b1)
                    for s_i in (s0, s1):
                        sl4 = s_i % 4
                        for kb in range(4):
                            for j in range(2):
                                TE.op(lambda e, so=sl4, kk=kb, qq=qb, gg=g, jj=j,
                                      bb=(b0 if j == 0 else b1), st=(s_i == s0 and kb == 0),
                                      sp=(s_i == s1 and kb == 3):
                                      e.matmul(ps[bb],
                                               sc_sb[so][0:128, 512 * kk + 128 * qq:512 * kk + 128 * (qq + 1)],
                                               v_sup[s_i % 4][0:128, E * kk + 1024 * gg + 512 * jj:
                                                              E * kk + 1024 * gg + 512 * (jj + 1)],
                                               start=st, stop=sp),
                                      sT if (s_i == s1 and kb == 3 and j == 1) else None)
                    m_grp = sT.n
                    if gi < len(nxt):
                        emit_scores_kb(p, nxt[gi][0], nxt[gi][1])
                        gi += 1
                    VE.wait(sT, m_grp)
                    VE.wait(sV, sV.n)
                    dsl = av_acc[:, E * qb + 1024 * g:E * qb + 1024 * (g + 1)]
                    pss = ps_all[0:128, 512 * b0:512 * b0 + 1024]
                    if pr == 0:
                        VE.op(lambda e, d=dsl, b_=pss: e.tensor_copy(out=d, in_=b_), sV)
                    else:
                        VE.op(lambda e, d=dsl, b_=pss: e.tensor_add(out=d, in0=b_, in1=d), sV)
                    bankfree[b0] = (sV, sV.n)
                    bankfree[b1] = (sV, sV.n)
            m_avpair[(p, pr)] = sT.n

        m_uavst = {}
        for p in range(NPASS):
            if p == 0:
                for kb in range(4):
                    emit_scores_kb(p, 0, kb)
                for kb in range(4):
                    emit_scores_kb(p, 1, kb)
            for pr in range(NPAIR):
                emit_avpair(p, pr)
            if p + 1 < NPASS:
                for kb in range(4):
                    emit_scores_kb(p + 1, 0, kb)
                for kb in range(4):
                    emit_scores_kb(p + 1, 1, kb)
            if p == 0:
                SY.wait(sSTU, sSTU.n)
                for g0 in range(2):
                    dma(SY, u_t[g0][:], u_hbm[128 * g0:128 * (g0 + 1), :], sUT[g0])
            for qb in range(4):
                sl = qb % 2
                gq_ = 4 * p + qb
                usl = gq_ % 3
                if gq_ + 2 < 4 * NPASS:
                    s2 = (gq_ + 2) % 3
                    if gq_ >= 1:
                        SY.wait(sV, m_uav[gq_ - 1])
                    dma(SY, u_t[s2][:], u_hbm[128 * (gq_ + 2):128 * (gq_ + 3), :], sUT[s2])
                VE.wait(sUT[usl], sUT[usl].n)
                if gq_ >= 2:
                    VE.wait(sUAVT[sl], sUAVT[sl].n)
                VE.wait(sV, sV.n)
                m_uav[gq_] = VE.op(
                    lambda e, s2=sl, u2=usl, qq=qb:
                    e.tensor_mul(out=uav_f[s2][:], in0=u_t[u2][:], in1=av_acc[:, E * qq:E * (qq + 1)]), sV)
                SY.wait(sV, m_uav[gq_])
                m_uavst[gq_] = dma(SY, uav_hbm[QP * p + 128 * qb:QP * p + 128 * (qb + 1), :],
                                   uav_f[sl][:], sUAVT[sl])

        stP3.close()
        stA.close()
        barrier()
        stW = ExitStack()
        w_st = [sb(stW, f"wst{i}", [128, D], F32) for i in range(4)]
        wmx = sb(stW, "wmx", [128, 96], F32)
        stP35 = ExitStack()
        uavld = [sb(stP35, f"uavld{i}", [128, E], F32) for i in range(3)]
        uav_fm = sb(stP35, "uav_fm", [128, 2048], BF16)
        Wout_bf = sb(stP35, "Wout_bf", [128, 16 * D], BF16)
        w35 = [sb(stP35, f"w35{i}", [128, D], F32) for i in range(2)]
        bout_bc = sb(stP35, "bout_bc", [128, D], F32)
        xr_t = [sb(stP35, f"xrt{i}", [128, D], F32) for i in range(3)]
        a1_t = [sb(stP35, f"a1t{i}", [128, D], F32) for i in range(2)]
        scr35 = sb(stP35, "scr35", [128, D], F32)

        fill(25, bank=0)
        dma(SY, bout_bc[:], d_bout.partition_broadcast(128), sBC)
        m_bout = sBC.n
        m_wocv = {}
        for j in range(16):
            wsl = j % 2
            rq = GP if wsl == 0 else SY
            if j >= 2:
                rq.wait(sS, m_wocv[j - 2])
            dma(rq, w35[wsl][:], d_Wout[128 * j:128 * (j + 1), :], sWO[wsl])
            SC.wait(sWO[wsl], sWO[wsl].n)
            SC.wait(sS, sS.n)
            m_wocv[j] = SC.op(lambda e, jj=j, s2=wsl:
                              e.copy(out=Wout_bf[0:128, D * jj:D * (jj + 1)], in_=w35[s2][:]), sS)
            TE.wait(sS, m_wocv[j])
            fill(10, bank=0)
        m_wo_all = m_wocv[15]

        m_uavtr = {}
        m_wo = {}
        m_sq2q = {}
        m_mxq = {}
        m_a1st = {}
        wo_box = [0]

        for t in range(2):
            dma(SY, uavld[t][:], uav_hbm[128 * t:128 * (t + 1), :], sUL3[t])
            dma(GP, xr_t[t][:], dx[128 * t:128 * (t + 1), :], sXR[t])
        for t in range(NT):
            sl = t % 2
            ul = t % 3
            if t + 2 < NT:
                s2 = (t + 2) % 3
                if t >= 1:
                    SY.wait(sT, m_uavtr[t - 1])
                dma(SY, uavld[s2][:], uav_hbm[128 * (t + 2):128 * (t + 3), :], sUL3[s2])
                if t >= 1:
                    GP.wait(sV, m_mxq[t - 1])
                dma(GP, xr_t[s2][:], dx[128 * (t + 2):128 * (t + 3), :], sXR[s2])

            TE.wait(sUL3[ul], sUL3[ul].n)
            te_transpose(uav_fm, uavld[ul][:], 16, src_step=128, dst_step=128, banks=(4, 5, 6, 7))
            m_uavtr[t] = sT.n
            m_tr_ev = sS.n
            for dn in range(2):
                b = wo_box[0] % 4
                wo_box[0] += 1
                te_bank(b)
                if dn == 0:
                    TE.wait(sS, m_tr_ev)
                for j in range(16):
                    if t == 0 and dn == 0:
                        TE.wait(sS, m_wocv[j])
                    TE.op(lambda e, jj=j, dd=dn, bb=b:
                          e.matmul(ps[bb],
                                   uav_fm[0:128, 128 * jj:128 * (jj + 1)],
                                   Wout_bf[0:128, D * jj + 512 * dd:D * jj + 512 * (dd + 1)],
                                   start=(jj == 0), stop=(jj == 15)),
                          sT if j == 15 else None)
                m_wodn = sT.n
                VE.wait(sT, m_wodn)
                VE.wait(sXR[ul], sXR[ul].n)
                VE.wait(sBC, m_bout)
                if t >= 2 and dn == 0:
                    VE.wait(sS, m_sq2q[t - 2])
                    VE.wait(sX2ST[sl], sX2ST[sl].n)
                VE.wait(sV, sV.n)
                VE.op(lambda e, bb=b, dd=dn, s2=sl:
                      e.tensor_add(out=a1_t[s2][:, 512 * dd:512 * (dd + 1)], in0=ps[bb],
                                   in1=bout_bc[:, 512 * dd:512 * (dd + 1)]), sV)
                VE.wait(sV, sV.n)
                mep = VE.op(lambda e, s2=sl, u2=ul, dd=dn:
                            e.scalar_tensor_tensor(out=a1_t[s2][:, 512 * dd:512 * (dd + 1)],
                                                   in0=xr_t[u2][:, 512 * dd:512 * (dd + 1)],
                                                   scalar=2.0, in1=a1_t[s2][:, 512 * dd:512 * (dd + 1)],
                                                   op0=OP.mult, op1=OP.add), sV)
                bankfree[b] = (sV, mep)
            m_wo[t] = sT.n
            m_a1 = sV.n
            VE.wait(sV, m_a1)
            VE.op(lambda e, gi=t, s2=sl: e.reduce_sum(out=stats[:, gi:gi + 1], in_=a1_t[s2][:], axis=AX), sV)
            SC.wait(sV, sV.n)
            SC.wait(sS, sS.n)
            m_sq2q[t] = SC.op(lambda e, gi=t, s2=sl:
                              e.activation(out=scrS[:], in_=a1_t[s2][:], func=ACT.Square,
                                           accum_out=stats[:, 16 + gi:17 + gi]), sS)
            VE.wait(sV, sV.n)
            VE.op(lambda e, gi=t: e.tensor_scalar_mul(stats[:, 60:61], stats[:, gi:gi + 1], 1.0 / D), sV)
            VE.wait(sV, sV.n)
            VE.op(lambda e, s2=sl: e.tensor_single_scalar(out=scr35[:], in_=a1_t[s2][:],
                                                          scalar=stats[:, 60:61], op=OP.subtract), sV)
            VE.wait(sV, sV.n)
            m_mxq[t] = VE.op(lambda e, gi=t:
                             e.reduce_max(out=stats[:, 32 + gi:33 + gi], in_=scr35[:], axis=AX,
                                          apply_absolute_value=True), sV)
            GP.wait(sV, m_mxq[t])
            GP.wait(sS, m_sq2q[t])
            m_a1st[t] = dma(GP, x2_hbm[128 * t:128 * (t + 1), :], a1_t[sl][:], sX2ST[sl])

        VE.wait(sS, sS.n)
        VE.wait(sV, sV.n)
        VE.op(lambda e: e.tensor_scalar_mul(cols2[:, 0:16], stats[:, 0:16], 1.0 / D), sV)
        VE.wait(sV, sV.n)
        VE.op(lambda e: e.tensor_scalar_mul(cols2[:, 32:48], stats[:, 16:32], 1.0 / D), sV)
        VE.wait(sV, sV.n)
        VE.op(lambda e: e.tensor_tensor(out=stats[:, 16:32], in0=cols2[:, 0:16], in1=cols2[:, 0:16], op=OP.mult), sV)
        VE.wait(sV, sV.n)
        VE.op(lambda e: e.tensor_tensor(out=cols2[:, 32:48], in0=cols2[:, 32:48], in1=stats[:, 16:32], op=OP.subtract), sV)
        VE.wait(sV, sV.n)
        mv2 = VE.op(lambda e: e.tensor_scalar_add(cols2[:, 32:48], cols2[:, 32:48], LN_EPS), sV)
        SC.wait(sV, mv2)
        SC.wait(sS, sS.n)
        ms2 = SC.op(lambda e: e.activation(out=stats[:, 16:32], in_=cols2[:, 32:48], func=ACT.Sqrt), sS)
        VE.wait(sS, ms2)
        VE.op(lambda e: e.reciprocal(out=cols2[:, 16:32], in_=stats[:, 16:32]), sV)
        VE.wait(sV, sV.n)
        VE.op(lambda e: e.tensor_tensor(out=stats[:, 16:32], in0=cols2[:, 16:32], in1=cols2[:, 16:32], op=OP.mult), sV)
        VE.wait(sV, sV.n)
        VE.op(lambda e: e.tensor_tensor(out=stats[:, 16:32], in0=stats[:, 16:32], in1=cols2[:, 32:48], op=OP.mult), sV)
        VE.wait(sV, sV.n)
        VE.op(lambda e: e.tensor_scalar(out=stats[:, 16:32], in0=stats[:, 16:32], scalar1=3.0, scalar2=-0.5,
                                        op0=OP.subtract, op1=OP.mult), sV)
        VE.wait(sV, sV.n)
        VE.op(lambda e: e.tensor_tensor(out=cols2[:, 16:32], in0=cols2[:, 16:32], in1=stats[:, 16:32], op=OP.mult), sV)
        VE.wait(sV, sV.n)
        VE.op(lambda e: e.tensor_tensor(out=stats[:, 32:48], in0=stats[:, 32:48], in1=cols2[:, 16:32], op=OP.mult), sV)
        VE.wait(sV, sV.n)
        VE.op(lambda e: e.tensor_reduce(out=cols[:, C_MX:C_MX + 1], in_=stats[:, 32:48], axis=AX, op=OP.max), sV)
        m_mx = sV.n

        stP35.close()
        barrier()
        stB1 = ExitStack()
        xq_fm = sb(stB1, "xq_fm", [128, 8 * T], BF16)
        kgT = sb(stB1, "kgT", [128, 8 * I_FF], BF16)
        kuT = sb(stB1, "kuT", [128, 8 * I_FF], BF16)
        qi_t = [sb(stB1, f"qit{i}", [128, D], I32) for i in range(2)]
        qf_t = [sb(stB1, f"qft{i}", [128, D], F32) for i in range(2)]
        x2q_t = [sb(stB1, f"x2q{i}", [128, D], F32) for i in range(3)]
        sgx = sb(stB1, "sgx", [128, NI], F32)
        sux = sb(stB1, "sux", [128, NI], F32)
        g_tmp = [sb(stB1, f"gtmp{i}", [128, 512], F32) for i in range(3)]
        up_tmp = [sb(stB1, f"uptmp{i}", [128, 512], F32) for i in range(3)]
        vh_t = [sb(stB1, f"vht{i}", [128, 512], F32) for i in range(3)]

        fill(100, bank=0)
        GP.wait(sV, m_mx)
        dma(GP, mx_in, cols[:, C_MX:C_MX + 1], sSCB)
        GP.wait(sSCB, sSCB.n)
        GP.op(lambda e: e.collective_compute("AllReduce", OP.max, replica_groups=RG,
                                             ins=[mx_in], outs=[mx_out]), sCC)
        m_cc_mx = sCC.n
        TE.wait(sCC, m_cc_mx)
        fill(40, bank=0)
        GP.wait(sCC, sCC.n)
        dma(GP, row128[0:1, :], mx_out.rearrange("p o -> o p"), sSCB)
        VE.wait(sSCB, sSCB.n)
        VE.wait(sV, sV.n)
        VE.op(lambda e: e.reduce_max(out=one1[0:1, O_M:O_M + 1], in_=row128[0:1, :], axis=AX), sV)
        SC.wait(sBC, m_bc)
        SC.wait(sS, sS.n)
        m_sig = SC.op(lambda e: e.activation(out=one1[0:1, O_SIG:O_SIG + 1],
                                             in_=one1[0:1, O_CLX:O_CLX + 1], func=ACT.Sigmoid), sS)
        VE.wait(sS, m_sig)
        VE.wait(sV, sV.n)
        VE.op(lambda e: e.tensor_tensor(out=one1[0:1, O_SX:O_SX + 1], in0=one1[0:1, O_SIG:O_SIG + 1],
                                        in1=one1[0:1, O_M:O_M + 1], op=OP.mult), sV)
        VE.wait(sV, sV.n)
        VE.op(lambda e: e.tensor_scalar_mul(one1[0:1, O_SX:O_SX + 1], one1[0:1, O_SX:O_SX + 1],
                                            1.0 / 127.0), sV)
        m_rsx = recip_nr(one1[0:1, O_SX:O_SX + 1], one1[0:1, O_RSX:O_RSX + 1])
        GP.wait(sV, m_rsx)
        dma(GP, sc_hbm[0:1, 0:2], one1[0:1, O_SX:O_RSX + 1], sSCB)
        GP.wait(sSCB, sSCB.n)
        dma(GP, cols[:, C_SX:C_SX + 1], sc_hbm[0:1, 0:1].partition_broadcast(128), sSCB)
        dma(GP, cols[:, C_RSX:C_RSX + 1], sc_hbm[0:1, 1:2].partition_broadcast(128), sSCB)
        m_sxc = sSCB.n
        VE.wait(sSCB, m_sxc)
        VE.wait(sV, sV.n)
        m_rs2 = VE.op(lambda e: e.tensor_scalar_mul(cols2[:, 32:48], cols2[:, 16:32],
                                                    cols[:, C_RSX:C_RSX + 1]), sV)

        m_kx = {}
        m_kxtr = {}
        m_kxtr_s = {}
        for i in range(2):
            dma(SY, x2q_t[i][:], x2_hbm[128 * i:128 * (i + 1), :], sXQ[i])
        for i in range(NT):
            sl = i % 2
            xs = i % 3
            if i + 2 < NT:
                s2 = (i + 2) % 3
                if i >= 1:
                    SY.wait(sV, m_kx[i - 1][0])
                dma(SY, x2q_t[s2][:], x2_hbm[128 * (i + 2):128 * (i + 3), :], sXQ[s2])
            VE.wait(sXQ[xs], sXQ[xs].n)
            VE.wait(sV, max(sV.n, m_rs2))
            if i >= 2:
                VE.wait(sT, m_kxtr[i - 2])
            mq1 = VE.op(lambda e, s2=sl, x3=xs, ii=i:
                        e.tensor_scalar(out=qf_t[s2][:], in0=x2q_t[x3][:],
                                        scalar1=cols2[:, ii:ii + 1], scalar2=cols2[:, 32 + ii:33 + ii],
                                        op0=OP.subtract, op1=OP.mult), sV)
            VE.wait(sV, sV.n)
            mq2 = VE.op(lambda e, s2=sl: e.tensor_scalar(out=qi_t[s2][:], in0=qf_t[s2][:],
                                                         scalar1=-128.0, scalar2=127.0,
                                                         op0=OP.max, op1=OP.min), sV)
            SC.wait(sV, mq2)
            if i >= 2:
                SC.wait(sT, m_kxtr[i - 2])
            SC.wait(sS, sS.n)
            mq3 = SC.op(lambda e, s2=sl: e.copy(out=qf_t[s2][:], in_=qi_t[s2][:]), sS)
            m_kx[i] = (mq1, mq3)
            TE.wait(sS, mq3)
            te_transpose(xq_fm[:, 128 * i:], qf_t[sl][:], 8, src_step=128, dst_step=T, banks=(6, 7))
            m_kxtr[i] = sT.n
            m_kxtr_s[i] = sS.n
            fill(6, bank=0)
        m_xq_tr = m_kxtr_s[NT - 1]
        dbg_tap(10, xq_fm[:, 0:T], T, sS, m_xq_tr)

        m_kwq = {}
        m_kwtr = {}
        m_kwtr_s = {}
        m_gev = {}
        m_upev = {}
        m_vhmul = {}
        m_vhst = {}
        vhe_box = [0]

        m_kwq1 = {}

        def issue_kw_load(c):
            if c >= NI:
                return
            for wsel, dsrc in enumerate((d_wg, d_wu)):
                idx = 2 * c + wsel
                sl = idx % 4
                if idx >= 4:
                    SY.wait(sV, m_kwq1[idx - 4])
                dma(SY, w_st[sl][:], dsrc[128 * c:128 * (c + 1), :], sWS[sl])

        m_scl = {}

        def emit_kw_prep(c):
            for wsel, (dsrc, kT) in enumerate(((d_wg, kgT), (d_wu, kuT))):
                idx = 2 * c + wsel
                cx = wsel * NI + c
                sl = idx % 2
                ws4 = idx % 4
                scol = sgx if wsel == 0 else sux
                VE.wait(sWS[ws4], sWS[ws4].n)
                if idx == 0:
                    VE.wait(sSCB, m_sxc)
                VE.wait(sV, sV.n)
                VE.op(lambda e, w4=ws4, ix=cx:
                      e.reduce_max(out=wmx[:, ix:ix + 1], in_=w_st[w4][:],
                                   axis=AX, apply_absolute_value=True), sV)
                VE.wait(sV, sV.n)
                VE.op(lambda e, ix=cx: e.reciprocal(out=wmx[:, 44:45], in_=wmx[:, ix:ix + 1]), sV)
                VE.wait(sV, sV.n)
                VE.op(lambda e, ix=cx: e.tensor_tensor(out=wmx[:, 45:46], in0=wmx[:, ix:ix + 1],
                                                       in1=wmx[:, 44:45], op=OP.mult), sV)
                VE.wait(sV, sV.n)
                VE.op(lambda e: e.tensor_scalar(out=wmx[:, 45:46], in0=wmx[:, 45:46],
                                                scalar1=2.0, scalar2=-1.0,
                                                op0=OP.subtract, op1=OP.mult), sV)
                VE.wait(sV, sV.n)
                VE.op(lambda e: e.tensor_tensor(out=wmx[:, 44:45], in0=wmx[:, 44:45],
                                                in1=wmx[:, 45:46], op=OP.mult), sV)
                VE.wait(sV, sV.n)
                VE.op(lambda e, ix=cx: e.tensor_scalar_mul(wmx[:, 48 + ix:49 + ix],
                                                           wmx[:, 44:45], 127.0), sV)
                VE.wait(sV, sV.n)
                m_scl[idx] = VE.op(lambda e, sc2=scol, cc=c, ix=cx:
                                   e.tensor_scalar(out=sc2[:, cc:cc + 1], in0=wmx[:, ix:ix + 1],
                                                   scalar1=cols[:, C_SX:C_SX + 1],
                                                   scalar2=1.0 / 127.0,
                                                   op0=OP.mult, op1=OP.mult), sV)
                VE.wait(sT, m_kwtr[idx - 2] if idx >= 2 else m_kxtr[NT - 2 + idx])
                VE.wait(sV, sV.n)
                mm1 = VE.op(lambda e, s2=sl, w4=ws4, ix=cx:
                            e.tensor_scalar(out=qf_t[s2][:], in0=w_st[w4][:],
                                            scalar1=wmx[:, 48 + ix:49 + ix], scalar2=-128.0,
                                            op0=OP.mult, op1=OP.max), sV)
                m_kwq1[idx] = mm1
                VE.wait(sV, sV.n)
                mm2 = VE.op(lambda e, s2=sl: e.tensor_scalar(out=qi_t[s2][:], in0=qf_t[s2][:],
                                                             scalar1=127.0, scalar2=None,
                                                             op0=OP.min), sV)
                SC.wait(sV, mm2)
                SC.wait(sS, sS.n)
                m_kwq[idx] = SC.op(lambda e, s2=sl: e.copy(out=qf_t[s2][:], in_=qi_t[s2][:]), sS)
                TE.wait(sS, m_kwq[idx])
                te_transpose(kT[:, 128 * c:], qf_t[sl][:], 8, src_step=128, dst_step=I_FF, banks=(6, 7))
                m_kwtr[idx] = sT.n
                m_kwtr_s[idx] = sS.n

        def emit_gateup_mm(c):
            TE.wait(sS, m_kwtr_s[2 * c + 1])
            for n in range(4):
                vhe = vhe_box[0]
                if c == 0:
                    TE.wait(sS, m_kxtr_s[4 * n + 3])
                bg = vhe % 3
                bu = 3 + vhe % 3
                te_bank(bg)
                for d_ in range(8):
                    TE.op(lambda e, dd=d_, cc=c, nn=n, bb=bg:
                          e.matmul(ps[bb],
                                   kgT[0:128, I_FF * dd + 128 * cc:I_FF * dd + 128 * (cc + 1)],
                                   xq_fm[0:128, dd * T + 512 * nn:dd * T + 512 * (nn + 1)],
                                   start=(dd == 0), stop=(dd == 7)),
                          sT if d_ == 7 else None)
                m_gmm = sT.n
                te_bank(bu)
                for d_ in range(8):
                    TE.op(lambda e, dd=d_, cc=c, nn=n, bb=bu:
                          e.matmul(ps[bb],
                                   kuT[0:128, I_FF * dd + 128 * cc:I_FF * dd + 128 * (cc + 1)],
                                   xq_fm[0:128, dd * T + 512 * nn:dd * T + 512 * (nn + 1)],
                                   start=(dd == 0), stop=(dd == 7)),
                          sT if d_ == 7 else None)
                m_umm = sT.n
                SC.wait(sT, m_gmm)
                if n == 0:
                    SC.wait(sV, m_scl[2 * c])
                if vhe >= 3:
                    SC.wait(sV, m_vhmul[vhe - 3])
                SC.wait(sS, sS.n)
                m_gev[vhe] = SC.op(lambda e, bb=bg, s2=vhe % 3, cc=c:
                                   e.activation(out=g_tmp[s2][:], in_=ps[bb], func=ACT.Silu,
                                                scale=sgx[:, cc:cc + 1]), sS)
                bankfree[bg] = (sS, m_gev[vhe])
                VE.wait(sT, m_umm)
                if n == 0:
                    VE.wait(sV, m_scl[2 * c + 1])
                VE.wait(sV, sV.n)
                m_upev[vhe] = VE.op(lambda e, bb=bu, s2=vhe % 3, cc=c:
                                    e.tensor_scalar_mul(up_tmp[s2][:], ps[bb], sux[:, cc:cc + 1]), sV)
                bankfree[bu] = (sV, m_upev[vhe])
                VE.wait(sS, m_gev[vhe])
                if vhe >= 3:
                    VE.wait(sVH[vhe % 3], sVH[vhe % 3].n)
                VE.wait(sV, sV.n)
                m_vhmul[vhe] = VE.op(lambda e, s2=vhe % 3:
                                     e.tensor_mul(out=vh_t[s2][:], in0=g_tmp[s2][:], in1=up_tmp[s2][:]), sV)
                VE.wait(sV, sV.n)
                VE.op(lambda e, s2=vhe % 3: e.reduce_max(out=stats[:, 20:21], in_=vh_t[s2][:], axis=AX,
                                                         apply_absolute_value=True), sV)
                VE.wait(sV, sV.n)
                if vhe == 0:
                    m_mh_ = VE.op(lambda e: e.tensor_copy(out=cols[:, C_MH:C_MH + 1],
                                                          in_=stats[:, 20:21]), sV)
                else:
                    m_mh_ = VE.op(lambda e: e.tensor_tensor(out=cols[:, C_MH:C_MH + 1],
                                                            in0=cols[:, C_MH:C_MH + 1],
                                                            in1=stats[:, 20:21], op=OP.max), sV)
                GP.wait(sV, m_mh_)
                m_vhst[vhe] = dma(GP, vh_hbm[128 * c:128 * (c + 1), 512 * n:512 * (n + 1)],
                                  vh_t[vhe % 3][:], sVH[vhe % 3])
                vhe_box[0] = vhe + 1
            return m_mh_

        issue_kw_load(0)
        issue_kw_load(1)
        for c in range(NI):
            emit_kw_prep(c)
            issue_kw_load(c + 2)
            if c >= 1:
                m_mh = emit_gateup_mm(c - 1)
        m_mh = emit_gateup_mm(NI - 1)

        GP.wait(sV, m_mh)
        dma(GP, mh_in, cols[:, C_MH:C_MH + 1], sSCB)
        GP.wait(sSCB, sSCB.n)
        GP.op(lambda e: e.collective_compute("AllReduce", OP.max, replica_groups=RG,
                                             ins=[mh_in], outs=[mh_out]), sCC)
        m_ccmh = sCC.n

        stB1.close()
        stW.close()
        barrier(exclude=(sCC,))
        stB2 = ExitStack()
        kdT = sb(stB2, "kdT", [128, NI * D], BF16)
        h_fm = sb(stB2, "h_fm", [128, NI * T], BF16)
        wd_st = sb(stB2, "wd_st", [128, I_FF], F32)
        wmx2 = sb(stB2, "wmx2", [128, 32], F32)
        qi2 = [sb(stB2, f"qi2{i}", [128, 1024], I32) for i in range(2)]
        qf2 = [sb(stB2, f"qf2{i}", [128, 1024], F32) for i in range(2)]
        vh_ld = [sb(stB2, f"vhld{i}", [128, 1024], F32) for i in range(3)]
        sdh_bc = sb(stB2, "sdh_bc", [128, D], F32)
        x2r_t = [sb(stB2, f"x2rt{i}", [128, D], F32) for i in range(3)]
        a2_t = [sb(stB2, f"a2t{i}", [128, D], F32) for i in range(2)]
        ep2 = [sb(stB2, f"ep2{i}", [128, 512], F32) for i in range(2)]

        fill(50, bank=4)
        m_kdq = {}
        m_kdtr = {}
        m_sdc = {}
        HW2 = 704
        for c in range(8):
            if c >= 1:
                SY.wait(sS, m_kdq[c - 1])
                SY.wait(sT, m_kdtr[c - 1])
            dma(SY, wd_st[:], d_wd[128 * c:128 * (c + 1), :], sWD)
            VE.wait(sWD, sWD.n)
            VE.wait(sV, sV.n)
            VE.op(lambda e, cc=c: e.reduce_max(out=wmx2[:, cc:cc + 1], in_=wd_st[:],
                                               axis=AX, apply_absolute_value=True), sV)
            VE.wait(sV, sV.n)
            VE.op(lambda e, cc=c: e.reciprocal(out=wmx2[:, 16:17], in_=wmx2[:, cc:cc + 1]), sV)
            VE.wait(sV, sV.n)
            VE.op(lambda e, cc=c: e.tensor_tensor(out=wmx2[:, 17:18], in0=wmx2[:, cc:cc + 1],
                                                  in1=wmx2[:, 16:17], op=OP.mult), sV)
            VE.wait(sV, sV.n)
            VE.op(lambda e: e.tensor_scalar(out=wmx2[:, 17:18], in0=wmx2[:, 17:18],
                                            scalar1=2.0, scalar2=-1.0,
                                            op0=OP.subtract, op1=OP.mult), sV)
            VE.wait(sV, sV.n)
            VE.op(lambda e: e.tensor_tensor(out=wmx2[:, 16:17], in0=wmx2[:, 16:17],
                                            in1=wmx2[:, 17:18], op=OP.mult), sV)
            VE.wait(sV, sV.n)
            VE.op(lambda e, cc=c: e.tensor_scalar_mul(wmx2[:, 24 + cc:25 + cc],
                                                      wmx2[:, 16:17], 127.0), sV)
            VE.wait(sV, sV.n)
            m_sdc[c] = VE.op(lambda e, cc=c: e.tensor_scalar_mul(wmx2[:, cc:cc + 1],
                                                                 wmx2[:, cc:cc + 1], 1.0 / 127.0), sV)
            SY.wait(sV, m_sdc[c])
            dma(SY, sd_hbm[0:1, 128 * c:128 * (c + 1)], wmx2[:, c:c + 1], sSD)
            for hf in range(4):
                sl = hf % 2
                pre = [] if (c < 1 or hf > 0) else [(sTRKD, sTRKD.n)]
                for w_ in pre:
                    SC.wait(*w_)
                VE.wait(sV, sV.n)
                VE.op(lambda e, cc=c, h2=hf, s2=sl:
                      e.tensor_scalar(out=qf2[s2][0:128, 0:HW2], in0=wd_st[0:128, HW2 * h2:HW2 * (h2 + 1)],
                                      scalar1=wmx2[:, 24 + cc:25 + cc], scalar2=-128.0,
                                      op0=OP.mult, op1=OP.max), sV)
                VE.wait(sV, sV.n)
                md2 = VE.op(lambda e, s2=sl: e.tensor_scalar(out=qi2[s2][0:128, 0:HW2],
                                                             in0=qf2[s2][0:128, 0:HW2],
                                                             scalar1=127.0, scalar2=None,
                                                             op0=OP.min), sV)
                SC.wait(sV, md2)
                if c >= 1 and hf == 0:
                    SC.wait(sT, m_kdtr[c - 1])
                SC.wait(sS, sS.n)
                m_kdq[c] = SC.op(lambda e, s2=sl, h2=hf: e.copy(out=wd_st[0:128, HW2 * h2:HW2 * (h2 + 1)],
                                                                in_=qi2[s2][0:128, 0:HW2]), sS)
            TE.wait(sS, m_kdq[c])
            te_transpose(kdT[:, 128 * c:], wd_st[:], NI, src_step=128, dst_step=D, banks=(4, 5, 6, 7))
            m_kdtr[c] = sT.n
            m_kdtr_ev = sS.n
            fill(20, bank=0)
        m_kd_tr = m_kdtr_ev

        GP.wait(sCC, m_ccmh)
        dma(GP, row128[0:1, :], mh_out.rearrange("p o -> o p"), sSCB)
        VE.wait(sSCB, sSCB.n)
        VE.wait(sV, sV.n)
        VE.op(lambda e: e.reduce_max(out=one1[0:1, O_M:O_M + 1], in_=row128[0:1, :], axis=AX), sV)
        SC.wait(sS, sS.n)
        m_sigd = SC.op(lambda e: e.activation(out=one1[0:1, O_SIG:O_SIG + 1],
                                              in_=one1[0:1, O_CLD:O_CLD + 1], func=ACT.Sigmoid), sS)
        VE.wait(sS, m_sigd)
        VE.wait(sV, sV.n)
        VE.op(lambda e: e.tensor_tensor(out=one1[0:1, O_SH:O_SH + 1], in0=one1[0:1, O_SIG:O_SIG + 1],
                                        in1=one1[0:1, O_M:O_M + 1], op=OP.mult), sV)
        VE.wait(sV, sV.n)
        VE.op(lambda e: e.tensor_scalar_mul(one1[0:1, O_SH:O_SH + 1], one1[0:1, O_SH:O_SH + 1],
                                            1.0 / 127.0), sV)
        m_rsh = recip_nr(one1[0:1, O_SH:O_SH + 1], one1[0:1, O_RSH:O_RSH + 1])
        GP.wait(sV, m_rsh)
        dma(GP, sc_hbm[0:1, 2:4], one1[0:1, O_SH:O_RSH + 1], sSCB)
        GP.wait(sSCB, sSCB.n)
        dma(GP, cols[:, C_SH:C_SH + 1], sc_hbm[0:1, 2:3].partition_broadcast(128), sSCB)
        dma(GP, cols[:, C_RSH:C_RSH + 1], sc_hbm[0:1, 3:4].partition_broadcast(128), sSCB)
        m_shc = sSCB.n

        SY.wait(sSD, sSD.n)
        dma(SY, sdh_bc[:], sd_hbm.partition_broadcast(128), sBC)
        VE.wait(sBC, sBC.n)
        VE.wait(sSCB, m_shc)
        VE.wait(sV, sV.n)
        m_sdh = VE.op(lambda e: e.tensor_scalar_mul(sdh_bc[:], sdh_bc[:], cols[:, C_SH:C_SH + 1]), sV)

        m_hq = {}
        for j0 in range(2):
            rq = SY if j0 % 2 == 0 else GP
            dma(rq, vh_ld[j0][:], vh_hbm[128 * (j0 // 2):128 * (j0 // 2) + 128,
                                         1024 * (j0 % 2):1024 * (j0 % 2) + 1024], sVHL[j0])
        for j in range(2 * NI):
            c, hf = divmod(j, 2)
            sl = j % 3
            if j + 2 < 2 * NI:
                s2 = (j + 2) % 3
                c3, h3 = divmod(j + 2, 2)
                rq = SY if (j + 2) % 2 == 0 else GP
                if j >= 1:
                    rq.wait(sS, m_hq[j - 1])
                dma(rq, vh_ld[s2][:], vh_hbm[128 * c3:128 * (c3 + 1), 1024 * h3:1024 * (h3 + 1)],
                    sVHL[s2])
            VE.wait(sVHL[sl], sVHL[sl].n)
            if j < 2:
                VE.wait(sSCB, m_shc)
            pre_ve = [(sS, m_hq[j - 2])] if j >= 2 else []
            m_hq[j] = quant(vh_ld[sl][:], cols[:, C_RSH:C_RSH + 1],
                            qi2[j % 2][0:128, 0:1024], qf2[j % 2][0:128, 0:1024],
                            h_fm[0:128, T * c + 1024 * hf:T * c + 1024 * (hf + 1)],
                            pre_ve_waits=pre_ve)
            TE.wait(sS, m_hq[j])
            fill(9, bank=6)
        m_hq_all = m_hq[2 * NI - 1]
        dbg_tap(13, h_fm[:, 0:T], T, sS, m_hq_all)
        dbg_tap(14, kdT[:, 0:2048], 2048, sS, m_kd_tr)
        dbg_tap(15, sdh_bc[:], D, sV, m_sdh)

        m_oev = {}
        m_ost = {}
        TE.wait(sS, m_hq_all)
        TE.wait(sS, m_kd_tr)
        for t in range(2):
            dma(SY, x2r_t[t][:], x2_hbm[128 * t:128 * (t + 1), :], sX2R[t])
        for t in range(NT):
            sl = t % 3
            gidx = t
            if t + 2 < NT:
                s2 = (t + 2) % 3
                if t >= 1:
                    SY.wait(sV, m_oev[t - 1])
                    SY.wait(sOUTs[s2], sOUTs[s2].n)
                dma(SY, x2r_t[s2][:], x2_hbm[128 * (t + 2):128 * (t + 3), :], sX2R[s2])
            VE.wait(sX2R[sl], sX2R[sl].n)
            VE.wait(sV, sV.n)
            m_x2n = VE.op(lambda e, s2=sl, gi=gidx:
                          e.tensor_scalar(out=x2r_t[s2][:], in0=x2r_t[s2][:],
                                          scalar1=cols2[:, gi:gi + 1], scalar2=cols2[:, 16 + gi:17 + gi],
                                          op0=OP.subtract, op1=OP.mult), sV)
            for hn in range(2):
                b = 2 * sl + hn
                te_bank(b)
                for cc in range(NI):
                    TE.op(lambda e, c2=cc, tt=t, hh=hn, bb=b:
                          e.matmul(ps[bb],
                                   h_fm[0:128, T * c2 + 128 * tt:T * c2 + 128 * (tt + 1)],
                                   kdT[0:128, D * c2 + 512 * hh:D * c2 + 512 * (hh + 1)],
                                   start=(c2 == 0), stop=(c2 == NI - 1)),
                          sT if cc == NI - 1 else None)
                m_dmm = sT.n
                VE.wait(sT, m_dmm)
                if t == 0 and hn == 0:
                    VE.wait(sV, m_sdh)
                VE.wait(sV, sV.n)
                VE.op(lambda e, bb=b, hh=hn:
                      e.tensor_mul(out=ep2[hh][:], in0=ps[bb],
                                   in1=sdh_bc[:, 512 * hh:512 * (hh + 1)]), sV)
                VE.wait(sV, max(sV.n, m_x2n))
                mep = VE.op(lambda e, s2=sl, a2=t % 2, hh=hn:
                            e.tensor_add(out=a2_t[a2][:, 512 * hh:512 * (hh + 1)],
                                         in0=x2r_t[s2][:, 512 * hh:512 * (hh + 1)], in1=ep2[hh][:]), sV)
                bankfree[b] = (sV, mep)
            ln_stats(a2_t[t % 2][:], stats[:, 14:15], stats[:, 15:16])
            VE.wait(sV, sV.n)
            m_oev[t] = VE.op(lambda e, s2=sl, a2=t % 2:
                             e.tensor_scalar(out=x2r_t[s2][:], in0=a2_t[a2][:],
                                             scalar1=stats[:, 14:15],
                                             scalar2=stats[:, 15:16],
                                             op0=OP.subtract, op1=OP.mult), sV)
            GP.wait(sV, m_oev[t])
            m_ost[t] = dma(GP, d_out[128 * t:128 * (t + 1), :], x2r_t[sl][:], sOUTs[sl])

        GP.wait(sOUTs[0], sOUTs[0].n)
        GP.wait(sOUTs[1], sOUTs[1].n)
        GP.wait(sOUTs[2], sOUTs[2].n)
        stB2.close()

        blk = st0.enter_context(nc.Block())

        def realize(rec):
            def runner(eng):
                for it in rec.items:
                    if it[0] == "w":
                        eng.wait_ge(it[1].h, it[2])
                    else:
                        instr = it[1](eng)
                        if it[2] is not None:
                            instr.then_inc(it[2].h, it[3])
            return runner

        blk.sync(realize(SY))
        blk.vector(realize(VE))
        blk.scalar(realize(SC))
        blk.tensor(realize(TE))
        blk.gpsimd(realize(GP))

    return nc


_NC_CACHE = [None]


def kernel(L0_attn_x, L0_attn_ln_g, L0_attn_ln_b, L0_attn_W_in, L0_attn_b_in,
           L0_attn_W_out, L0_attn_b_out, L0_attn_gamma_q, L0_attn_beta_q,
           L0_attn_gamma_k, L0_attn_beta_k, L0_ffn_clip_x, L0_ffn_clip_down,
           L0_ffn_w_gate, L0_ffn_w_up, L0_ffn_w_down):
    f32 = np.float32
    xr = np.ascontiguousarray(np.asarray(L0_attn_x, dtype=f32).reshape(L, D))
    shared = {
        "ln_g": np.asarray(L0_attn_ln_g, f32).reshape(1, D),
        "ln_b": np.asarray(L0_attn_ln_b, f32).reshape(1, D),
        "W_in": np.ascontiguousarray(np.asarray(L0_attn_W_in, f32)),
        "b_in": np.asarray(L0_attn_b_in, f32).reshape(1, F_IN),
        "W_out": np.ascontiguousarray(np.asarray(L0_attn_W_out, f32)),
        "b_out": np.asarray(L0_attn_b_out, f32).reshape(1, D),
        "gq": np.ascontiguousarray(np.asarray(L0_attn_gamma_q, f32).reshape(S, 1)),
        "bq": np.ascontiguousarray(np.asarray(L0_attn_beta_q, f32).reshape(S, 1)),
        "gk": np.ascontiguousarray(np.asarray(L0_attn_gamma_k, f32).reshape(S, 1)),
        "bk": np.ascontiguousarray(np.asarray(L0_attn_beta_k, f32).reshape(S, 1)),
        "clip_x": np.asarray(L0_ffn_clip_x, f32).reshape(1, 1),
        "clip_d": np.asarray(L0_ffn_clip_down, f32).reshape(1, 1),
        "w_gate": np.ascontiguousarray(np.asarray(L0_ffn_w_gate, f32)),
        "w_up": np.ascontiguousarray(np.asarray(L0_ffn_w_up, f32)),
        "w_down": np.ascontiguousarray(np.asarray(L0_ffn_w_down, f32)),
        "ident": np.eye(128, dtype=f32),
    }
    if _NC_CACHE[0] is None:
        _NC_CACHE[0] = build_program()
    nc = _NC_CACHE[0]
    in_maps = []
    for c in range(NCORES):
        m = dict(shared)
        m["x"] = np.ascontiguousarray(xr[T * c:T * (c + 1)])
        in_maps.append(m)
    res = run_bass_kernel_spmd(nc, in_maps, core_ids=list(range(NCORES)))
    out = np.concatenate([res.results[c]["out"] for c in range(NCORES)], axis=0)
    return out.reshape(1, L, D).astype(np.float32)


# version 13
# speedup vs baseline: 1.0110x; 1.0110x over previous
import numpy as np
from contextlib import ExitStack
import concourse.bass as bass
import concourse.mybir as mybir
from concourse.bass_utils import run_bass_kernel_spmd

F32 = mybir.dt.float32
BF16 = mybir.dt.bfloat16
I32 = mybir.dt.int32
AX = mybir.AxisListType.X
OP = mybir.AluOpType
ACT = mybir.ActivationFunctionType

NCORES = 8
L = 16384
T = L // NCORES
D = 1024
E = 2048
S = 128
F_IN = 2 * E + S
I_FF = 2816
NI = I_FF // 128
LN_EPS = 1e-5
NT = T // 128
KS = 512
NSUP = L // KS
NPASS = 4
QP = T // NPASS
INV_L = 1.0 / L


class Rec:
    def __init__(self, name):
        self.name = name
        self.items = []

    def wait(self, sem, val=None):
        v = sem.n if val is None else val
        assert v <= sem.n, f"{self.name}: wait {sem.name}>={v} > recorded {sem.n}"
        if v > 0:
            self.items.append(("w", sem, v))

    def op(self, fn, sem=None, amt=1):
        self.items.append(("o", fn, sem, amt if sem else 0))
        if sem is not None:
            sem.n += amt
            return sem.n
        return None


class Sem:
    def __init__(self, stack, nc, name):
        self.h = stack.enter_context(nc.semaphore(name))
        self.name = name
        self.n = 0


def build_program(debug=False):
    nc = bass.Bass("TRN2", target_bir_lowering=False, debug=False)

    dx = nc.dram_tensor("x", [T, D], F32, kind="ExternalInput").ap()
    d_ln_g = nc.dram_tensor("ln_g", [1, D], F32, kind="ExternalInput").ap()
    d_ln_b = nc.dram_tensor("ln_b", [1, D], F32, kind="ExternalInput").ap()
    d_Win = nc.dram_tensor("W_in", [D, F_IN], F32, kind="ExternalInput").ap()
    d_bin = nc.dram_tensor("b_in", [1, F_IN], F32, kind="ExternalInput").ap()
    d_Wout = nc.dram_tensor("W_out", [E, D], F32, kind="ExternalInput").ap()
    d_bout = nc.dram_tensor("b_out", [1, D], F32, kind="ExternalInput").ap()
    d_gq = nc.dram_tensor("gq", [S, 1], F32, kind="ExternalInput").ap()
    d_bq = nc.dram_tensor("bq", [S, 1], F32, kind="ExternalInput").ap()
    d_gk = nc.dram_tensor("gk", [S, 1], F32, kind="ExternalInput").ap()
    d_bk = nc.dram_tensor("bk", [S, 1], F32, kind="ExternalInput").ap()
    d_clip_x = nc.dram_tensor("clip_x", [1, 1], F32, kind="ExternalInput").ap()
    d_clip_d = nc.dram_tensor("clip_d", [1, 1], F32, kind="ExternalInput").ap()
    d_wg = nc.dram_tensor("w_gate", [I_FF, D], F32, kind="ExternalInput").ap()
    d_wu = nc.dram_tensor("w_up", [I_FF, D], F32, kind="ExternalInput").ap()
    d_wd = nc.dram_tensor("w_down", [D, I_FF], F32, kind="ExternalInput").ap()
    d_id = nc.dram_tensor("ident", [128, 128], F32, kind="ExternalInput").ap()
    d_out = nc.dram_tensor("out", [T, D], F32, kind="ExternalOutput").ap()
    d_dbg = nc.dram_tensor("dbg", [16 * 128, 2048], F32, kind="ExternalOutput").ap() if debug else None
    sDBG = [None]

    d_uavT = nc.dram_tensor("uavT", [T, E], BF16).ap()
    zg_in = nc.dram_tensor("zg_in", [S, T], BF16).ap()
    zg_out = nc.dram_tensor("zg_out", [S * NCORES, T], BF16, addr_space="Shared").ap()
    vg_in = nc.dram_tensor("vg_in", [T, E], BF16).ap()
    vg_outA = nc.dram_tensor("vg_outA", [L // 2, E], BF16, addr_space="Shared").ap()
    vg_outB = nc.dram_tensor("vg_outB", [L // 2, E], BF16, addr_space="Shared").ap()
    u_hbm = nc.dram_tensor("u_hbm", [T, E], BF16).ap()
    uav_hbm = nc.dram_tensor("uav_hbm", [T, E], F32).ap()
    x2_hbm = nc.dram_tensor("x2_hbm", [T, D], F32).ap()
    vh_hbm = nc.dram_tensor("vh_hbm", [I_FF, T], F32).ap()
    mx_in = nc.dram_tensor("mx_in", [128, 1], F32).ap()
    mx_out = nc.dram_tensor("mx_out", [128, 1], F32).ap()
    mh_in = nc.dram_tensor("mh_in", [128, 1], F32).ap()
    mh_out = nc.dram_tensor("mh_out", [128, 1], F32).ap()
    sc_hbm = nc.dram_tensor("sc_hbm", [1, 8], F32).ap()
    sd_hbm = nc.dram_tensor("sd_hbm", [1, D], F32).ap()

    RG = [list(range(NCORES))]
    SY, VE, SC, TE, GP = Rec("sync"), Rec("vector"), Rec("scalar"), Rec("tensor"), Rec("gpsimd")
    RECS = [SY, VE, SC, TE, GP]

    with ExitStack() as st0:
        def sem(name):
            return Sem(st0, nc, name)

        def sb(stk, name, shape, dtype):
            return stk.enter_context(nc.sbuf_tensor(name, shape, dtype))

        sT = sem("sT"); sV = sem("sV"); sS = sem("sS"); sCC = sem("sCC")
        sBC = sem("sBC")
        sXT = [sem("sXT0"), sem("sXT1"), sem("sXT2")]
        sWI = sem("sWI")
        sTRXN = sem("sTRXN")
        sUAVT = [sem("sUAVT0"), sem("sUAVT1")]
        sTRXQ = sem("sTRXQ")
        sTRKW = [sem("sTRKW0"), sem("sTRKW1")]
        sTRKD = sem("sTRKD")
        sSTZ = sem("sSTZ")
        sSTV = sem("sSTV")
        sSTU = sem("sSTU")
        sX2ST = [sem("sX2ST0"), sem("sX2ST1")]
        sSD = sem("sSD")
        sVL = [sem("sVL0"), sem("sVL1")]
        sGZ = sem("sGZ")
        sUL = [sem("sUL0"), sem("sUL1")]
        sWO = [sem("sWO0"), sem("sWO1")]
        sUT = [sem("sUT0"), sem("sUT1"), sem("sUT2")]
        sUL3 = [sem("sUL3_0"), sem("sUL3_1"), sem("sUL3_2")]
        sXR = [sem("sXR0"), sem("sXR1"), sem("sXR2")]
        sXQ = [sem("sXQ0"), sem("sXQ1"), sem("sXQ2")]
        sWS = [sem("sWS0"), sem("sWS1"), sem("sWS2"), sem("sWS3")]
        sWD = sem("sWD")
        sVH = [sem("sVH0"), sem("sVH1"), sem("sVH2")]
        sVHL = [sem("sVHL0"), sem("sVHL1"), sem("sVHL2"), sem("sVHL3")]
        sX2R = [sem("sX2R0"), sem("sX2R1"), sem("sX2R2")]
        sSCB = sem("sSCB")
        sOUTs = [sem("sOUT0"), sem("sOUT1"), sem("sOUT2")]
        sDBG[0] = sem("sDBG")
        ALL_SEMS = [sT, sV, sS, sCC, sBC, sWI, sTRXN, sTRXQ, sTRKD, sSTZ, sSTV,
                    sSTU, sSD, sGZ, sSCB, sWD] + sXT + sUAVT + sTRKW + sX2ST + \
                   sVL + sUL + sWO + sUT + sUL3 + sXR + sXQ + sWS + sVH + sVHL + sX2R + sOUTs + [sDBG[0]]

        ps_all = st0.enter_context(nc.psum_tensor("ps_all", [128, 4096], F32))
        ps = [ps_all[0:128, 512 * b:512 * (b + 1)] for b in range(8)]
        bankfree = {}

        def te_bank(b):
            if b in bankfree:
                TE.wait(*bankfree[b])

        ident = sb(st0, "ident_sb", [128, 128], F32)
        cols2 = sb(st0, "cols2", [128, 48], F32)
        tr_state = {"n": 0}
        stats = sb(st0, "stats", [128, 64], F32)
        cols = sb(st0, "cols", [128, 16], F32)
        one1 = sb(st0, "one1", [1, 16], F32)
        row128 = sb(st0, "row128", [1, 128], F32)
        scrS = sb(st0, "scrS", [128, 1024], F32)

        C_GQ, C_BQ, C_GK, C_BK, C_BZ, C_MX, C_MH, C_RSX, C_RSH, C_SX, C_SH = range(11)
        O_CLX, O_CLD, O_M, O_SIG, O_SX, O_RSX, O_T0, O_SH, O_RSH = range(9)

        def dma(rec, out_ap, in_ap, semx, transpose=False):
            rec.op(lambda e, o=out_ap, i=in_ap, tr=transpose:
                   e.dma_start(out=o, in_=i, transpose=tr), semx, 16)
            return semx.n

        def fill(n, bank=0):
            te_bank(bank)
            for _ in range(n):
                TE.op(lambda e, b=bank:
                      e.matmul(ps_all[0:128, 512 * b:512 * b + 128],
                               ident[:], ident[:], start=True, stop=True), None)

        def dbg_tap(k, src_ap, width, wait_sem=None, wait_val=None):
            if d_dbg is None:
                return
            if wait_sem is not None:
                GP.wait(wait_sem, wait_val)
            dma(GP, d_dbg[128 * k:128 * (k + 1), 0:width], src_ap, sDBG[0])

        def te_transpose(dst_bf, src_f32, nblk, src_step=128, dst_step=128, banks=(6, 7)):
            m = None
            for j in range(nblk):
                b = banks[tr_state["n"] % len(banks)]
                tr_state["n"] += 1
                te_bank(b)
                TE.op(lambda e, jj=j, bb=b, s=src_f32, st_=src_step:
                      e.transpose(ps_all[0:128, 512 * bb:512 * bb + 128],
                                  s[0:128, st_ * jj:st_ * jj + 128], ident[:]), sT)
                mt = sT.n
                SC.wait(sT, mt)
                SC.wait(sS, sS.n)
                m = SC.op(lambda e, jj=j, bb=b, d=dst_bf, st_=dst_step:
                          e.copy(out=d[0:128, st_ * jj:st_ * jj + 128], in_=ps_all[0:128, 512 * bb:512 * bb + 128]), sS)
                bankfree[b] = (sS, m)
            return m

        def barrier(exclude=()):
            snap = [(s, s.n) for s in ALL_SEMS if s.n > 0 and s not in exclude]
            for rec in RECS:
                for s, v in snap:
                    rec.wait(s, v)

        def ln_rstd(src_col, dst_col):
            t0, t1, t2, t3 = (stats[:, c:c + 1] for c in (60, 61, 62, 63))
            VE.wait(sV, sV.n)
            mvar = VE.op(lambda e, s=src_col: e.tensor_copy(out=t0, in_=s), sV)
            SC.wait(sV, mvar)
            SC.wait(sS, sS.n)
            msq = SC.op(lambda e: e.activation(out=t1, in_=t0, func=ACT.Sqrt), sS)
            VE.wait(sS, msq)
            VE.op(lambda e: e.reciprocal(out=t2, in_=t1), sV)
            VE.wait(sV, sV.n)
            VE.op(lambda e: e.tensor_tensor(out=t3, in0=t2, in1=t2, op=OP.mult), sV)
            VE.wait(sV, sV.n)
            VE.op(lambda e, s=src_col: e.tensor_tensor(out=t1, in0=t3, in1=s, op=OP.mult), sV)
            VE.wait(sV, sV.n)
            VE.op(lambda e: e.tensor_scalar(out=t1, in0=t1, scalar1=3.0, scalar2=-0.5,
                                            op0=OP.subtract, op1=OP.mult), sV)
            VE.wait(sV, sV.n)
            return VE.op(lambda e, d=dst_col: e.tensor_tensor(out=d, in0=t2, in1=t1, op=OP.mult), sV)

        def ln_stats(a_ap, mu_col, rstd_col):
            s0, s1, s2, s3 = (stats[:, c:c + 1] for c in (8, 9, 10, 11))
            VE.wait(sV, sV.n)
            VE.op(lambda e, a=a_ap: e.reduce_sum(out=s0, in_=a, axis=AX), sV)
            SC.wait(sV, sV.n)
            SC.wait(sS, sS.n)
            msq = SC.op(lambda e, a=a_ap: e.activation(out=scrS[:], in_=a, func=ACT.Square,
                                                       accum_out=s1), sS)
            VE.wait(sV, sV.n)
            VE.op(lambda e, m=mu_col: e.tensor_scalar_mul(m, s0, 1.0 / D), sV)
            VE.wait(sS, msq)
            VE.wait(sV, sV.n)
            VE.op(lambda e: e.tensor_scalar_mul(s2, s1, 1.0 / D), sV)
            VE.wait(sV, sV.n)
            VE.op(lambda e, m=mu_col: e.tensor_tensor(out=s3, in0=m, in1=m, op=OP.mult), sV)
            VE.wait(sV, sV.n)
            VE.op(lambda e: e.tensor_scalar(out=s2, in0=s2, scalar1=s3, scalar2=LN_EPS,
                                            op0=OP.subtract, op1=OP.add), sV)
            return ln_rstd(s2, rstd_col)

        def recip_nr(src, dst):
            t = one1[0:1, O_T0:O_T0 + 1]
            VE.wait(sV, sV.n)
            VE.op(lambda e, s=src, d=dst: e.reciprocal(out=d, in_=s), sV)
            VE.wait(sV, sV.n)
            VE.op(lambda e, s=src, d=dst: e.tensor_tensor(out=t, in0=s, in1=d, op=OP.mult), sV)
            VE.wait(sV, sV.n)
            VE.op(lambda e: e.tensor_scalar(out=t, in0=t, scalar1=2.0, scalar2=-1.0,
                                            op0=OP.subtract, op1=OP.mult), sV)
            VE.wait(sV, sV.n)
            return VE.op(lambda e, d=dst: e.tensor_tensor(out=d, in0=d, in1=t, op=OP.mult), sV)

        def rw_from_max(mx_col, rw_col):
            r0, tt = stats[:, 17:18], stats[:, 18:19]
            VE.wait(sV, sV.n)
            VE.op(lambda e, m=mx_col: e.reciprocal(out=r0, in_=m), sV)
            VE.wait(sV, sV.n)
            VE.op(lambda e, m=mx_col: e.tensor_tensor(out=tt, in0=m, in1=r0, op=OP.mult), sV)
            VE.wait(sV, sV.n)
            VE.op(lambda e: e.tensor_scalar(out=tt, in0=tt, scalar1=2.0, scalar2=-1.0,
                                            op0=OP.subtract, op1=OP.mult), sV)
            VE.wait(sV, sV.n)
            VE.op(lambda e: e.tensor_tensor(out=r0, in0=r0, in1=tt, op=OP.mult), sV)
            VE.wait(sV, sV.n)
            return VE.op(lambda e, d=rw_col: e.tensor_scalar_mul(d, r0, 127.0), sV)

        def quant(in_f32, rcol, itile, f32tmp, out_bf, pre_sc_waits=(), pre_ve_waits=()):
            for w_ in pre_ve_waits:
                VE.wait(*w_)
            VE.wait(sV, sV.n)
            m1 = VE.op(lambda e, s=in_f32, d=f32tmp, r=rcol:
                       e.tensor_scalar(out=d, in0=s, scalar1=r, scalar2=-128.0,
                                       op0=OP.mult, op1=OP.max), sV)
            VE.wait(sV, sV.n)
            m2 = VE.op(lambda e, s=f32tmp, d=itile:
                       e.tensor_scalar(out=d, in0=s, scalar1=127.0, scalar2=None,
                                       op0=OP.min), sV)
            for w_ in pre_sc_waits:
                SC.wait(*w_)
            SC.wait(sV, m2)
            SC.wait(sS, sS.n)
            return SC.op(lambda e, s=itile, d=out_bf: e.copy(out=d, in_=s), sS)

        stA = ExitStack()
        scr1 = sb(stA, "scr1", [128, 1024], F32)
        q_fm = sb(stA, "q_fm", [128, T], BF16)
        k_fm = sb(stA, "k_fm", [128, L], BF16)

        stP12 = ExitStack()
        xt = [sb(stP12, f"xt{i}", [128, D], F32) for i in range(3)]
        lng_bc = sb(stP12, "lng_bc", [128, D], F32)
        lnb_bc = sb(stP12, "lnb_bc", [128, D], F32)
        xn3 = sb(stP12, "xn3", [128, D], F32)
        xn_fm = sb(stP12, "xn_fm", [128, 8 * T], BF16)
        wi_st = sb(stP12, "wi_st", [128, F_IN], F32)
        wi_bf = sb(stP12, "wi_bf", [128, 8 * F_IN], BF16)
        z_bf = sb(stP12, "z_bf", [128, T], BF16)
        uv_sub = [sb(stP12, f"uvsub{i}", [128, 512], BF16) for i in range(2)]
        zch = sb(stP12, "zch", [128, T], BF16)
        ones_r = sb(stP12, "ones_r", [1, T], BF16)
        binr = sb(stP12, "binr", [1, F_IN], BF16)

        dma(SY, lng_bc[:], d_ln_g.partition_broadcast(128), sBC)
        dma(SY, lnb_bc[:], d_ln_b.partition_broadcast(128), sBC)
        dma(GP, binr[0:1, :], d_bin, sBC)
        dma(SY, cols[:, C_GQ:C_GQ + 1], d_gq, sBC)
        dma(SY, cols[:, C_BQ:C_BQ + 1], d_bq, sBC)
        dma(SY, cols[:, C_GK:C_GK + 1], d_gk, sBC)
        dma(SY, cols[:, C_BK:C_BK + 1], d_bk, sBC)
        dma(SY, cols[:, C_BZ:C_BZ + 1], d_bin[0:1, 2 * E:F_IN].rearrange("o p -> p o"), sBC)
        dma(SY, ident[:], d_id, sBC)
        dma(SY, one1[0:1, O_CLX:O_CLX + 1], d_clip_x, sBC)
        dma(SY, one1[0:1, O_CLD:O_CLD + 1], d_clip_d, sBC)
        m_bc = sBC.n
        VE.wait(sV, sV.n)
        m_ones = VE.op(lambda e: e.memset(ones_r[0:1, :], 1.0), sV)
        TE.wait(sBC, m_bc)

        m_sumA = {}
        m_sqA = {}
        m_wicv = {}
        dma(GP, wi_st[:], d_Win[0:128, :], sWI)
        for i in range(2):
            dma(SY, xt[i][:], dx[128 * i:128 * (i + 1), :], sXT[i])
        for i in range(NT):
            sl = i % 3
            if i + 2 < NT:
                s2 = (i + 2) % 3
                if i >= 1:
                    SY.wait(sV, m_sumA[i - 1])
                    SY.wait(sS, m_sqA[i - 1])
                dma(SY, xt[s2][:], dx[128 * (i + 2):128 * (i + 3), :], sXT[s2])
            VE.wait(sXT[sl], sXT[sl].n)
            VE.wait(sV, sV.n)
            m_sumA[i] = VE.op(lambda e, s=xt[sl], ii=i:
                              e.reduce_sum(out=stats[:, ii:ii + 1], in_=s[:], axis=AX), sV)
            SC.wait(sXT[sl], sXT[sl].n)
            SC.wait(sS, sS.n)
            m_sqA[i] = SC.op(lambda e, s=xt[sl], ii=i:
                             e.activation(out=scrS[:], in_=s[:], func=ACT.Square,
                                          accum_out=stats[:, 16 + ii:17 + ii]), sS)
            if i % 2 == 1 and i // 2 < 8:
                d_ = i // 2
                SC.wait(sWI, sWI.n)
                SC.wait(sS, sS.n)
                m_wicv[d_] = SC.op(lambda e, dd=d_:
                                   e.copy(out=wi_bf[0:128, F_IN * dd:F_IN * (dd + 1)], in_=wi_st[:]), sS)
                if d_ + 1 < 8:
                    GP.wait(sS, m_wicv[d_])
                    dma(GP, wi_st[:], d_Win[128 * (d_ + 1):128 * (d_ + 2), :], sWI)
            TE.wait(sS, m_sqA[i])
            fill(14, bank=0)
        VE.wait(sS, m_sqA[NT - 1])
        VE.wait(sV, sV.n)
        VE.op(lambda e: e.tensor_scalar_mul(stats[:, 32:48], stats[:, 0:16], 1.0 / D), sV)
        VE.wait(sV, sV.n)
        VE.op(lambda e: e.tensor_scalar_mul(stats[:, 16:32], stats[:, 16:32], 1.0 / D), sV)
        VE.wait(sV, sV.n)
        VE.op(lambda e: e.tensor_tensor(out=stats[:, 0:16], in0=stats[:, 32:48], in1=stats[:, 32:48], op=OP.mult), sV)
        VE.wait(sV, sV.n)
        VE.op(lambda e: e.tensor_tensor(out=stats[:, 16:32], in0=stats[:, 16:32], in1=stats[:, 0:16], op=OP.subtract), sV)
        VE.wait(sV, sV.n)
        mvB = VE.op(lambda e: e.tensor_scalar_add(stats[:, 16:32], stats[:, 16:32], LN_EPS), sV)
        SC.wait(sV, mvB)
        SC.wait(sS, sS.n)
        msB = SC.op(lambda e: e.activation(out=stats[:, 0:16], in_=stats[:, 16:32], func=ACT.Sqrt), sS)
        VE.wait(sS, msB)
        VE.op(lambda e: e.reciprocal(out=stats[:, 48:64], in_=stats[:, 0:16]), sV)
        VE.wait(sV, sV.n)
        VE.op(lambda e: e.tensor_tensor(out=stats[:, 0:16], in0=stats[:, 48:64], in1=stats[:, 48:64], op=OP.mult), sV)
        VE.wait(sV, sV.n)
        VE.op(lambda e: e.tensor_tensor(out=stats[:, 0:16], in0=stats[:, 0:16], in1=stats[:, 16:32], op=OP.mult), sV)
        VE.wait(sV, sV.n)
        VE.op(lambda e: e.tensor_scalar(out=stats[:, 0:16], in0=stats[:, 0:16], scalar1=3.0, scalar2=-0.5,
                                        op0=OP.subtract, op1=OP.mult), sV)
        VE.wait(sV, sV.n)
        VE.op(lambda e: e.tensor_tensor(out=stats[:, 16:32], in0=stats[:, 48:64], in1=stats[:, 0:16], op=OP.mult), sV)
        m_rstdB = sV.n
        m_xn = {}
        m_scr1 = {}
        m_xntr = {}
        m_xntr_te = {}
        for i in range(2):
            SY.wait(sV, m_sumA[NT - 1])
            SY.wait(sS, m_sqA[NT - 1])
            dma(SY, xt[i][:], dx[128 * i:128 * (i + 1), :], sXT[i])
        for i in range(NT):
            sl = i % 3
            if i + 2 < NT:
                s2 = (i + 2) % 3
                if i >= 1:
                    SY.wait(sV, m_scr1[i - 1])
                SY.wait(sS, m_sqA[NT - 1])
                dma(SY, xt[s2][:], dx[128 * (i + 2):128 * (i + 3), :], sXT[s2])
            VE.wait(sXT[sl], sXT[sl].n)
            VE.wait(sBC, m_bc)
            VE.wait(sV, max(sV.n, m_rstdB))
            m_scr1[i] = VE.op(lambda e, s=xt[sl], ii=i:
                              e.scalar_tensor_tensor(out=scr1[:], in0=s[:], scalar=stats[:, 32 + ii:33 + ii],
                                                     in1=lng_bc[:], op0=OP.subtract, op1=OP.mult), sV)
            VE.wait(sV, sV.n)
            if i >= 2:
                VE.wait(sT, m_xntr_te[i - 2])
            xnbuf = scrS if i % 2 == 0 else xn3
            if i % 2 == 0:
                VE.wait(sS, m_sqA[NT - 1])
            m_xn[i] = VE.op(lambda e, d=xnbuf, ii=i:
                            e.scalar_tensor_tensor(out=d[:], in0=scr1[:], scalar=stats[:, 16 + ii:17 + ii],
                                                   in1=lnb_bc[:], op0=OP.mult, op1=OP.add), sV)
            TE.wait(sV, m_xn[i])
            m_xntr[i] = te_transpose(xn_fm[:, 128 * i:], xnbuf[:], 8,
                                     src_step=128, dst_step=T, banks=(0, 1, 2, 3, 4, 5, 6, 7))
            m_xntr_te[i] = sT.n
            fill(6, bank=(i % 2) * 4)
        m_xn_tr = m_xntr[NT - 1]
        dbg_tap(0, xn3[:], D, sV, m_xn[NT - 1])

        m_zev = {}
        for n in range(4):
            TE.wait(sS, m_xntr[4 * n + 3])
            TE.wait(sS, m_wicv[7])
            te_bank(n)
            for d_ in range(8):
                TE.op(lambda e, dd=d_, nn=n:
                      e.matmul(ps[nn],
                               wi_bf[0:128, F_IN * dd + 2 * E:F_IN * dd + 2 * E + 128],
                               xn_fm[0:128, dd * T + 512 * nn:dd * T + 512 * (nn + 1)],
                               start=(dd == 0), stop=(dd == 7)),
                      sT if d_ == 7 else None)
            m_zmm = sT.n
            SC.wait(sT, m_zmm)
            SC.wait(sBC, m_bc)
            SC.wait(sS, sS.n)
            m_zev[n] = SC.op(lambda e, nn=n:
                             e.activation(out=z_bf[:, 512 * nn:512 * (nn + 1)], in_=ps[nn],
                                          func=ACT.Silu, bias=cols[:, C_BZ:C_BZ + 1]), sS)
            bankfree[n] = (sS, m_zev[n])
        VE.wait(sS, m_zev[3])
        VE.wait(sBC, m_bc)
        VE.wait(sV, sV.n)
        m_q = VE.op(lambda e: e.tensor_scalar(out=q_fm[:], in0=z_bf[:], scalar1=cols[:, C_GQ:C_GQ + 1],
                                              scalar2=cols[:, C_BQ:C_BQ + 1], op0=OP.mult, op1=OP.add), sV)
        SY.wait(sS, m_zev[3])
        dma(SY, zg_in, z_bf[:], sSTZ)
        dbg_tap(1, z_bf[:], T, sS, m_zev[3])
        dbg_tap(2, q_fm[:], T, sV, m_q)

        GP.wait(sSTZ, sSTZ.n)
        GP.op(lambda e: e.collective_compute("AllGather", OP.bypass, replica_groups=RG,
                                             ins=[zg_in], outs=[zg_out]), sCC)
        GP.wait(sCC, sCC.n)

        ev = 0
        m_uvadd = {}
        m_uvev = {}
        m_substore = {}
        for w in range(2):
            col0 = E if w == 0 else 0
            for t in range(NT):
                if w == 0 and t == NT // 2:
                    GP.wait(sSTV, sSTV.n)
                    GP.op(lambda e: e.collective_compute("AllGather", OP.bypass, replica_groups=RG,
                                                         ins=[vg_in[0:T // 2, :]], outs=[vg_outA]), sCC)
                    m_vgA = sCC.n
                for n in range(4):
                    b = ev % 8
                    te_bank(b)
                    if ev == 0:
                        TE.wait(sV, m_ones)
                        TE.wait(sBC, m_bc)
                    for d_ in range(9):
                        if d_ < 8:
                            TE.op(lambda e, dd=d_, tt=t, c0=col0, nn=n, bb=b:
                                  e.matmul(ps[bb],
                                           xn_fm[0:128, dd * T + 128 * tt:dd * T + 128 * (tt + 1)],
                                           wi_bf[0:128, F_IN * dd + c0 + 512 * nn:F_IN * dd + c0 + 512 * (nn + 1)],
                                           start=(dd == 0), stop=False),
                                  None)
                        else:
                            TE.op(lambda e, tt=t, c0=col0, nn=n, bb=b:
                                  e.matmul(ps[bb],
                                           ones_r[0:1, 128 * tt:128 * (tt + 1)],
                                           binr[0:1, c0 + 512 * nn:c0 + 512 * (nn + 1)],
                                           start=False, stop=True),
                                  sT)
                    m_mm = sT.n
                    SC.wait(sT, m_mm)
                    if ev >= 2:
                        SC.wait(*m_substore[ev - 2])
                    SC.wait(sS, sS.n)
                    m_uvev[ev] = SC.op(
                        lambda e, esl=ev % 2, bb=b:
                        e.activation(out=uv_sub[esl][:], in_=ps[bb], func=ACT.Silu), sS)
                    bankfree[b] = (sS, m_uvev[ev])
                    SY.wait(sS, m_uvev[ev])
                    dst, stx = (vg_in, sSTV) if w == 0 else (u_hbm, sSTU)
                    dma(SY, dst[128 * t:128 * (t + 1), 512 * n:512 * (n + 1)], uv_sub[ev % 2][:], stx)
                    m_substore[ev] = (stx, stx.n)
                    ev += 1
            if w == 0:
                GP.wait(sSTV, sSTV.n)
                GP.op(lambda e: e.collective_compute("AllGather", OP.bypass, replica_groups=RG,
                                                     ins=[vg_in[T // 2:T, :]], outs=[vg_outB]), sCC)
                m_vg = sCC.n

        m_kch = {}
        for r in range(NCORES):
            if r >= 1:
                GP.wait(sV, m_kch[r - 1])
            dma(GP, zch[:], zg_out[S * r:S * (r + 1), :], sGZ)
            VE.wait(sGZ, sGZ.n)
            VE.wait(sBC, m_bc)
            VE.wait(sV, sV.n)
            m_kch[r] = VE.op(lambda e, rr=r:
                             e.tensor_scalar(out=k_fm[:, T * rr:T * (rr + 1)], in0=zch[:],
                                             scalar1=cols[:, C_GK:C_GK + 1],
                                             scalar2=cols[:, C_BK:C_BK + 1],
                                             op0=OP.mult, op1=OP.add), sV)
            TE.wait(sV, m_kch[r])
            fill(14, bank=0)
        m_k = m_kch[NCORES - 1]
        dbg_tap(3, k_fm[:, 0:T], T, sV, m_k)
        fill(70, bank=0)
        TE.wait(sCC, m_vgA)
        fill(70, bank=0)

        stP12.close()
        barrier(exclude=(sCC,))
        stP3 = ExitStack()
        v_sup = [sb(stP3, f"vsup{i}", [128, 4 * E], BF16) for i in range(4)]
        sc_sb = [sb(stP3, f"scsb{i}", [128, 4 * 512], BF16) for i in range(4)]
        rl_tmp = sb(stP3, "rltmp", [128, 512], F32)
        av_acc = sb(stP3, "av_acc", [128, 4 * E], F32)
        u_t = [sb(stP3, f"ut{i}", [128, E], BF16) for i in range(3)]
        uav_f = [sb(stP3, f"uavf{i}", [128, E], F32) for i in range(2)]
        uavT_st = [sb(stP3, f"uavT{i}", [128, E], BF16) for i in range(2)]

        NPAIR = NSUP // 2
        m_avpair = {}
        m_sq = {}
        m_vld = {}
        m_uav = {}
        grp_box = [0]

        def emit_scores_kb(p, s_, kb):
            sl4 = s_ % 4
            idx = p * NSUP + s_
            if kb == 0:
                if idx >= 4:
                    pp, ss = (p, s_ - 4) if s_ >= 4 else (p - 1, NSUP - 4 + s_)
                    GP.wait(sT, m_avpair[(pp, ss // 2)])
                r_, o_ = divmod(KS * s_, T)
                if o_ < T // 2:
                    vsrc = vg_outA[(T // 2) * r_ + o_:(T // 2) * r_ + o_ + KS, :]
                    GP.wait(sCC, m_vgA)
                else:
                    vsrc = vg_outB[(T // 2) * r_ + o_ - T // 2:(T // 2) * r_ + o_ - T // 2 + KS, :]
                    GP.wait(sCC, m_vg)
                dma(GP, v_sup[s_ % 4][:], vsrc, sVL[s_ % 2])
                m_vld[(p, s_)] = sVL[s_ % 2].n
            if idx == 0 and kb == 0:
                TE.wait(sV, m_k)
            b = kb % 2
            te_bank(b)
            TE.op(lambda e, pp=p, ss=s_, kk=kb, bb=b:
                  e.matmul(ps[bb],
                           k_fm[0:128, KS * ss + 128 * kk:KS * ss + 128 * (kk + 1)],
                           q_fm[0:128, QP * pp:QP * (pp + 1)],
                           start=True, stop=True), sT)
            m_scmm = sT.n
            SC.wait(sT, m_scmm)
            SC.wait(sS, sS.n)
            m_rl = SC.op(lambda e, bb=b:
                         e.activation(out=rl_tmp[:], in_=ps[bb], func=ACT.Relu, scale=INV_L), sS)
            bankfree[b] = (sS, m_rl)
            if kb == 0 and idx >= 4:
                pp, ss = (p, s_ - 4) if s_ >= 4 else (p - 1, NSUP - 4 + s_)
                SC.wait(sT, m_avpair[(pp, ss // 2)])
            SC.wait(sS, sS.n)
            m_sq[(p, s_, kb)] = SC.op(
                lambda e, so=sl4, kk=kb:
                e.activation(out=sc_sb[so][0:128, 512 * kk:512 * (kk + 1)],
                             in_=rl_tmp[:], func=ACT.Square), sS)

        def emit_avpair(p, pr):
            s0, s1 = 2 * pr, 2 * pr + 1
            nxt = []
            if pr + 1 < NPAIR:
                nxt = [(2 * (pr + 1) + i // 4, i % 4) for i in range(8)]
            elif p + 1 < NPASS:
                nxt = [(0, None)] * 0
            TE.wait(sS, m_sq[(p, s1, 3)])
            TE.wait(sVL[s0 % 2], m_vld[(p, s0)])
            TE.wait(sVL[s1 % 2], m_vld[(p, s1)])
            gi = 0
            for qb in range(4):
                for g in range(2):
                    r = grp_box[0] % 2
                    grp_box[0] += 1
                    b0, b1 = 2 + 2 * r, 3 + 2 * r
                    te_bank(b0)
                    te_bank(b1)
                    for s_i in (s0, s1):
                        sl4 = s_i % 4
                        for kb in range(4):
                            for j in range(2):
                                TE.op(lambda e, so=sl4, kk=kb, qq=qb, gg=g, jj=j,
                                      bb=(b0 if j == 0 else b1), st=(s_i == s0 and kb == 0),
                                      sp=(s_i == s1 and kb == 3):
                                      e.matmul(ps[bb],
                                               sc_sb[so][0:128, 512 * kk + 128 * qq:512 * kk + 128 * (qq + 1)],
                                               v_sup[s_i % 4][0:128, E * kk + 1024 * gg + 512 * jj:
                                                              E * kk + 1024 * gg + 512 * (jj + 1)],
                                               start=st, stop=sp),
                                      sT if (s_i == s1 and kb == 3 and j == 1) else None)
                    m_grp = sT.n
                    if gi < len(nxt):
                        emit_scores_kb(p, nxt[gi][0], nxt[gi][1])
                        gi += 1
                    VE.wait(sT, m_grp)
                    VE.wait(sV, sV.n)
                    dsl = av_acc[:, E * qb + 1024 * g:E * qb + 1024 * (g + 1)]
                    pss = ps_all[0:128, 512 * b0:512 * b0 + 1024]
                    if pr == 0:
                        VE.op(lambda e, d=dsl, b_=pss: e.tensor_copy(out=d, in_=b_), sV)
                    else:
                        VE.op(lambda e, d=dsl, b_=pss: e.tensor_add(out=d, in0=b_, in1=d), sV)
                    bankfree[b0] = (sV, sV.n)
                    bankfree[b1] = (sV, sV.n)
            m_avpair[(p, pr)] = sT.n

        m_uavst = {}
        m_uavtrA = {}
        for p in range(NPASS):
            if p == 0:
                for kb in range(4):
                    emit_scores_kb(p, 0, kb)
                for kb in range(4):
                    emit_scores_kb(p, 1, kb)
            for pr in range(NPAIR):
                emit_avpair(p, pr)
            if p + 1 < NPASS:
                for kb in range(4):
                    emit_scores_kb(p + 1, 0, kb)
                for kb in range(4):
                    emit_scores_kb(p + 1, 1, kb)
            if p == 0:
                SY.wait(sSTU, sSTU.n)
                for g0 in range(2):
                    dma(SY, u_t[g0][:], u_hbm[128 * g0:128 * (g0 + 1), :], sUT[g0])
            for qb in range(4):
                sl = qb % 2
                gq_ = 4 * p + qb
                usl = gq_ % 3
                if gq_ + 2 < 4 * NPASS:
                    s2 = (gq_ + 2) % 3
                    if gq_ >= 1:
                        SY.wait(sV, m_uav[gq_ - 1])
                    dma(SY, u_t[s2][:], u_hbm[128 * (gq_ + 2):128 * (gq_ + 3), :], sUT[s2])
                VE.wait(sUT[usl], sUT[usl].n)
                if gq_ >= 2:
                    VE.wait(sT, m_uavtrA[gq_ - 2])
                VE.wait(sV, sV.n)
                m_uav[gq_] = VE.op(
                    lambda e, s2=sl, u2=usl, qq=qb:
                    e.tensor_mul(out=uav_f[s2][:], in0=u_t[u2][:], in1=av_acc[:, E * qq:E * (qq + 1)]), sV)
                TE.wait(sV, m_uav[gq_])
                if gq_ >= 2:
                    TE.wait(sUAVT[sl], sUAVT[sl].n)
                te_transpose(uavT_st[sl], uav_f[sl][:], 16, src_step=128, dst_step=128,
                             banks=(6, 7))
                m_uavtrA[gq_] = sT.n
                m_trevA = sS.n
                GP.wait(sS, m_trevA)
                m_uavst[gq_] = dma(GP, d_uavT[128 * gq_:128 * (gq_ + 1), :], uavT_st[sl][:],
                                   sUAVT[sl])

        stP3.close()
        stA.close()
        barrier()
        stW = ExitStack()
        w_st = [sb(stW, f"wst{i}", [128, D], F32) for i in range(4)]
        wmx = sb(stW, "wmx", [128, 96], F32)
        stP35 = ExitStack()
        uavld = [sb(stP35, f"uavld{i}", [128, E], BF16) for i in range(3)]
        Wout_bf = sb(stP35, "Wout_bf", [128, 16 * D], BF16)
        w35 = [sb(stP35, f"w35{i}", [128, D], F32) for i in range(2)]
        bout_bc = sb(stP35, "bout_bc", [128, D], F32)
        xr_t = [sb(stP35, f"xrt{i}", [128, D], F32) for i in range(3)]
        a1_t = [sb(stP35, f"a1t{i}", [128, D], F32) for i in range(2)]
        scr35 = sb(stP35, "scr35", [128, D], F32)

        fill(25, bank=0)
        dma(SY, bout_bc[:], d_bout.partition_broadcast(128), sBC)
        m_bout = sBC.n
        m_wocv = {}
        for j in range(16):
            wsl = j % 2
            rq = GP if wsl == 0 else SY
            if j >= 2:
                rq.wait(sS, m_wocv[j - 2])
            dma(rq, w35[wsl][:], d_Wout[128 * j:128 * (j + 1), :], sWO[wsl])
            SC.wait(sWO[wsl], sWO[wsl].n)
            SC.wait(sS, sS.n)
            m_wocv[j] = SC.op(lambda e, jj=j, s2=wsl:
                              e.copy(out=Wout_bf[0:128, D * jj:D * (jj + 1)], in_=w35[s2][:]), sS)
            TE.wait(sS, m_wocv[j])
            fill(10, bank=0)
        m_wo_all = m_wocv[15]

        m_uavtr = {}
        m_wo = {}
        m_sq2q = {}
        m_mxq = {}
        m_a1st = {}
        wo_box = [0]

        for t in range(2):
            dma(SY, uavld[t][:], d_uavT[128 * t:128 * (t + 1), :], sUL3[t])
            dma(GP, xr_t[t][:], dx[128 * t:128 * (t + 1), :], sXR[t])
        for t in range(NT):
            sl = t % 2
            ul = t % 3
            if t + 2 < NT:
                s2 = (t + 2) % 3
                if t >= 1:
                    SY.wait(sT, m_wo[t - 1])
                dma(SY, uavld[s2][:], d_uavT[128 * (t + 2):128 * (t + 3), :], sUL3[s2])
                if t >= 1:
                    GP.wait(sV, m_mxq[t - 1])
                dma(GP, xr_t[s2][:], dx[128 * (t + 2):128 * (t + 3), :], sXR[s2])

            TE.wait(sUL3[ul], sUL3[ul].n)
            for dn in range(2):
                b = wo_box[0] % 4
                wo_box[0] += 1
                te_bank(b)
                for j in range(16):
                    if t == 0 and dn == 0:
                        TE.wait(sS, m_wocv[j])
                    TE.op(lambda e, jj=j, dd=dn, bb=b, u2=ul:
                          e.matmul(ps[bb],
                                   uavld[u2][0:128, 128 * jj:128 * (jj + 1)],
                                   Wout_bf[0:128, D * jj + 512 * dd:D * jj + 512 * (dd + 1)],
                                   start=(jj == 0), stop=(jj == 15)),
                          sT if j == 15 else None)
                m_wodn = sT.n
                VE.wait(sT, m_wodn)
                VE.wait(sXR[ul], sXR[ul].n)
                VE.wait(sBC, m_bout)
                if t >= 2 and dn == 0:
                    VE.wait(sS, m_sq2q[t - 2])
                    VE.wait(sX2ST[sl], sX2ST[sl].n)
                VE.wait(sV, sV.n)
                VE.op(lambda e, bb=b, dd=dn, s2=sl:
                      e.tensor_add(out=a1_t[s2][:, 512 * dd:512 * (dd + 1)], in0=ps[bb],
                                   in1=bout_bc[:, 512 * dd:512 * (dd + 1)]), sV)
                VE.wait(sV, sV.n)
                mep = VE.op(lambda e, s2=sl, u2=ul, dd=dn:
                            e.scalar_tensor_tensor(out=a1_t[s2][:, 512 * dd:512 * (dd + 1)],
                                                   in0=xr_t[u2][:, 512 * dd:512 * (dd + 1)],
                                                   scalar=2.0, in1=a1_t[s2][:, 512 * dd:512 * (dd + 1)],
                                                   op0=OP.mult, op1=OP.add), sV)
                bankfree[b] = (sV, mep)
            m_wo[t] = sT.n
            m_a1 = sV.n
            VE.wait(sV, m_a1)
            VE.op(lambda e, gi=t, s2=sl: e.reduce_sum(out=stats[:, gi:gi + 1], in_=a1_t[s2][:], axis=AX), sV)
            SC.wait(sV, sV.n)
            SC.wait(sS, sS.n)
            m_sq2q[t] = SC.op(lambda e, gi=t, s2=sl:
                              e.activation(out=scrS[:], in_=a1_t[s2][:], func=ACT.Square,
                                           accum_out=stats[:, 16 + gi:17 + gi]), sS)
            VE.wait(sV, sV.n)
            VE.op(lambda e, gi=t: e.tensor_scalar_mul(stats[:, 60:61], stats[:, gi:gi + 1], 1.0 / D), sV)
            VE.wait(sV, sV.n)
            VE.op(lambda e, s2=sl: e.tensor_single_scalar(out=scr35[:], in_=a1_t[s2][:],
                                                          scalar=stats[:, 60:61], op=OP.subtract), sV)
            VE.wait(sV, sV.n)
            m_mxq[t] = VE.op(lambda e, gi=t:
                             e.reduce_max(out=stats[:, 32 + gi:33 + gi], in_=scr35[:], axis=AX,
                                          apply_absolute_value=True), sV)
            GP.wait(sV, m_mxq[t])
            GP.wait(sS, m_sq2q[t])
            m_a1st[t] = dma(GP, x2_hbm[128 * t:128 * (t + 1), :], a1_t[sl][:], sX2ST[sl])

        VE.wait(sS, sS.n)
        VE.wait(sV, sV.n)
        VE.op(lambda e: e.tensor_scalar_mul(cols2[:, 0:16], stats[:, 0:16], 1.0 / D), sV)
        VE.wait(sV, sV.n)
        VE.op(lambda e: e.tensor_scalar_mul(cols2[:, 32:48], stats[:, 16:32], 1.0 / D), sV)
        VE.wait(sV, sV.n)
        VE.op(lambda e: e.tensor_tensor(out=stats[:, 16:32], in0=cols2[:, 0:16], in1=cols2[:, 0:16], op=OP.mult), sV)
        VE.wait(sV, sV.n)
        VE.op(lambda e: e.tensor_tensor(out=cols2[:, 32:48], in0=cols2[:, 32:48], in1=stats[:, 16:32], op=OP.subtract), sV)
        VE.wait(sV, sV.n)
        mv2 = VE.op(lambda e: e.tensor_scalar_add(cols2[:, 32:48], cols2[:, 32:48], LN_EPS), sV)
        SC.wait(sV, mv2)
        SC.wait(sS, sS.n)
        ms2 = SC.op(lambda e: e.activation(out=stats[:, 16:32], in_=cols2[:, 32:48], func=ACT.Sqrt), sS)
        VE.wait(sS, ms2)
        VE.op(lambda e: e.reciprocal(out=cols2[:, 16:32], in_=stats[:, 16:32]), sV)
        VE.wait(sV, sV.n)
        VE.op(lambda e: e.tensor_tensor(out=stats[:, 16:32], in0=cols2[:, 16:32], in1=cols2[:, 16:32], op=OP.mult), sV)
        VE.wait(sV, sV.n)
        VE.op(lambda e: e.tensor_tensor(out=stats[:, 16:32], in0=stats[:, 16:32], in1=cols2[:, 32:48], op=OP.mult), sV)
        VE.wait(sV, sV.n)
        VE.op(lambda e: e.tensor_scalar(out=stats[:, 16:32], in0=stats[:, 16:32], scalar1=3.0, scalar2=-0.5,
                                        op0=OP.subtract, op1=OP.mult), sV)
        VE.wait(sV, sV.n)
        VE.op(lambda e: e.tensor_tensor(out=cols2[:, 16:32], in0=cols2[:, 16:32], in1=stats[:, 16:32], op=OP.mult), sV)
        VE.wait(sV, sV.n)
        VE.op(lambda e: e.tensor_tensor(out=stats[:, 32:48], in0=stats[:, 32:48], in1=cols2[:, 16:32], op=OP.mult), sV)
        VE.wait(sV, sV.n)
        VE.op(lambda e: e.tensor_reduce(out=cols[:, C_MX:C_MX + 1], in_=stats[:, 32:48], axis=AX, op=OP.max), sV)
        m_mx = sV.n

        stP35.close()
        barrier()
        stB1 = ExitStack()
        xq_fm = sb(stB1, "xq_fm", [128, 8 * T], BF16)
        kgT = sb(stB1, "kgT", [128, 8 * I_FF], BF16)
        kuT = sb(stB1, "kuT", [128, 8 * I_FF], BF16)
        qi_t = [sb(stB1, f"qit{i}", [128, D], I32) for i in range(2)]
        qf_t = [sb(stB1, f"qft{i}", [128, D], F32) for i in range(2)]
        x2q_t = [sb(stB1, f"x2q{i}", [128, D], F32) for i in range(3)]
        sgx = sb(stB1, "sgx", [128, NI], F32)
        sux = sb(stB1, "sux", [128, NI], F32)
        g_tmp = [sb(stB1, f"gtmp{i}", [128, 512], F32) for i in range(3)]
        up_tmp = [sb(stB1, f"uptmp{i}", [128, 512], F32) for i in range(3)]
        vh_t = [sb(stB1, f"vht{i}", [128, 512], F32) for i in range(3)]

        fill(100, bank=0)
        GP.wait(sV, m_mx)
        dma(GP, mx_in, cols[:, C_MX:C_MX + 1], sSCB)
        GP.wait(sSCB, sSCB.n)
        GP.op(lambda e: e.collective_compute("AllReduce", OP.max, replica_groups=RG,
                                             ins=[mx_in], outs=[mx_out]), sCC)
        m_cc_mx = sCC.n
        TE.wait(sCC, m_cc_mx)
        fill(40, bank=0)
        GP.wait(sCC, sCC.n)
        dma(GP, row128[0:1, :], mx_out.rearrange("p o -> o p"), sSCB)
        VE.wait(sSCB, sSCB.n)
        VE.wait(sV, sV.n)
        VE.op(lambda e: e.reduce_max(out=one1[0:1, O_M:O_M + 1], in_=row128[0:1, :], axis=AX), sV)
        SC.wait(sBC, m_bc)
        SC.wait(sS, sS.n)
        m_sig = SC.op(lambda e: e.activation(out=one1[0:1, O_SIG:O_SIG + 1],
                                             in_=one1[0:1, O_CLX:O_CLX + 1], func=ACT.Sigmoid), sS)
        VE.wait(sS, m_sig)
        VE.wait(sV, sV.n)
        VE.op(lambda e: e.tensor_tensor(out=one1[0:1, O_SX:O_SX + 1], in0=one1[0:1, O_SIG:O_SIG + 1],
                                        in1=one1[0:1, O_M:O_M + 1], op=OP.mult), sV)
        VE.wait(sV, sV.n)
        VE.op(lambda e: e.tensor_scalar_mul(one1[0:1, O_SX:O_SX + 1], one1[0:1, O_SX:O_SX + 1],
                                            1.0 / 127.0), sV)
        m_rsx = recip_nr(one1[0:1, O_SX:O_SX + 1], one1[0:1, O_RSX:O_RSX + 1])
        GP.wait(sV, m_rsx)
        dma(GP, sc_hbm[0:1, 0:2], one1[0:1, O_SX:O_RSX + 1], sSCB)
        GP.wait(sSCB, sSCB.n)
        dma(GP, cols[:, C_SX:C_SX + 1], sc_hbm[0:1, 0:1].partition_broadcast(128), sSCB)
        dma(GP, cols[:, C_RSX:C_RSX + 1], sc_hbm[0:1, 1:2].partition_broadcast(128), sSCB)
        m_sxc = sSCB.n
        VE.wait(sSCB, m_sxc)
        VE.wait(sV, sV.n)
        m_rs2 = VE.op(lambda e: e.tensor_scalar_mul(cols2[:, 32:48], cols2[:, 16:32],
                                                    cols[:, C_RSX:C_RSX + 1]), sV)

        m_kx = {}
        m_kxtr = {}
        m_kxtr_s = {}
        for i in range(2):
            dma(SY, x2q_t[i][:], x2_hbm[128 * i:128 * (i + 1), :], sXQ[i])
        for i in range(NT):
            sl = i % 2
            xs = i % 3
            if i + 2 < NT:
                s2 = (i + 2) % 3
                if i >= 1:
                    SY.wait(sV, m_kx[i - 1][0])
                dma(SY, x2q_t[s2][:], x2_hbm[128 * (i + 2):128 * (i + 3), :], sXQ[s2])
            VE.wait(sXQ[xs], sXQ[xs].n)
            VE.wait(sV, max(sV.n, m_rs2))
            if i >= 2:
                VE.wait(sT, m_kxtr[i - 2])
            mq1 = VE.op(lambda e, s2=sl, x3=xs, ii=i:
                        e.tensor_scalar(out=qf_t[s2][:], in0=x2q_t[x3][:],
                                        scalar1=cols2[:, ii:ii + 1], scalar2=cols2[:, 32 + ii:33 + ii],
                                        op0=OP.subtract, op1=OP.mult), sV)
            VE.wait(sV, sV.n)
            mq2 = VE.op(lambda e, s2=sl: e.tensor_scalar(out=qi_t[s2][:], in0=qf_t[s2][:],
                                                         scalar1=-128.0, scalar2=127.0,
                                                         op0=OP.max, op1=OP.min), sV)
            SC.wait(sV, mq2)
            if i >= 2:
                SC.wait(sT, m_kxtr[i - 2])
            SC.wait(sS, sS.n)
            mq3 = SC.op(lambda e, s2=sl: e.copy(out=qf_t[s2][:], in_=qi_t[s2][:]), sS)
            m_kx[i] = (mq1, mq3)
            TE.wait(sS, mq3)
            te_transpose(xq_fm[:, 128 * i:], qf_t[sl][:], 8, src_step=128, dst_step=T, banks=(6, 7))
            m_kxtr[i] = sT.n
            m_kxtr_s[i] = sS.n
            fill(6, bank=0)
        m_xq_tr = m_kxtr_s[NT - 1]
        dbg_tap(10, xq_fm[:, 0:T], T, sS, m_xq_tr)

        m_kwq = {}
        m_kwtr = {}
        m_kwtr_s = {}
        m_gev = {}
        m_upev = {}
        m_vhmul = {}
        m_vhst = {}
        vhe_box = [0]

        m_kwq1 = {}

        def issue_kw_load(c):
            if c >= NI:
                return
            for wsel, dsrc in enumerate((d_wg, d_wu)):
                idx = 2 * c + wsel
                sl = idx % 4
                if idx >= 4:
                    SY.wait(sV, m_kwq1[idx - 4])
                dma(SY, w_st[sl][:], dsrc[128 * c:128 * (c + 1), :], sWS[sl])

        m_scl = {}

        def emit_kw_prep(c):
            for wsel, (dsrc, kT) in enumerate(((d_wg, kgT), (d_wu, kuT))):
                idx = 2 * c + wsel
                cx = wsel * NI + c
                sl = idx % 2
                ws4 = idx % 4
                scol = sgx if wsel == 0 else sux
                VE.wait(sWS[ws4], sWS[ws4].n)
                if idx == 0:
                    VE.wait(sSCB, m_sxc)
                VE.wait(sV, sV.n)
                VE.op(lambda e, w4=ws4, ix=cx:
                      e.reduce_max(out=wmx[:, ix:ix + 1], in_=w_st[w4][:],
                                   axis=AX, apply_absolute_value=True), sV)
                VE.wait(sV, sV.n)
                VE.op(lambda e, ix=cx: e.reciprocal(out=wmx[:, 44:45], in_=wmx[:, ix:ix + 1]), sV)
                VE.wait(sV, sV.n)
                VE.op(lambda e, ix=cx: e.tensor_tensor(out=wmx[:, 45:46], in0=wmx[:, ix:ix + 1],
                                                       in1=wmx[:, 44:45], op=OP.mult), sV)
                VE.wait(sV, sV.n)
                VE.op(lambda e: e.tensor_scalar(out=wmx[:, 45:46], in0=wmx[:, 45:46],
                                                scalar1=2.0, scalar2=-1.0,
                                                op0=OP.subtract, op1=OP.mult), sV)
                VE.wait(sV, sV.n)
                VE.op(lambda e: e.tensor_tensor(out=wmx[:, 44:45], in0=wmx[:, 44:45],
                                                in1=wmx[:, 45:46], op=OP.mult), sV)
                VE.wait(sV, sV.n)
                VE.op(lambda e, ix=cx: e.tensor_scalar_mul(wmx[:, 48 + ix:49 + ix],
                                                           wmx[:, 44:45], 127.0), sV)
                VE.wait(sV, sV.n)
                m_scl[idx] = VE.op(lambda e, sc2=scol, cc=c, ix=cx:
                                   e.tensor_scalar(out=sc2[:, cc:cc + 1], in0=wmx[:, ix:ix + 1],
                                                   scalar1=cols[:, C_SX:C_SX + 1],
                                                   scalar2=1.0 / 127.0,
                                                   op0=OP.mult, op1=OP.mult), sV)
                VE.wait(sT, m_kwtr[idx - 2] if idx >= 2 else m_kxtr[NT - 2 + idx])
                VE.wait(sV, sV.n)
                mm1 = VE.op(lambda e, s2=sl, w4=ws4, ix=cx:
                            e.tensor_scalar(out=qf_t[s2][:], in0=w_st[w4][:],
                                            scalar1=wmx[:, 48 + ix:49 + ix], scalar2=-128.0,
                                            op0=OP.mult, op1=OP.max), sV)
                m_kwq1[idx] = mm1
                VE.wait(sV, sV.n)
                mm2 = VE.op(lambda e, s2=sl: e.tensor_scalar(out=qi_t[s2][:], in0=qf_t[s2][:],
                                                             scalar1=127.0, scalar2=None,
                                                             op0=OP.min), sV)
                SC.wait(sV, mm2)
                SC.wait(sS, sS.n)
                m_kwq[idx] = SC.op(lambda e, s2=sl: e.copy(out=qf_t[s2][:], in_=qi_t[s2][:]), sS)
                TE.wait(sS, m_kwq[idx])
                te_transpose(kT[:, 128 * c:], qf_t[sl][:], 8, src_step=128, dst_step=I_FF, banks=(6, 7))
                m_kwtr[idx] = sT.n
                m_kwtr_s[idx] = sS.n

        def emit_gateup_mm(c):
            TE.wait(sS, m_kwtr_s[2 * c + 1])
            for n in range(4):
                vhe = vhe_box[0]
                if c == 0:
                    TE.wait(sS, m_kxtr_s[4 * n + 3])
                bg = vhe % 3
                bu = 3 + vhe % 3
                te_bank(bg)
                for d_ in range(8):
                    TE.op(lambda e, dd=d_, cc=c, nn=n, bb=bg:
                          e.matmul(ps[bb],
                                   kgT[0:128, I_FF * dd + 128 * cc:I_FF * dd + 128 * (cc + 1)],
                                   xq_fm[0:128, dd * T + 512 * nn:dd * T + 512 * (nn + 1)],
                                   start=(dd == 0), stop=(dd == 7)),
                          sT if d_ == 7 else None)
                m_gmm = sT.n
                te_bank(bu)
                for d_ in range(8):
                    TE.op(lambda e, dd=d_, cc=c, nn=n, bb=bu:
                          e.matmul(ps[bb],
                                   kuT[0:128, I_FF * dd + 128 * cc:I_FF * dd + 128 * (cc + 1)],
                                   xq_fm[0:128, dd * T + 512 * nn:dd * T + 512 * (nn + 1)],
                                   start=(dd == 0), stop=(dd == 7)),
                          sT if d_ == 7 else None)
                m_umm = sT.n
                SC.wait(sT, m_gmm)
                if n == 0:
                    SC.wait(sV, m_scl[2 * c])
                if vhe >= 3:
                    SC.wait(sV, m_vhmul[vhe - 3])
                SC.wait(sS, sS.n)
                m_gev[vhe] = SC.op(lambda e, bb=bg, s2=vhe % 3, cc=c:
                                   e.activation(out=g_tmp[s2][:], in_=ps[bb], func=ACT.Silu,
                                                scale=sgx[:, cc:cc + 1]), sS)
                bankfree[bg] = (sS, m_gev[vhe])
                VE.wait(sT, m_umm)
                if n == 0:
                    VE.wait(sV, m_scl[2 * c + 1])
                VE.wait(sV, sV.n)
                m_upev[vhe] = VE.op(lambda e, bb=bu, s2=vhe % 3, cc=c:
                                    e.tensor_scalar_mul(up_tmp[s2][:], ps[bb], sux[:, cc:cc + 1]), sV)
                bankfree[bu] = (sV, m_upev[vhe])
                VE.wait(sS, m_gev[vhe])
                if vhe >= 3:
                    VE.wait(sVH[vhe % 3], sVH[vhe % 3].n)
                VE.wait(sV, sV.n)
                m_vhmul[vhe] = VE.op(lambda e, s2=vhe % 3:
                                     e.tensor_mul(out=vh_t[s2][:], in0=g_tmp[s2][:], in1=up_tmp[s2][:]), sV)
                VE.wait(sV, sV.n)
                VE.op(lambda e, s2=vhe % 3: e.reduce_max(out=stats[:, 20:21], in_=vh_t[s2][:], axis=AX,
                                                         apply_absolute_value=True), sV)
                VE.wait(sV, sV.n)
                if vhe == 0:
                    m_mh_ = VE.op(lambda e: e.tensor_copy(out=cols[:, C_MH:C_MH + 1],
                                                          in_=stats[:, 20:21]), sV)
                else:
                    m_mh_ = VE.op(lambda e: e.tensor_tensor(out=cols[:, C_MH:C_MH + 1],
                                                            in0=cols[:, C_MH:C_MH + 1],
                                                            in1=stats[:, 20:21], op=OP.max), sV)
                GP.wait(sV, m_mh_)
                m_vhst[vhe] = dma(GP, vh_hbm[128 * c:128 * (c + 1), 512 * n:512 * (n + 1)],
                                  vh_t[vhe % 3][:], sVH[vhe % 3])
                vhe_box[0] = vhe + 1
            return m_mh_

        issue_kw_load(0)
        issue_kw_load(1)
        for c in range(NI):
            emit_kw_prep(c)
            issue_kw_load(c + 2)
            if c >= 1:
                m_mh = emit_gateup_mm(c - 1)
        m_mh = emit_gateup_mm(NI - 1)

        GP.wait(sV, m_mh)
        dma(GP, mh_in, cols[:, C_MH:C_MH + 1], sSCB)
        GP.wait(sSCB, sSCB.n)
        GP.op(lambda e: e.collective_compute("AllReduce", OP.max, replica_groups=RG,
                                             ins=[mh_in], outs=[mh_out]), sCC)
        m_ccmh = sCC.n

        stB1.close()
        stW.close()
        barrier(exclude=(sCC,))
        stB2 = ExitStack()
        kdT = sb(stB2, "kdT", [128, NI * D], BF16)
        h_fm = sb(stB2, "h_fm", [128, NI * T], BF16)
        wd_st = sb(stB2, "wd_st", [128, I_FF], F32)
        wmx2 = sb(stB2, "wmx2", [128, 32], F32)
        qi2 = [sb(stB2, f"qi2{i}", [128, 1024], I32) for i in range(2)]
        qf2 = [sb(stB2, f"qf2{i}", [128, 1024], F32) for i in range(2)]
        vh_ld = [sb(stB2, f"vhld{i}", [128, 1024], F32) for i in range(3)]
        sdh_bc = sb(stB2, "sdh_bc", [128, D], F32)
        x2r_t = [sb(stB2, f"x2rt{i}", [128, D], F32) for i in range(3)]
        a2_t = [sb(stB2, f"a2t{i}", [128, D], F32) for i in range(2)]
        ep2 = [sb(stB2, f"ep2{i}", [128, 512], F32) for i in range(2)]

        fill(50, bank=4)
        m_kdq = {}
        m_kdtr = {}
        m_sdc = {}
        HW2 = 704
        for c in range(8):
            if c >= 1:
                SY.wait(sS, m_kdq[c - 1])
                SY.wait(sT, m_kdtr[c - 1])
            dma(SY, wd_st[:], d_wd[128 * c:128 * (c + 1), :], sWD)
            VE.wait(sWD, sWD.n)
            VE.wait(sV, sV.n)
            VE.op(lambda e, cc=c: e.reduce_max(out=wmx2[:, cc:cc + 1], in_=wd_st[:],
                                               axis=AX, apply_absolute_value=True), sV)
            VE.wait(sV, sV.n)
            VE.op(lambda e, cc=c: e.reciprocal(out=wmx2[:, 16:17], in_=wmx2[:, cc:cc + 1]), sV)
            VE.wait(sV, sV.n)
            VE.op(lambda e, cc=c: e.tensor_tensor(out=wmx2[:, 17:18], in0=wmx2[:, cc:cc + 1],
                                                  in1=wmx2[:, 16:17], op=OP.mult), sV)
            VE.wait(sV, sV.n)
            VE.op(lambda e: e.tensor_scalar(out=wmx2[:, 17:18], in0=wmx2[:, 17:18],
                                            scalar1=2.0, scalar2=-1.0,
                                            op0=OP.subtract, op1=OP.mult), sV)
            VE.wait(sV, sV.n)
            VE.op(lambda e: e.tensor_tensor(out=wmx2[:, 16:17], in0=wmx2[:, 16:17],
                                            in1=wmx2[:, 17:18], op=OP.mult), sV)
            VE.wait(sV, sV.n)
            VE.op(lambda e, cc=c: e.tensor_scalar_mul(wmx2[:, 24 + cc:25 + cc],
                                                      wmx2[:, 16:17], 127.0), sV)
            VE.wait(sV, sV.n)
            m_sdc[c] = VE.op(lambda e, cc=c: e.tensor_scalar_mul(wmx2[:, cc:cc + 1],
                                                                 wmx2[:, cc:cc + 1], 1.0 / 127.0), sV)
            SY.wait(sV, m_sdc[c])
            dma(SY, sd_hbm[0:1, 128 * c:128 * (c + 1)], wmx2[:, c:c + 1], sSD)
            for hf in range(4):
                sl = hf % 2
                pre = [] if (c < 1 or hf > 0) else [(sTRKD, sTRKD.n)]
                for w_ in pre:
                    SC.wait(*w_)
                VE.wait(sV, sV.n)
                VE.op(lambda e, cc=c, h2=hf, s2=sl:
                      e.tensor_scalar(out=qf2[s2][0:128, 0:HW2], in0=wd_st[0:128, HW2 * h2:HW2 * (h2 + 1)],
                                      scalar1=wmx2[:, 24 + cc:25 + cc], scalar2=-128.0,
                                      op0=OP.mult, op1=OP.max), sV)
                VE.wait(sV, sV.n)
                md2 = VE.op(lambda e, s2=sl: e.tensor_scalar(out=qi2[s2][0:128, 0:HW2],
                                                             in0=qf2[s2][0:128, 0:HW2],
                                                             scalar1=127.0, scalar2=None,
                                                             op0=OP.min), sV)
                SC.wait(sV, md2)
                if c >= 1 and hf == 0:
                    SC.wait(sT, m_kdtr[c - 1])
                SC.wait(sS, sS.n)
                m_kdq[c] = SC.op(lambda e, s2=sl, h2=hf: e.copy(out=wd_st[0:128, HW2 * h2:HW2 * (h2 + 1)],
                                                                in_=qi2[s2][0:128, 0:HW2]), sS)
            TE.wait(sS, m_kdq[c])
            te_transpose(kdT[:, 128 * c:], wd_st[:], NI, src_step=128, dst_step=D, banks=(4, 5, 6, 7))
            m_kdtr[c] = sT.n
            m_kdtr_ev = sS.n
            fill(20, bank=0)
        m_kd_tr = m_kdtr_ev

        GP.wait(sCC, m_ccmh)
        dma(GP, row128[0:1, :], mh_out.rearrange("p o -> o p"), sSCB)
        VE.wait(sSCB, sSCB.n)
        VE.wait(sV, sV.n)
        VE.op(lambda e: e.reduce_max(out=one1[0:1, O_M:O_M + 1], in_=row128[0:1, :], axis=AX), sV)
        SC.wait(sS, sS.n)
        m_sigd = SC.op(lambda e: e.activation(out=one1[0:1, O_SIG:O_SIG + 1],
                                              in_=one1[0:1, O_CLD:O_CLD + 1], func=ACT.Sigmoid), sS)
        VE.wait(sS, m_sigd)
        VE.wait(sV, sV.n)
        VE.op(lambda e: e.tensor_tensor(out=one1[0:1, O_SH:O_SH + 1], in0=one1[0:1, O_SIG:O_SIG + 1],
                                        in1=one1[0:1, O_M:O_M + 1], op=OP.mult), sV)
        VE.wait(sV, sV.n)
        VE.op(lambda e: e.tensor_scalar_mul(one1[0:1, O_SH:O_SH + 1], one1[0:1, O_SH:O_SH + 1],
                                            1.0 / 127.0), sV)
        m_rsh = recip_nr(one1[0:1, O_SH:O_SH + 1], one1[0:1, O_RSH:O_RSH + 1])
        GP.wait(sV, m_rsh)
        dma(GP, sc_hbm[0:1, 2:4], one1[0:1, O_SH:O_RSH + 1], sSCB)
        GP.wait(sSCB, sSCB.n)
        dma(GP, cols[:, C_SH:C_SH + 1], sc_hbm[0:1, 2:3].partition_broadcast(128), sSCB)
        dma(GP, cols[:, C_RSH:C_RSH + 1], sc_hbm[0:1, 3:4].partition_broadcast(128), sSCB)
        m_shc = sSCB.n

        SY.wait(sSD, sSD.n)
        dma(SY, sdh_bc[:], sd_hbm.partition_broadcast(128), sBC)
        VE.wait(sBC, sBC.n)
        VE.wait(sSCB, m_shc)
        VE.wait(sV, sV.n)
        m_sdh = VE.op(lambda e: e.tensor_scalar_mul(sdh_bc[:], sdh_bc[:], cols[:, C_SH:C_SH + 1]), sV)

        m_hq = {}
        for j0 in range(2):
            rq = SY if j0 % 2 == 0 else GP
            dma(rq, vh_ld[j0][:], vh_hbm[128 * (j0 // 2):128 * (j0 // 2) + 128,
                                         1024 * (j0 % 2):1024 * (j0 % 2) + 1024], sVHL[j0])
        for j in range(2 * NI):
            c, hf = divmod(j, 2)
            sl = j % 3
            if j + 2 < 2 * NI:
                s2 = (j + 2) % 3
                c3, h3 = divmod(j + 2, 2)
                rq = SY if (j + 2) % 2 == 0 else GP
                if j >= 1:
                    rq.wait(sS, m_hq[j - 1])
                dma(rq, vh_ld[s2][:], vh_hbm[128 * c3:128 * (c3 + 1), 1024 * h3:1024 * (h3 + 1)],
                    sVHL[s2])
            VE.wait(sVHL[sl], sVHL[sl].n)
            if j < 2:
                VE.wait(sSCB, m_shc)
            pre_ve = [(sS, m_hq[j - 2])] if j >= 2 else []
            m_hq[j] = quant(vh_ld[sl][:], cols[:, C_RSH:C_RSH + 1],
                            qi2[j % 2][0:128, 0:1024], qf2[j % 2][0:128, 0:1024],
                            h_fm[0:128, T * c + 1024 * hf:T * c + 1024 * (hf + 1)],
                            pre_ve_waits=pre_ve)
            TE.wait(sS, m_hq[j])
            fill(9, bank=6)
        m_hq_all = m_hq[2 * NI - 1]
        dbg_tap(13, h_fm[:, 0:T], T, sS, m_hq_all)
        dbg_tap(14, kdT[:, 0:2048], 2048, sS, m_kd_tr)
        dbg_tap(15, sdh_bc[:], D, sV, m_sdh)

        m_oev = {}
        m_ost = {}
        TE.wait(sS, m_hq_all)
        TE.wait(sS, m_kd_tr)
        for t in range(2):
            dma(SY, x2r_t[t][:], x2_hbm[128 * t:128 * (t + 1), :], sX2R[t])
        for t in range(NT):
            sl = t % 3
            gidx = t
            if t + 2 < NT:
                s2 = (t + 2) % 3
                if t >= 1:
                    SY.wait(sV, m_oev[t - 1])
                    SY.wait(sOUTs[s2], sOUTs[s2].n)
                dma(SY, x2r_t[s2][:], x2_hbm[128 * (t + 2):128 * (t + 3), :], sX2R[s2])
            VE.wait(sX2R[sl], sX2R[sl].n)
            VE.wait(sV, sV.n)
            m_x2n = VE.op(lambda e, s2=sl, gi=gidx:
                          e.tensor_scalar(out=x2r_t[s2][:], in0=x2r_t[s2][:],
                                          scalar1=cols2[:, gi:gi + 1], scalar2=cols2[:, 16 + gi:17 + gi],
                                          op0=OP.subtract, op1=OP.mult), sV)
            for hn in range(2):
                b = 2 * sl + hn
                te_bank(b)
                for cc in range(NI):
                    TE.op(lambda e, c2=cc, tt=t, hh=hn, bb=b:
                          e.matmul(ps[bb],
                                   h_fm[0:128, T * c2 + 128 * tt:T * c2 + 128 * (tt + 1)],
                                   kdT[0:128, D * c2 + 512 * hh:D * c2 + 512 * (hh + 1)],
                                   start=(c2 == 0), stop=(c2 == NI - 1)),
                          sT if cc == NI - 1 else None)
                m_dmm = sT.n
                VE.wait(sT, m_dmm)
                if t == 0 and hn == 0:
                    VE.wait(sV, m_sdh)
                VE.wait(sV, sV.n)
                VE.op(lambda e, bb=b, hh=hn:
                      e.tensor_mul(out=ep2[hh][:], in0=ps[bb],
                                   in1=sdh_bc[:, 512 * hh:512 * (hh + 1)]), sV)
                VE.wait(sV, max(sV.n, m_x2n))
                mep = VE.op(lambda e, s2=sl, a2=t % 2, hh=hn:
                            e.tensor_add(out=a2_t[a2][:, 512 * hh:512 * (hh + 1)],
                                         in0=x2r_t[s2][:, 512 * hh:512 * (hh + 1)], in1=ep2[hh][:]), sV)
                bankfree[b] = (sV, mep)
            ln_stats(a2_t[t % 2][:], stats[:, 14:15], stats[:, 15:16])
            VE.wait(sV, sV.n)
            m_oev[t] = VE.op(lambda e, s2=sl, a2=t % 2:
                             e.tensor_scalar(out=x2r_t[s2][:], in0=a2_t[a2][:],
                                             scalar1=stats[:, 14:15],
                                             scalar2=stats[:, 15:16],
                                             op0=OP.subtract, op1=OP.mult), sV)
            GP.wait(sV, m_oev[t])
            m_ost[t] = dma(GP, d_out[128 * t:128 * (t + 1), :], x2r_t[sl][:], sOUTs[sl])

        GP.wait(sOUTs[0], sOUTs[0].n)
        GP.wait(sOUTs[1], sOUTs[1].n)
        GP.wait(sOUTs[2], sOUTs[2].n)
        stB2.close()

        blk = st0.enter_context(nc.Block())

        def realize(rec):
            def runner(eng):
                for it in rec.items:
                    if it[0] == "w":
                        eng.wait_ge(it[1].h, it[2])
                    else:
                        instr = it[1](eng)
                        if it[2] is not None:
                            instr.then_inc(it[2].h, it[3])
            return runner

        blk.sync(realize(SY))
        blk.vector(realize(VE))
        blk.scalar(realize(SC))
        blk.tensor(realize(TE))
        blk.gpsimd(realize(GP))

    return nc


_NC_CACHE = [None]


def kernel(L0_attn_x, L0_attn_ln_g, L0_attn_ln_b, L0_attn_W_in, L0_attn_b_in,
           L0_attn_W_out, L0_attn_b_out, L0_attn_gamma_q, L0_attn_beta_q,
           L0_attn_gamma_k, L0_attn_beta_k, L0_ffn_clip_x, L0_ffn_clip_down,
           L0_ffn_w_gate, L0_ffn_w_up, L0_ffn_w_down):
    f32 = np.float32
    xr = np.ascontiguousarray(np.asarray(L0_attn_x, dtype=f32).reshape(L, D))
    shared = {
        "ln_g": np.asarray(L0_attn_ln_g, f32).reshape(1, D),
        "ln_b": np.asarray(L0_attn_ln_b, f32).reshape(1, D),
        "W_in": np.ascontiguousarray(np.asarray(L0_attn_W_in, f32)),
        "b_in": np.asarray(L0_attn_b_in, f32).reshape(1, F_IN),
        "W_out": np.ascontiguousarray(np.asarray(L0_attn_W_out, f32)),
        "b_out": np.asarray(L0_attn_b_out, f32).reshape(1, D),
        "gq": np.ascontiguousarray(np.asarray(L0_attn_gamma_q, f32).reshape(S, 1)),
        "bq": np.ascontiguousarray(np.asarray(L0_attn_beta_q, f32).reshape(S, 1)),
        "gk": np.ascontiguousarray(np.asarray(L0_attn_gamma_k, f32).reshape(S, 1)),
        "bk": np.ascontiguousarray(np.asarray(L0_attn_beta_k, f32).reshape(S, 1)),
        "clip_x": np.asarray(L0_ffn_clip_x, f32).reshape(1, 1),
        "clip_d": np.asarray(L0_ffn_clip_down, f32).reshape(1, 1),
        "w_gate": np.ascontiguousarray(np.asarray(L0_ffn_w_gate, f32)),
        "w_up": np.ascontiguousarray(np.asarray(L0_ffn_w_up, f32)),
        "w_down": np.ascontiguousarray(np.asarray(L0_ffn_w_down, f32)),
        "ident": np.eye(128, dtype=f32),
    }
    if _NC_CACHE[0] is None:
        _NC_CACHE[0] = build_program()
    nc = _NC_CACHE[0]
    in_maps = []
    for c in range(NCORES):
        m = dict(shared)
        m["x"] = np.ascontiguousarray(xr[T * c:T * (c + 1)])
        in_maps.append(m)
    res = run_bass_kernel_spmd(nc, in_maps, core_ids=list(range(NCORES)))
    out = np.concatenate([res.results[c]["out"] for c in range(NCORES)], axis=0)
    return out.reshape(1, L, D).astype(np.float32)


# version 14 (final)
# speedup vs baseline: 1.0348x; 1.0236x over previous
import numpy as np
from contextlib import ExitStack
import concourse.bass as bass
import concourse.mybir as mybir
from concourse.bass_utils import run_bass_kernel_spmd

F32 = mybir.dt.float32
BF16 = mybir.dt.bfloat16
I32 = mybir.dt.int32
AX = mybir.AxisListType.X
OP = mybir.AluOpType
ACT = mybir.ActivationFunctionType

NCORES = 8
L = 16384
T = L // NCORES
D = 1024
E = 2048
S = 128
F_IN = 2 * E + S
I_FF = 2816
NI = I_FF // 128
LN_EPS = 1e-5
NT = T // 128
KS = 512
NSUP = L // KS
NPASS = 4
QP = T // NPASS
INV_L = 1.0 / L


class Rec:
    def __init__(self, name):
        self.name = name
        self.items = []

    def wait(self, sem, val=None):
        v = sem.n if val is None else val
        assert v <= sem.n, f"{self.name}: wait {sem.name}>={v} > recorded {sem.n}"
        if v > 0:
            self.items.append(("w", sem, v))

    def op(self, fn, sem=None, amt=1):
        self.items.append(("o", fn, sem, amt if sem else 0))
        if sem is not None:
            sem.n += amt
            return sem.n
        return None


class Sem:
    def __init__(self, stack, nc, name):
        self.h = stack.enter_context(nc.semaphore(name))
        self.name = name
        self.n = 0


def build_program(debug=False):
    nc = bass.Bass("TRN2", target_bir_lowering=False, debug=False)

    dx = nc.dram_tensor("x", [T, D], F32, kind="ExternalInput").ap()
    d_ln_g = nc.dram_tensor("ln_g", [1, D], F32, kind="ExternalInput").ap()
    d_ln_b = nc.dram_tensor("ln_b", [1, D], F32, kind="ExternalInput").ap()
    d_Win = nc.dram_tensor("W_in", [D, F_IN], F32, kind="ExternalInput").ap()
    d_bin = nc.dram_tensor("b_in", [1, F_IN], F32, kind="ExternalInput").ap()
    d_Wout = nc.dram_tensor("W_out", [E, D], F32, kind="ExternalInput").ap()
    d_bout = nc.dram_tensor("b_out", [1, D], F32, kind="ExternalInput").ap()
    d_gq = nc.dram_tensor("gq", [S, 1], F32, kind="ExternalInput").ap()
    d_bq = nc.dram_tensor("bq", [S, 1], F32, kind="ExternalInput").ap()
    d_gk = nc.dram_tensor("gk", [S, 1], F32, kind="ExternalInput").ap()
    d_bk = nc.dram_tensor("bk", [S, 1], F32, kind="ExternalInput").ap()
    d_clip_x = nc.dram_tensor("clip_x", [1, 1], F32, kind="ExternalInput").ap()
    d_clip_d = nc.dram_tensor("clip_d", [1, 1], F32, kind="ExternalInput").ap()
    d_wg = nc.dram_tensor("w_gate", [I_FF, D], F32, kind="ExternalInput").ap()
    d_wu = nc.dram_tensor("w_up", [I_FF, D], F32, kind="ExternalInput").ap()
    d_wd = nc.dram_tensor("w_down", [D, I_FF], F32, kind="ExternalInput").ap()
    d_id = nc.dram_tensor("ident", [128, 128], F32, kind="ExternalInput").ap()
    d_out = nc.dram_tensor("out", [T, D], F32, kind="ExternalOutput").ap()
    d_dbg = nc.dram_tensor("dbg", [16 * 128, 2048], F32, kind="ExternalOutput").ap() if debug else None
    sDBG = [None]

    d_uavT = nc.dram_tensor("uavT", [T, E], BF16).ap()
    zg_in = nc.dram_tensor("zg_in", [S, T], BF16).ap()
    zg_out = nc.dram_tensor("zg_out", [S * NCORES, T], BF16, addr_space="Shared").ap()
    vg_in = nc.dram_tensor("vg_in", [T, E], BF16).ap()
    vg_outA = nc.dram_tensor("vg_outA", [L // 2, E], BF16, addr_space="Shared").ap()
    vg_outB = nc.dram_tensor("vg_outB", [L // 2, E], BF16, addr_space="Shared").ap()
    u_hbm = nc.dram_tensor("u_hbm", [T, E], BF16).ap()
    uav_hbm = nc.dram_tensor("uav_hbm", [T, E], F32).ap()
    x2_hbm = nc.dram_tensor("x2_hbm", [T, D], F32).ap()
    vh_hbm = nc.dram_tensor("vh_hbm", [I_FF, T], F32).ap()
    mx_in = nc.dram_tensor("mx_in", [128, 1], F32).ap()
    mx_out = nc.dram_tensor("mx_out", [128, 1], F32).ap()
    mh_in = nc.dram_tensor("mh_in", [128, 1], F32).ap()
    mh_out = nc.dram_tensor("mh_out", [128, 1], F32).ap()
    sc_hbm = nc.dram_tensor("sc_hbm", [1, 8], F32).ap()
    sd_hbm = nc.dram_tensor("sd_hbm", [1, D], F32).ap()

    RG = [list(range(NCORES))]
    SY, VE, SC, TE, GP = Rec("sync"), Rec("vector"), Rec("scalar"), Rec("tensor"), Rec("gpsimd")
    RECS = [SY, VE, SC, TE, GP]

    with ExitStack() as st0:
        def sem(name):
            return Sem(st0, nc, name)

        def sb(stk, name, shape, dtype):
            return stk.enter_context(nc.sbuf_tensor(name, shape, dtype))

        sT = sem("sT"); sV = sem("sV"); sS = sem("sS"); sCC = sem("sCC")
        sBC = sem("sBC")
        sXT = [sem("sXT0"), sem("sXT1"), sem("sXT2")]
        sWI = sem("sWI")
        sTRXN = sem("sTRXN")
        sUAVT = [sem("sUAVT0"), sem("sUAVT1")]
        sTRXQ = sem("sTRXQ")
        sTRKW = [sem("sTRKW0"), sem("sTRKW1")]
        sTRKD = sem("sTRKD")
        sSTZ = sem("sSTZ")
        sSTV = sem("sSTV")
        sSTU = sem("sSTU")
        sX2ST = [sem("sX2ST0"), sem("sX2ST1")]
        sSD = sem("sSD")
        sVL = [sem("sVL0"), sem("sVL1")]
        sGZ = sem("sGZ")
        sUL = [sem("sUL0"), sem("sUL1")]
        sWO = [sem("sWO0"), sem("sWO1")]
        sUT = [sem("sUT0"), sem("sUT1"), sem("sUT2")]
        sUL3 = [sem("sUL3_0"), sem("sUL3_1"), sem("sUL3_2")]
        sXR = [sem("sXR0"), sem("sXR1"), sem("sXR2")]
        sXQ = [sem("sXQ0"), sem("sXQ1"), sem("sXQ2")]
        sWS = [sem(f"sWS{i}") for i in range(6)]
        sWD = sem("sWD")
        sVH = [sem("sVH0"), sem("sVH1"), sem("sVH2")]
        sVHL = [sem("sVHL0"), sem("sVHL1"), sem("sVHL2"), sem("sVHL3")]
        sX2R = [sem("sX2R0"), sem("sX2R1"), sem("sX2R2")]
        sSCB = sem("sSCB")
        sOUTs = [sem("sOUT0"), sem("sOUT1"), sem("sOUT2")]
        sDBG[0] = sem("sDBG")
        ALL_SEMS = [sT, sV, sS, sCC, sBC, sWI, sTRXN, sTRXQ, sTRKD, sSTZ, sSTV,
                    sSTU, sSD, sGZ, sSCB, sWD] + sXT + sUAVT + sTRKW + sX2ST + \
                   sVL + sUL + sWO + sUT + sUL3 + sXR + sXQ + sWS + sVH + sVHL + sX2R + sOUTs + [sDBG[0]]

        ps_all = st0.enter_context(nc.psum_tensor("ps_all", [128, 4096], F32))
        ps = [ps_all[0:128, 512 * b:512 * (b + 1)] for b in range(8)]
        bankfree = {}

        def te_bank(b):
            if b in bankfree:
                TE.wait(*bankfree[b])

        ident = sb(st0, "ident_sb", [128, 128], F32)
        identb = sb(st0, "identb", [128, 128], BF16)
        cols2 = sb(st0, "cols2", [128, 48], F32)
        tr_state = {"n": 0}
        stats = sb(st0, "stats", [128, 64], F32)
        cols = sb(st0, "cols", [128, 16], F32)
        one1 = sb(st0, "one1", [1, 16], F32)
        row128 = sb(st0, "row128", [1, 128], F32)
        scrS = sb(st0, "scrS", [128, 1024], F32)

        C_GQ, C_BQ, C_GK, C_BK, C_BZ, C_MX, C_MH, C_RSX, C_RSH, C_SX, C_SH = range(11)
        O_CLX, O_CLD, O_M, O_SIG, O_SX, O_RSX, O_T0, O_SH, O_RSH = range(9)

        def dma(rec, out_ap, in_ap, semx, transpose=False):
            rec.op(lambda e, o=out_ap, i=in_ap, tr=transpose:
                   e.dma_start(out=o, in_=i, transpose=tr), semx, 16)
            return semx.n

        def fill(n, bank=0):
            te_bank(bank)
            for _ in range(n):
                TE.op(lambda e, b=bank:
                      e.matmul(ps_all[0:128, 512 * b:512 * b + 128],
                               ident[:], ident[:], start=True, stop=True), None)

        def dbg_tap(k, src_ap, width, wait_sem=None, wait_val=None):
            if d_dbg is None:
                return
            if wait_sem is not None:
                GP.wait(wait_sem, wait_val)
            dma(GP, d_dbg[128 * k:128 * (k + 1), 0:width], src_ap, sDBG[0])

        def te_transpose(dst_bf, src_f32, nblk, src_step=128, dst_step=128, banks=(6, 7),
                         idn=None):
            m = None
            idn = ident if idn is None else idn
            isbf = (idn is not ident)
            for j in range(nblk):
                b = banks[tr_state["n"] % len(banks)]
                tr_state["n"] += 1
                te_bank(b)
                if isbf:
                    pview = ps_all[0:128, 512 * b:512 * b + 64].bitcast(BF16)
                else:
                    pview = ps_all[0:128, 512 * b:512 * b + 128]
                TE.op(lambda e, jj=j, s=src_f32, st_=src_step, id2=idn, pv=pview:
                      e.transpose(pv, s[0:128, st_ * jj:st_ * jj + 128], id2[:]), sT)
                mt = sT.n
                SC.wait(sT, mt)
                SC.wait(sS, sS.n)
                m = SC.op(lambda e, jj=j, d=dst_bf, st_=dst_step, pv=pview:
                          e.copy(out=d[0:128, st_ * jj:st_ * jj + 128], in_=pv), sS)
                bankfree[b] = (sS, m)
            return m

        def barrier(exclude=()):
            snap = [(s, s.n) for s in ALL_SEMS if s.n > 0 and s not in exclude]
            for rec in RECS:
                for s, v in snap:
                    rec.wait(s, v)

        def ln_rstd(src_col, dst_col):
            t0, t1, t2, t3 = (stats[:, c:c + 1] for c in (60, 61, 62, 63))
            VE.wait(sV, sV.n)
            mvar = VE.op(lambda e, s=src_col: e.tensor_copy(out=t0, in_=s), sV)
            SC.wait(sV, mvar)
            SC.wait(sS, sS.n)
            msq = SC.op(lambda e: e.activation(out=t1, in_=t0, func=ACT.Sqrt), sS)
            VE.wait(sS, msq)
            VE.op(lambda e: e.reciprocal(out=t2, in_=t1), sV)
            VE.wait(sV, sV.n)
            VE.op(lambda e: e.tensor_tensor(out=t3, in0=t2, in1=t2, op=OP.mult), sV)
            VE.wait(sV, sV.n)
            VE.op(lambda e, s=src_col: e.tensor_tensor(out=t1, in0=t3, in1=s, op=OP.mult), sV)
            VE.wait(sV, sV.n)
            VE.op(lambda e: e.tensor_scalar(out=t1, in0=t1, scalar1=3.0, scalar2=-0.5,
                                            op0=OP.subtract, op1=OP.mult), sV)
            VE.wait(sV, sV.n)
            return VE.op(lambda e, d=dst_col: e.tensor_tensor(out=d, in0=t2, in1=t1, op=OP.mult), sV)

        def ln_stats(a_ap, mu_col, rstd_col):
            s0, s1, s2, s3 = (stats[:, c:c + 1] for c in (8, 9, 10, 11))
            VE.wait(sV, sV.n)
            VE.op(lambda e, a=a_ap: e.reduce_sum(out=s0, in_=a, axis=AX), sV)
            SC.wait(sV, sV.n)
            SC.wait(sS, sS.n)
            msq = SC.op(lambda e, a=a_ap: e.activation(out=scrS[:], in_=a, func=ACT.Square,
                                                       accum_out=s1), sS)
            VE.wait(sV, sV.n)
            VE.op(lambda e, m=mu_col: e.tensor_scalar_mul(m, s0, 1.0 / D), sV)
            VE.wait(sS, msq)
            VE.wait(sV, sV.n)
            VE.op(lambda e: e.tensor_scalar_mul(s2, s1, 1.0 / D), sV)
            VE.wait(sV, sV.n)
            VE.op(lambda e, m=mu_col: e.tensor_tensor(out=s3, in0=m, in1=m, op=OP.mult), sV)
            VE.wait(sV, sV.n)
            VE.op(lambda e: e.tensor_scalar(out=s2, in0=s2, scalar1=s3, scalar2=LN_EPS,
                                            op0=OP.subtract, op1=OP.add), sV)
            return ln_rstd(s2, rstd_col)

        def recip_nr(src, dst):
            t = one1[0:1, O_T0:O_T0 + 1]
            VE.wait(sV, sV.n)
            VE.op(lambda e, s=src, d=dst: e.reciprocal(out=d, in_=s), sV)
            VE.wait(sV, sV.n)
            VE.op(lambda e, s=src, d=dst: e.tensor_tensor(out=t, in0=s, in1=d, op=OP.mult), sV)
            VE.wait(sV, sV.n)
            VE.op(lambda e: e.tensor_scalar(out=t, in0=t, scalar1=2.0, scalar2=-1.0,
                                            op0=OP.subtract, op1=OP.mult), sV)
            VE.wait(sV, sV.n)
            return VE.op(lambda e, d=dst: e.tensor_tensor(out=d, in0=d, in1=t, op=OP.mult), sV)

        def rw_from_max(mx_col, rw_col):
            r0, tt = stats[:, 17:18], stats[:, 18:19]
            VE.wait(sV, sV.n)
            VE.op(lambda e, m=mx_col: e.reciprocal(out=r0, in_=m), sV)
            VE.wait(sV, sV.n)
            VE.op(lambda e, m=mx_col: e.tensor_tensor(out=tt, in0=m, in1=r0, op=OP.mult), sV)
            VE.wait(sV, sV.n)
            VE.op(lambda e: e.tensor_scalar(out=tt, in0=tt, scalar1=2.0, scalar2=-1.0,
                                            op0=OP.subtract, op1=OP.mult), sV)
            VE.wait(sV, sV.n)
            VE.op(lambda e: e.tensor_tensor(out=r0, in0=r0, in1=tt, op=OP.mult), sV)
            VE.wait(sV, sV.n)
            return VE.op(lambda e, d=rw_col: e.tensor_scalar_mul(d, r0, 127.0), sV)

        def quant(in_f32, rcol, itile, f32tmp, out_bf, pre_sc_waits=(), pre_ve_waits=()):
            for w_ in pre_ve_waits:
                VE.wait(*w_)
            VE.wait(sV, sV.n)
            m1 = VE.op(lambda e, s=in_f32, d=f32tmp, r=rcol:
                       e.tensor_scalar(out=d, in0=s, scalar1=r, scalar2=-128.0,
                                       op0=OP.mult, op1=OP.max), sV)
            VE.wait(sV, sV.n)
            m2 = VE.op(lambda e, s=f32tmp, d=itile:
                       e.tensor_scalar(out=d, in0=s, scalar1=127.0, scalar2=None,
                                       op0=OP.min), sV)
            for w_ in pre_sc_waits:
                SC.wait(*w_)
            SC.wait(sV, m2)
            SC.wait(sS, sS.n)
            return SC.op(lambda e, s=itile, d=out_bf: e.copy(out=d, in_=s), sS)

        stA = ExitStack()
        scr1 = sb(stA, "scr1", [128, 1024], F32)
        q_fm = sb(stA, "q_fm", [128, T], BF16)
        k_fm = sb(stA, "k_fm", [128, L], BF16)

        stP12 = ExitStack()
        xt = [sb(stP12, f"xt{i}", [128, D], F32) for i in range(3)]
        xnb16 = [sb(stP12, f"xnb16_{i}", [128, D], BF16) for i in range(2)]
        lng_bc = sb(stP12, "lng_bc", [128, D], F32)
        lnb_bc = sb(stP12, "lnb_bc", [128, D], F32)
        xn_fm = sb(stP12, "xn_fm", [128, 8 * T], BF16)
        wi_st = sb(stP12, "wi_st", [128, F_IN], F32)
        wi_bf = sb(stP12, "wi_bf", [128, 8 * F_IN], BF16)
        z_bf = sb(stP12, "z_bf", [128, T], BF16)
        uv_sub = [sb(stP12, f"uvsub{i}", [128, 512], BF16) for i in range(2)]
        zch = sb(stP12, "zch", [128, T], BF16)
        ones_r = sb(stP12, "ones_r", [1, T], BF16)
        binr = sb(stP12, "binr", [1, F_IN], BF16)

        dma(SY, lng_bc[:], d_ln_g.partition_broadcast(128), sBC)
        dma(SY, lnb_bc[:], d_ln_b.partition_broadcast(128), sBC)
        dma(GP, binr[0:1, :], d_bin, sBC)
        dma(SY, cols[:, C_GQ:C_GQ + 1], d_gq, sBC)
        dma(SY, cols[:, C_BQ:C_BQ + 1], d_bq, sBC)
        dma(SY, cols[:, C_GK:C_GK + 1], d_gk, sBC)
        dma(SY, cols[:, C_BK:C_BK + 1], d_bk, sBC)
        dma(SY, cols[:, C_BZ:C_BZ + 1], d_bin[0:1, 2 * E:F_IN].rearrange("o p -> p o"), sBC)
        dma(SY, ident[:], d_id, sBC)
        dma(SY, one1[0:1, O_CLX:O_CLX + 1], d_clip_x, sBC)
        dma(SY, one1[0:1, O_CLD:O_CLD + 1], d_clip_d, sBC)
        m_bc = sBC.n
        VE.wait(sV, sV.n)
        m_ones = VE.op(lambda e: e.memset(ones_r[0:1, :], 1.0), sV)
        SC.wait(sBC, m_bc)
        SC.wait(sS, sS.n)
        m_idb = SC.op(lambda e: e.copy(out=identb[:], in_=ident[:]), sS)
        TE.wait(sBC, m_bc)
        TE.wait(sS, m_idb)

        m_sumA = {}
        m_sqA = {}
        m_wicv = {}
        dma(GP, wi_st[:], d_Win[0:128, :], sWI)
        for i in range(2):
            dma(SY, xt[i][:], dx[128 * i:128 * (i + 1), :], sXT[i])
        for i in range(NT):
            sl = i % 3
            if i + 2 < NT:
                s2 = (i + 2) % 3
                if i >= 1:
                    SY.wait(sV, m_sumA[i - 1])
                    SY.wait(sS, m_sqA[i - 1])
                dma(SY, xt[s2][:], dx[128 * (i + 2):128 * (i + 3), :], sXT[s2])
            VE.wait(sXT[sl], sXT[sl].n)
            VE.wait(sV, sV.n)
            m_sumA[i] = VE.op(lambda e, s=xt[sl], ii=i:
                              e.reduce_sum(out=stats[:, ii:ii + 1], in_=s[:], axis=AX), sV)
            SC.wait(sXT[sl], sXT[sl].n)
            SC.wait(sS, sS.n)
            m_sqA[i] = SC.op(lambda e, s=xt[sl], ii=i:
                             e.activation(out=scrS[:], in_=s[:], func=ACT.Square,
                                          accum_out=stats[:, 16 + ii:17 + ii]), sS)
            if i % 2 == 1 and i // 2 < 8:
                d_ = i // 2
                SC.wait(sWI, sWI.n)
                SC.wait(sS, sS.n)
                m_wicv[d_] = SC.op(lambda e, dd=d_:
                                   e.copy(out=wi_bf[0:128, F_IN * dd:F_IN * (dd + 1)], in_=wi_st[:]), sS)
                if d_ + 1 < 8:
                    GP.wait(sS, m_wicv[d_])
                    dma(GP, wi_st[:], d_Win[128 * (d_ + 1):128 * (d_ + 2), :], sWI)
            TE.wait(sS, m_sqA[i])
            fill(14, bank=0)
        VE.wait(sS, m_sqA[NT - 1])
        VE.wait(sV, sV.n)
        VE.op(lambda e: e.tensor_scalar_mul(stats[:, 32:48], stats[:, 0:16], 1.0 / D), sV)
        VE.wait(sV, sV.n)
        VE.op(lambda e: e.tensor_scalar_mul(stats[:, 16:32], stats[:, 16:32], 1.0 / D), sV)
        VE.wait(sV, sV.n)
        VE.op(lambda e: e.tensor_tensor(out=stats[:, 0:16], in0=stats[:, 32:48], in1=stats[:, 32:48], op=OP.mult), sV)
        VE.wait(sV, sV.n)
        VE.op(lambda e: e.tensor_tensor(out=stats[:, 16:32], in0=stats[:, 16:32], in1=stats[:, 0:16], op=OP.subtract), sV)
        VE.wait(sV, sV.n)
        mvB = VE.op(lambda e: e.tensor_scalar_add(stats[:, 16:32], stats[:, 16:32], LN_EPS), sV)
        SC.wait(sV, mvB)
        SC.wait(sS, sS.n)
        msB = SC.op(lambda e: e.activation(out=stats[:, 0:16], in_=stats[:, 16:32], func=ACT.Sqrt), sS)
        VE.wait(sS, msB)
        VE.op(lambda e: e.reciprocal(out=stats[:, 48:64], in_=stats[:, 0:16]), sV)
        VE.wait(sV, sV.n)
        VE.op(lambda e: e.tensor_tensor(out=stats[:, 0:16], in0=stats[:, 48:64], in1=stats[:, 48:64], op=OP.mult), sV)
        VE.wait(sV, sV.n)
        VE.op(lambda e: e.tensor_tensor(out=stats[:, 0:16], in0=stats[:, 0:16], in1=stats[:, 16:32], op=OP.mult), sV)
        VE.wait(sV, sV.n)
        VE.op(lambda e: e.tensor_scalar(out=stats[:, 0:16], in0=stats[:, 0:16], scalar1=3.0, scalar2=-0.5,
                                        op0=OP.subtract, op1=OP.mult), sV)
        VE.wait(sV, sV.n)
        VE.op(lambda e: e.tensor_tensor(out=stats[:, 16:32], in0=stats[:, 48:64], in1=stats[:, 0:16], op=OP.mult), sV)
        m_rstdB = sV.n
        m_xn = {}
        m_scr1 = {}
        m_xntr = {}
        m_xntr_te = {}
        for i in range(2):
            SY.wait(sV, m_sumA[NT - 1])
            SY.wait(sS, m_sqA[NT - 1])
            dma(SY, xt[i][:], dx[128 * i:128 * (i + 1), :], sXT[i])
        for i in range(NT):
            sl = i % 3
            if i + 2 < NT:
                s2 = (i + 2) % 3
                if i >= 1:
                    SY.wait(sV, m_scr1[i - 1])
                SY.wait(sS, m_sqA[NT - 1])
                dma(SY, xt[s2][:], dx[128 * (i + 2):128 * (i + 3), :], sXT[s2])
            VE.wait(sXT[sl], sXT[sl].n)
            VE.wait(sBC, m_bc)
            VE.wait(sV, max(sV.n, m_rstdB))
            m_scr1[i] = VE.op(lambda e, s=xt[sl], ii=i:
                              e.scalar_tensor_tensor(out=scr1[:], in0=s[:], scalar=stats[:, 32 + ii:33 + ii],
                                                     in1=lng_bc[:], op0=OP.subtract, op1=OP.mult), sV)
            VE.wait(sV, sV.n)
            if i >= 2:
                VE.wait(sT, m_xntr_te[i - 2])
            xnbuf = xnb16[i % 2]
            m_xn[i] = VE.op(lambda e, d=xnbuf, ii=i:
                            e.scalar_tensor_tensor(out=d[:], in0=scr1[:], scalar=stats[:, 16 + ii:17 + ii],
                                                   in1=lnb_bc[:], op0=OP.mult, op1=OP.add), sV)
            TE.wait(sV, m_xn[i])
            m_xntr[i] = te_transpose(xn_fm[:, 128 * i:], xnbuf[:], 8,
                                     src_step=128, dst_step=T, banks=(0, 1, 2, 3, 4, 5, 6, 7),
                                     idn=identb)
            m_xntr_te[i] = sT.n
            fill(6, bank=(i % 2) * 4)
        m_xn_tr = m_xntr[NT - 1]
        dbg_tap(0, xnb16[1][:], D, sV, m_xn[NT - 1])

        m_zev = {}
        for n in range(4):
            TE.wait(sS, m_xntr[4 * n + 3])
            TE.wait(sS, m_wicv[7])
            te_bank(n)
            for d_ in range(8):
                TE.op(lambda e, dd=d_, nn=n:
                      e.matmul(ps[nn],
                               wi_bf[0:128, F_IN * dd + 2 * E:F_IN * dd + 2 * E + 128],
                               xn_fm[0:128, dd * T + 512 * nn:dd * T + 512 * (nn + 1)],
                               start=(dd == 0), stop=(dd == 7)),
                      sT if d_ == 7 else None)
            m_zmm = sT.n
            SC.wait(sT, m_zmm)
            SC.wait(sBC, m_bc)
            SC.wait(sS, sS.n)
            m_zev[n] = SC.op(lambda e, nn=n:
                             e.activation(out=z_bf[:, 512 * nn:512 * (nn + 1)], in_=ps[nn],
                                          func=ACT.Silu, bias=cols[:, C_BZ:C_BZ + 1]), sS)
            bankfree[n] = (sS, m_zev[n])
        VE.wait(sS, m_zev[3])
        VE.wait(sBC, m_bc)
        VE.wait(sV, sV.n)
        m_q = VE.op(lambda e: e.tensor_scalar(out=q_fm[:], in0=z_bf[:], scalar1=cols[:, C_GQ:C_GQ + 1],
                                              scalar2=cols[:, C_BQ:C_BQ + 1], op0=OP.mult, op1=OP.add), sV)
        SY.wait(sS, m_zev[3])
        dma(SY, zg_in, z_bf[:], sSTZ)
        dbg_tap(1, z_bf[:], T, sS, m_zev[3])
        dbg_tap(2, q_fm[:], T, sV, m_q)

        GP.wait(sSTZ, sSTZ.n)
        GP.op(lambda e: e.collective_compute("AllGather", OP.bypass, replica_groups=RG,
                                             ins=[zg_in], outs=[zg_out]), sCC)
        GP.wait(sCC, sCC.n)

        ev = 0
        m_uvadd = {}
        m_uvev = {}
        m_substore = {}
        for w in range(2):
            col0 = E if w == 0 else 0
            for t in range(NT):
                if w == 0 and t == NT // 2:
                    GP.wait(sSTV, sSTV.n)
                    GP.op(lambda e: e.collective_compute("AllGather", OP.bypass, replica_groups=RG,
                                                         ins=[vg_in[0:T // 2, :]], outs=[vg_outA]), sCC)
                    m_vgA = sCC.n
                for n in range(4):
                    b = ev % 8
                    te_bank(b)
                    if ev == 0:
                        TE.wait(sV, m_ones)
                        TE.wait(sBC, m_bc)
                    for d_ in range(9):
                        if d_ < 8:
                            TE.op(lambda e, dd=d_, tt=t, c0=col0, nn=n, bb=b:
                                  e.matmul(ps[bb],
                                           xn_fm[0:128, dd * T + 128 * tt:dd * T + 128 * (tt + 1)],
                                           wi_bf[0:128, F_IN * dd + c0 + 512 * nn:F_IN * dd + c0 + 512 * (nn + 1)],
                                           start=(dd == 0), stop=False),
                                  None)
                        else:
                            TE.op(lambda e, tt=t, c0=col0, nn=n, bb=b:
                                  e.matmul(ps[bb],
                                           ones_r[0:1, 128 * tt:128 * (tt + 1)],
                                           binr[0:1, c0 + 512 * nn:c0 + 512 * (nn + 1)],
                                           start=False, stop=True),
                                  sT)
                    m_mm = sT.n
                    SC.wait(sT, m_mm)
                    if ev >= 2:
                        SC.wait(*m_substore[ev - 2])
                    SC.wait(sS, sS.n)
                    m_uvev[ev] = SC.op(
                        lambda e, esl=ev % 2, bb=b:
                        e.activation(out=uv_sub[esl][:], in_=ps[bb], func=ACT.Silu), sS)
                    bankfree[b] = (sS, m_uvev[ev])
                    SY.wait(sS, m_uvev[ev])
                    dst, stx = (vg_in, sSTV) if w == 0 else (u_hbm, sSTU)
                    dma(SY, dst[128 * t:128 * (t + 1), 512 * n:512 * (n + 1)], uv_sub[ev % 2][:], stx)
                    m_substore[ev] = (stx, stx.n)
                    ev += 1
            if w == 0:
                GP.wait(sSTV, sSTV.n)
                GP.op(lambda e: e.collective_compute("AllGather", OP.bypass, replica_groups=RG,
                                                     ins=[vg_in[T // 2:T, :]], outs=[vg_outB]), sCC)
                m_vg = sCC.n

        m_kch = {}
        for r in range(NCORES):
            if r >= 1:
                GP.wait(sV, m_kch[r - 1])
            dma(GP, zch[:], zg_out[S * r:S * (r + 1), :], sGZ)
            VE.wait(sGZ, sGZ.n)
            VE.wait(sBC, m_bc)
            VE.wait(sV, sV.n)
            m_kch[r] = VE.op(lambda e, rr=r:
                             e.tensor_scalar(out=k_fm[:, T * rr:T * (rr + 1)], in0=zch[:],
                                             scalar1=cols[:, C_GK:C_GK + 1],
                                             scalar2=cols[:, C_BK:C_BK + 1],
                                             op0=OP.mult, op1=OP.add), sV)
            TE.wait(sV, m_kch[r])
            fill(14, bank=0)
        m_k = m_kch[NCORES - 1]
        dbg_tap(3, k_fm[:, 0:T], T, sV, m_k)
        fill(70, bank=0)
        TE.wait(sCC, m_vgA)
        fill(70, bank=0)

        stP12.close()
        barrier(exclude=(sCC,))
        stP3 = ExitStack()
        v_sup = [sb(stP3, f"vsup{i}", [128, 4 * E], BF16) for i in range(4)]
        sc_sb = [sb(stP3, f"scsb{i}", [128, 4 * 512], BF16) for i in range(4)]
        rl_tmp = sb(stP3, "rltmp", [128, 512], F32)
        av_acc = sb(stP3, "av_acc", [128, 4 * E], F32)
        u_t = [sb(stP3, f"ut{i}", [128, E], BF16) for i in range(3)]
        uav_f = [sb(stP3, f"uavf{i}", [128, E], BF16) for i in range(2)]
        uavT_st = [sb(stP3, f"uavT{i}", [128, E], BF16) for i in range(2)]

        NPAIR = NSUP // 2
        m_avpair = {}
        m_sq = {}
        m_vld = {}
        m_uav = {}
        grp_box = [0]

        def emit_scores_kb(p, s_, kb):
            sl4 = s_ % 4
            idx = p * NSUP + s_
            if kb == 0:
                if idx >= 4:
                    pp, ss = (p, s_ - 4) if s_ >= 4 else (p - 1, NSUP - 4 + s_)
                    GP.wait(sT, m_avpair[(pp, ss // 2)])
                r_, o_ = divmod(KS * s_, T)
                if o_ < T // 2:
                    vsrc = vg_outA[(T // 2) * r_ + o_:(T // 2) * r_ + o_ + KS, :]
                    GP.wait(sCC, m_vgA)
                else:
                    vsrc = vg_outB[(T // 2) * r_ + o_ - T // 2:(T // 2) * r_ + o_ - T // 2 + KS, :]
                    GP.wait(sCC, m_vg)
                dma(GP, v_sup[s_ % 4][:], vsrc, sVL[s_ % 2])
                m_vld[(p, s_)] = sVL[s_ % 2].n
            if idx == 0 and kb == 0:
                TE.wait(sV, m_k)
            b = kb % 2
            te_bank(b)
            TE.op(lambda e, pp=p, ss=s_, kk=kb, bb=b:
                  e.matmul(ps[bb],
                           k_fm[0:128, KS * ss + 128 * kk:KS * ss + 128 * (kk + 1)],
                           q_fm[0:128, QP * pp:QP * (pp + 1)],
                           start=True, stop=True), sT)
            m_scmm = sT.n
            SC.wait(sT, m_scmm)
            SC.wait(sS, sS.n)
            m_rl = SC.op(lambda e, bb=b:
                         e.activation(out=rl_tmp[:], in_=ps[bb], func=ACT.Relu, scale=INV_L), sS)
            bankfree[b] = (sS, m_rl)
            if kb == 0 and idx >= 4:
                pp, ss = (p, s_ - 4) if s_ >= 4 else (p - 1, NSUP - 4 + s_)
                SC.wait(sT, m_avpair[(pp, ss // 2)])
            SC.wait(sS, sS.n)
            m_sq[(p, s_, kb)] = SC.op(
                lambda e, so=sl4, kk=kb:
                e.activation(out=sc_sb[so][0:128, 512 * kk:512 * (kk + 1)],
                             in_=rl_tmp[:], func=ACT.Square), sS)

        def emit_avpair(p, pr):
            s0, s1 = 2 * pr, 2 * pr + 1
            nxt = []
            if pr + 1 < NPAIR:
                nxt = [(2 * (pr + 1) + i // 4, i % 4) for i in range(8)]
            elif p + 1 < NPASS:
                nxt = [(0, None)] * 0
            TE.wait(sS, m_sq[(p, s1, 3)])
            TE.wait(sVL[s0 % 2], m_vld[(p, s0)])
            TE.wait(sVL[s1 % 2], m_vld[(p, s1)])
            gi = 0
            for qb in range(4):
                for g in range(2):
                    r = grp_box[0] % 2
                    grp_box[0] += 1
                    b0, b1 = 2 + 2 * r, 3 + 2 * r
                    te_bank(b0)
                    te_bank(b1)
                    for s_i in (s0, s1):
                        sl4 = s_i % 4
                        for kb in range(4):
                            for j in range(2):
                                TE.op(lambda e, so=sl4, kk=kb, qq=qb, gg=g, jj=j,
                                      bb=(b0 if j == 0 else b1), st=(s_i == s0 and kb == 0),
                                      sp=(s_i == s1 and kb == 3):
                                      e.matmul(ps[bb],
                                               sc_sb[so][0:128, 512 * kk + 128 * qq:512 * kk + 128 * (qq + 1)],
                                               v_sup[s_i % 4][0:128, E * kk + 1024 * gg + 512 * jj:
                                                              E * kk + 1024 * gg + 512 * (jj + 1)],
                                               start=st, stop=sp),
                                      sT if (s_i == s1 and kb == 3 and j == 1) else None)
                    m_grp = sT.n
                    if gi < len(nxt):
                        emit_scores_kb(p, nxt[gi][0], nxt[gi][1])
                        gi += 1
                    VE.wait(sT, m_grp)
                    VE.wait(sV, sV.n)
                    dsl = av_acc[:, E * qb + 1024 * g:E * qb + 1024 * (g + 1)]
                    pss = ps_all[0:128, 512 * b0:512 * b0 + 1024]
                    if pr == 0:
                        VE.op(lambda e, d=dsl, b_=pss: e.tensor_copy(out=d, in_=b_), sV)
                    else:
                        VE.op(lambda e, d=dsl, b_=pss: e.tensor_add(out=d, in0=b_, in1=d), sV)
                    bankfree[b0] = (sV, sV.n)
                    bankfree[b1] = (sV, sV.n)
            m_avpair[(p, pr)] = sT.n

        m_uavst = {}
        m_uavtrA = {}
        for p in range(NPASS):
            if p == 0:
                for kb in range(4):
                    emit_scores_kb(p, 0, kb)
                for kb in range(4):
                    emit_scores_kb(p, 1, kb)
            for pr in range(NPAIR):
                emit_avpair(p, pr)
            if p + 1 < NPASS:
                for kb in range(4):
                    emit_scores_kb(p + 1, 0, kb)
                for kb in range(4):
                    emit_scores_kb(p + 1, 1, kb)
            if p == 0:
                SY.wait(sSTU, sSTU.n)
                for g0 in range(2):
                    dma(SY, u_t[g0][:], u_hbm[128 * g0:128 * (g0 + 1), :], sUT[g0])
            for qb in range(4):
                sl = qb % 2
                gq_ = 4 * p + qb
                usl = gq_ % 3
                if gq_ + 2 < 4 * NPASS:
                    s2 = (gq_ + 2) % 3
                    if gq_ >= 1:
                        SY.wait(sV, m_uav[gq_ - 1])
                    dma(SY, u_t[s2][:], u_hbm[128 * (gq_ + 2):128 * (gq_ + 3), :], sUT[s2])
                VE.wait(sUT[usl], sUT[usl].n)
                if gq_ >= 2:
                    VE.wait(sT, m_uavtrA[gq_ - 2])
                VE.wait(sV, sV.n)
                m_uav[gq_] = VE.op(
                    lambda e, s2=sl, u2=usl, qq=qb:
                    e.tensor_mul(out=uav_f[s2][:], in0=u_t[u2][:], in1=av_acc[:, E * qq:E * (qq + 1)]), sV)
                TE.wait(sV, m_uav[gq_])
                if gq_ >= 2:
                    TE.wait(sUAVT[sl], sUAVT[sl].n)
                te_transpose(uavT_st[sl], uav_f[sl][:], 16, src_step=128, dst_step=128,
                             banks=(6, 7), idn=identb)
                m_uavtrA[gq_] = sT.n
                m_trevA = sS.n
                GP.wait(sS, m_trevA)
                m_uavst[gq_] = dma(GP, d_uavT[128 * gq_:128 * (gq_ + 1), :], uavT_st[sl][:],
                                   sUAVT[sl])

        stP3.close()
        stA.close()
        barrier()
        stW = ExitStack()
        w_st = [sb(stW, f"wst{i}", [128, D], F32) for i in range(6)]
        wmx = sb(stW, "wmx", [128, 96], F32)
        stP35 = ExitStack()
        uavld = [sb(stP35, f"uavld{i}", [128, E], BF16) for i in range(3)]
        Wout_bf = sb(stP35, "Wout_bf", [128, 16 * D], BF16)
        w35 = [sb(stP35, f"w35{i}", [128, D], F32) for i in range(2)]
        bout_bc = sb(stP35, "bout_bc", [128, D], F32)
        xr_t = [sb(stP35, f"xrt{i}", [128, D], F32) for i in range(3)]
        a1_t = [sb(stP35, f"a1t{i}", [128, D], F32) for i in range(2)]
        scr35 = sb(stP35, "scr35", [128, D], F32)

        fill(25, bank=0)
        dma(SY, bout_bc[:], d_bout.partition_broadcast(128), sBC)
        m_bout = sBC.n
        m_wocv = {}
        for j in range(16):
            wsl = j % 2
            rq = GP if wsl == 0 else SY
            if j >= 2:
                rq.wait(sS, m_wocv[j - 2])
            dma(rq, w35[wsl][:], d_Wout[128 * j:128 * (j + 1), :], sWO[wsl])
            SC.wait(sWO[wsl], sWO[wsl].n)
            SC.wait(sS, sS.n)
            m_wocv[j] = SC.op(lambda e, jj=j, s2=wsl:
                              e.copy(out=Wout_bf[0:128, D * jj:D * (jj + 1)], in_=w35[s2][:]), sS)
            TE.wait(sS, m_wocv[j])
            fill(10, bank=0)
        m_wo_all = m_wocv[15]

        m_uavtr = {}
        m_wo = {}
        m_sq2q = {}
        m_mxq = {}
        m_a1st = {}
        wo_box = [0]

        for t in range(2):
            dma(SY, uavld[t][:], d_uavT[128 * t:128 * (t + 1), :], sUL3[t])
            dma(GP, xr_t[t][:], dx[128 * t:128 * (t + 1), :], sXR[t])
        for t in range(NT):
            sl = t % 2
            ul = t % 3
            if t + 2 < NT:
                s2 = (t + 2) % 3
                if t >= 1:
                    SY.wait(sT, m_wo[t - 1])
                dma(SY, uavld[s2][:], d_uavT[128 * (t + 2):128 * (t + 3), :], sUL3[s2])
                if t >= 1:
                    GP.wait(sV, m_mxq[t - 1])
                dma(GP, xr_t[s2][:], dx[128 * (t + 2):128 * (t + 3), :], sXR[s2])

            TE.wait(sUL3[ul], sUL3[ul].n)
            for dn in range(2):
                b = wo_box[0] % 4
                wo_box[0] += 1
                te_bank(b)
                for j in range(16):
                    if t == 0 and dn == 0:
                        TE.wait(sS, m_wocv[j])
                    TE.op(lambda e, jj=j, dd=dn, bb=b, u2=ul:
                          e.matmul(ps[bb],
                                   uavld[u2][0:128, 128 * jj:128 * (jj + 1)],
                                   Wout_bf[0:128, D * jj + 512 * dd:D * jj + 512 * (dd + 1)],
                                   start=(jj == 0), stop=(jj == 15)),
                          sT if j == 15 else None)
                m_wodn = sT.n
                VE.wait(sT, m_wodn)
                VE.wait(sXR[ul], sXR[ul].n)
                VE.wait(sBC, m_bout)
                if t >= 2 and dn == 0:
                    VE.wait(sS, m_sq2q[t - 2])
                    VE.wait(sX2ST[sl], sX2ST[sl].n)
                VE.wait(sV, sV.n)
                VE.op(lambda e, bb=b, dd=dn, s2=sl:
                      e.tensor_add(out=a1_t[s2][:, 512 * dd:512 * (dd + 1)], in0=ps[bb],
                                   in1=bout_bc[:, 512 * dd:512 * (dd + 1)]), sV)
                VE.wait(sV, sV.n)
                mep = VE.op(lambda e, s2=sl, u2=ul, dd=dn:
                            e.scalar_tensor_tensor(out=a1_t[s2][:, 512 * dd:512 * (dd + 1)],
                                                   in0=xr_t[u2][:, 512 * dd:512 * (dd + 1)],
                                                   scalar=2.0, in1=a1_t[s2][:, 512 * dd:512 * (dd + 1)],
                                                   op0=OP.mult, op1=OP.add), sV)
                bankfree[b] = (sV, mep)
            m_wo[t] = sT.n
            m_a1 = sV.n
            VE.wait(sV, m_a1)
            VE.op(lambda e, gi=t, s2=sl: e.reduce_sum(out=stats[:, gi:gi + 1], in_=a1_t[s2][:], axis=AX), sV)
            SC.wait(sV, sV.n)
            SC.wait(sS, sS.n)
            m_sq2q[t] = SC.op(lambda e, gi=t, s2=sl:
                              e.activation(out=scrS[:], in_=a1_t[s2][:], func=ACT.Square,
                                           accum_out=stats[:, 16 + gi:17 + gi]), sS)
            VE.wait(sV, sV.n)
            VE.op(lambda e, gi=t: e.tensor_scalar_mul(stats[:, 60:61], stats[:, gi:gi + 1], 1.0 / D), sV)
            VE.wait(sV, sV.n)
            VE.op(lambda e, s2=sl: e.tensor_single_scalar(out=scr35[:], in_=a1_t[s2][:],
                                                          scalar=stats[:, 60:61], op=OP.subtract), sV)
            VE.wait(sV, sV.n)
            m_mxq[t] = VE.op(lambda e, gi=t:
                             e.reduce_max(out=stats[:, 32 + gi:33 + gi], in_=scr35[:], axis=AX,
                                          apply_absolute_value=True), sV)
            GP.wait(sV, m_mxq[t])
            GP.wait(sS, m_sq2q[t])
            m_a1st[t] = dma(GP, x2_hbm[128 * t:128 * (t + 1), :], a1_t[sl][:], sX2ST[sl])

        VE.wait(sS, sS.n)
        VE.wait(sV, sV.n)
        VE.op(lambda e: e.tensor_scalar_mul(cols2[:, 0:16], stats[:, 0:16], 1.0 / D), sV)
        VE.wait(sV, sV.n)
        VE.op(lambda e: e.tensor_scalar_mul(cols2[:, 32:48], stats[:, 16:32], 1.0 / D), sV)
        VE.wait(sV, sV.n)
        VE.op(lambda e: e.tensor_tensor(out=stats[:, 16:32], in0=cols2[:, 0:16], in1=cols2[:, 0:16], op=OP.mult), sV)
        VE.wait(sV, sV.n)
        VE.op(lambda e: e.tensor_tensor(out=cols2[:, 32:48], in0=cols2[:, 32:48], in1=stats[:, 16:32], op=OP.subtract), sV)
        VE.wait(sV, sV.n)
        mv2 = VE.op(lambda e: e.tensor_scalar_add(cols2[:, 32:48], cols2[:, 32:48], LN_EPS), sV)
        SC.wait(sV, mv2)
        SC.wait(sS, sS.n)
        ms2 = SC.op(lambda e: e.activation(out=stats[:, 16:32], in_=cols2[:, 32:48], func=ACT.Sqrt), sS)
        VE.wait(sS, ms2)
        VE.op(lambda e: e.reciprocal(out=cols2[:, 16:32], in_=stats[:, 16:32]), sV)
        VE.wait(sV, sV.n)
        VE.op(lambda e: e.tensor_tensor(out=stats[:, 16:32], in0=cols2[:, 16:32], in1=cols2[:, 16:32], op=OP.mult), sV)
        VE.wait(sV, sV.n)
        VE.op(lambda e: e.tensor_tensor(out=stats[:, 16:32], in0=stats[:, 16:32], in1=cols2[:, 32:48], op=OP.mult), sV)
        VE.wait(sV, sV.n)
        VE.op(lambda e: e.tensor_scalar(out=stats[:, 16:32], in0=stats[:, 16:32], scalar1=3.0, scalar2=-0.5,
                                        op0=OP.subtract, op1=OP.mult), sV)
        VE.wait(sV, sV.n)
        VE.op(lambda e: e.tensor_tensor(out=cols2[:, 16:32], in0=cols2[:, 16:32], in1=stats[:, 16:32], op=OP.mult), sV)
        VE.wait(sV, sV.n)
        VE.op(lambda e: e.tensor_tensor(out=stats[:, 32:48], in0=stats[:, 32:48], in1=cols2[:, 16:32], op=OP.mult), sV)
        VE.wait(sV, sV.n)
        VE.op(lambda e: e.tensor_reduce(out=cols[:, C_MX:C_MX + 1], in_=stats[:, 32:48], axis=AX, op=OP.max), sV)
        m_mx = sV.n

        stP35.close()
        barrier()
        stB1 = ExitStack()
        xq_fm = sb(stB1, "xq_fm", [128, 8 * T], BF16)
        kgT = sb(stB1, "kgT", [128, 8 * I_FF], BF16)
        kuT = sb(stB1, "kuT", [128, 8 * I_FF], BF16)
        qi_t = [sb(stB1, f"qit{i}", [128, D], I32) for i in range(2)]
        qf_t = [sb(stB1, f"qft{i}", [128, D], F32) for i in range(2)]
        qb16 = [sb(stB1, f"qb16_{i}", [128, D], BF16) for i in range(2)]
        x2q_t = [sb(stB1, f"x2q{i}", [128, D], F32) for i in range(3)]
        sgx = sb(stB1, "sgx", [128, NI], F32)
        sux = sb(stB1, "sux", [128, NI], F32)
        g_tmp = [sb(stB1, f"gtmp{i}", [128, 512], F32) for i in range(3)]
        up_tmp = [sb(stB1, f"uptmp{i}", [128, 512], F32) for i in range(3)]
        vh_t = [sb(stB1, f"vht{i}", [128, 512], F32) for i in range(3)]

        fill(230, bank=0)
        GP.wait(sV, m_mx)
        dma(GP, mx_in, cols[:, C_MX:C_MX + 1], sSCB)
        GP.wait(sSCB, sSCB.n)
        GP.op(lambda e: e.collective_compute("AllReduce", OP.max, replica_groups=RG,
                                             ins=[mx_in], outs=[mx_out]), sCC)
        m_cc_mx = sCC.n
        TE.wait(sCC, m_cc_mx)
        fill(40, bank=0)
        GP.wait(sCC, sCC.n)
        dma(GP, row128[0:1, :], mx_out.rearrange("p o -> o p"), sSCB)
        VE.wait(sSCB, sSCB.n)
        VE.wait(sV, sV.n)
        VE.op(lambda e: e.reduce_max(out=one1[0:1, O_M:O_M + 1], in_=row128[0:1, :], axis=AX), sV)
        SC.wait(sBC, m_bc)
        SC.wait(sS, sS.n)
        m_sig = SC.op(lambda e: e.activation(out=one1[0:1, O_SIG:O_SIG + 1],
                                             in_=one1[0:1, O_CLX:O_CLX + 1], func=ACT.Sigmoid), sS)
        VE.wait(sS, m_sig)
        VE.wait(sV, sV.n)
        VE.op(lambda e: e.tensor_tensor(out=one1[0:1, O_SX:O_SX + 1], in0=one1[0:1, O_SIG:O_SIG + 1],
                                        in1=one1[0:1, O_M:O_M + 1], op=OP.mult), sV)
        VE.wait(sV, sV.n)
        VE.op(lambda e: e.tensor_scalar_mul(one1[0:1, O_SX:O_SX + 1], one1[0:1, O_SX:O_SX + 1],
                                            1.0 / 127.0), sV)
        m_rsx = recip_nr(one1[0:1, O_SX:O_SX + 1], one1[0:1, O_RSX:O_RSX + 1])
        GP.wait(sV, m_rsx)
        dma(GP, sc_hbm[0:1, 0:2], one1[0:1, O_SX:O_RSX + 1], sSCB)
        GP.wait(sSCB, sSCB.n)
        dma(GP, cols[:, C_SX:C_SX + 1], sc_hbm[0:1, 0:1].partition_broadcast(128), sSCB)
        dma(GP, cols[:, C_RSX:C_RSX + 1], sc_hbm[0:1, 1:2].partition_broadcast(128), sSCB)
        m_sxc = sSCB.n
        VE.wait(sSCB, m_sxc)
        VE.wait(sV, sV.n)
        m_rs2 = VE.op(lambda e: e.tensor_scalar_mul(cols2[:, 32:48], cols2[:, 16:32],
                                                    cols[:, C_RSX:C_RSX + 1]), sV)

        m_kx = {}
        m_kxtr = {}
        m_kxtr_s = {}
        for i in range(2):
            dma(SY, x2q_t[i][:], x2_hbm[128 * i:128 * (i + 1), :], sXQ[i])
        for i in range(NT):
            sl = i % 2
            xs = i % 3
            if i + 2 < NT:
                s2 = (i + 2) % 3
                if i >= 1:
                    SY.wait(sV, m_kx[i - 1][0])
                dma(SY, x2q_t[s2][:], x2_hbm[128 * (i + 2):128 * (i + 3), :], sXQ[s2])
            VE.wait(sXQ[xs], sXQ[xs].n)
            VE.wait(sV, max(sV.n, m_rs2))
            if i >= 2:
                VE.wait(sT, m_kxtr[i - 2])
            mq1 = VE.op(lambda e, s2=sl, x3=xs, ii=i:
                        e.tensor_scalar(out=qf_t[s2][:], in0=x2q_t[x3][:],
                                        scalar1=cols2[:, ii:ii + 1], scalar2=cols2[:, 32 + ii:33 + ii],
                                        op0=OP.subtract, op1=OP.mult), sV)
            VE.wait(sV, sV.n)
            mq2 = VE.op(lambda e, s2=sl: e.tensor_scalar(out=qi_t[s2][:], in0=qf_t[s2][:],
                                                         scalar1=-128.0, scalar2=127.0,
                                                         op0=OP.max, op1=OP.min), sV)
            SC.wait(sV, mq2)
            if i >= 2:
                SC.wait(sT, m_kxtr[i - 2])
            SC.wait(sS, sS.n)
            mq3 = SC.op(lambda e, s2=sl: e.copy(out=qb16[s2][:], in_=qi_t[s2][:]), sS)
            m_kx[i] = (mq1, mq3)
            TE.wait(sS, mq3)
            te_transpose(xq_fm[:, 128 * i:], qb16[sl][:], 8, src_step=128, dst_step=T,
                         banks=(6, 7), idn=identb)
            m_kxtr[i] = sT.n
            m_kxtr_s[i] = sS.n
            fill(6, bank=0)
        m_xq_tr = m_kxtr_s[NT - 1]
        dbg_tap(10, xq_fm[:, 0:T], T, sS, m_xq_tr)

        m_kwq = {}
        m_kwtr = {}
        m_kwtr_s = {}
        m_gev = {}
        m_upev = {}
        m_vhmul = {}
        m_vhst = {}
        vhe_box = [0]

        m_kwq1 = {}

        def issue_kw_load(c):
            if c >= NI:
                return
            for wsel, dsrc in enumerate((d_wg, d_wu)):
                idx = 2 * c + wsel
                sl = idx % 6
                if idx >= 6:
                    SY.wait(sV, m_kwq1[idx - 6])
                dma(SY, w_st[sl][:], dsrc[128 * c:128 * (c + 1), :], sWS[sl])

        m_scl = {}

        def emit_kw_prep(c):
            for wsel, (dsrc, kT) in enumerate(((d_wg, kgT), (d_wu, kuT))):
                idx = 2 * c + wsel
                cx = wsel * NI + c
                sl = idx % 2
                ws4 = idx % 6
                scol = sgx if wsel == 0 else sux
                VE.wait(sWS[ws4], sWS[ws4].n)
                if idx == 0:
                    VE.wait(sSCB, m_sxc)
                VE.wait(sV, sV.n)
                VE.op(lambda e, w4=ws4, ix=cx:
                      e.reduce_max(out=wmx[:, ix:ix + 1], in_=w_st[w4][:],
                                   axis=AX, apply_absolute_value=True), sV)
                VE.wait(sV, sV.n)
                VE.op(lambda e, ix=cx: e.reciprocal(out=wmx[:, 44:45], in_=wmx[:, ix:ix + 1]), sV)
                VE.wait(sV, sV.n)
                VE.op(lambda e, ix=cx: e.tensor_tensor(out=wmx[:, 45:46], in0=wmx[:, ix:ix + 1],
                                                       in1=wmx[:, 44:45], op=OP.mult), sV)
                VE.wait(sV, sV.n)
                VE.op(lambda e: e.tensor_scalar(out=wmx[:, 45:46], in0=wmx[:, 45:46],
                                                scalar1=2.0, scalar2=-1.0,
                                                op0=OP.subtract, op1=OP.mult), sV)
                VE.wait(sV, sV.n)
                VE.op(lambda e: e.tensor_tensor(out=wmx[:, 44:45], in0=wmx[:, 44:45],
                                                in1=wmx[:, 45:46], op=OP.mult), sV)
                VE.wait(sV, sV.n)
                VE.op(lambda e, ix=cx: e.tensor_scalar_mul(wmx[:, 48 + ix:49 + ix],
                                                           wmx[:, 44:45], 127.0), sV)
                VE.wait(sV, sV.n)
                m_scl[idx] = VE.op(lambda e, sc2=scol, cc=c, ix=cx:
                                   e.tensor_scalar(out=sc2[:, cc:cc + 1], in0=wmx[:, ix:ix + 1],
                                                   scalar1=cols[:, C_SX:C_SX + 1],
                                                   scalar2=1.0 / 127.0,
                                                   op0=OP.mult, op1=OP.mult), sV)
                VE.wait(sT, m_kwtr[idx - 2] if idx >= 2 else m_kxtr[NT - 2 + idx])
                VE.wait(sV, sV.n)
                mm1 = VE.op(lambda e, s2=sl, w4=ws4, ix=cx:
                            e.tensor_scalar(out=qf_t[s2][:], in0=w_st[w4][:],
                                            scalar1=wmx[:, 48 + ix:49 + ix], scalar2=-128.0,
                                            op0=OP.mult, op1=OP.max), sV)
                m_kwq1[idx] = mm1
                VE.wait(sV, sV.n)
                mm2 = VE.op(lambda e, s2=sl: e.tensor_scalar(out=qi_t[s2][:], in0=qf_t[s2][:],
                                                             scalar1=127.0, scalar2=None,
                                                             op0=OP.min), sV)
                SC.wait(sV, mm2)
                SC.wait(sT, m_kwtr[idx - 2] if idx >= 2 else m_kxtr[NT - 2 + idx])
                SC.wait(sS, sS.n)
                m_kwq[idx] = SC.op(lambda e, s2=sl: e.copy(out=qb16[s2][:], in_=qi_t[s2][:]), sS)
                TE.wait(sS, m_kwq[idx])
                te_transpose(kT[:, 128 * c:], qb16[sl][:], 8, src_step=128, dst_step=I_FF,
                             banks=(6, 7), idn=identb)
                m_kwtr[idx] = sT.n
                m_kwtr_s[idx] = sS.n

        def emit_gateup_mm(c):
            TE.wait(sS, m_kwtr_s[2 * c + 1])
            for n in range(4):
                vhe = vhe_box[0]
                if c == 0:
                    TE.wait(sS, m_kxtr_s[4 * n + 3])
                bg = vhe % 3
                bu = 3 + vhe % 3
                te_bank(bg)
                for d_ in range(8):
                    TE.op(lambda e, dd=d_, cc=c, nn=n, bb=bg:
                          e.matmul(ps[bb],
                                   kgT[0:128, I_FF * dd + 128 * cc:I_FF * dd + 128 * (cc + 1)],
                                   xq_fm[0:128, dd * T + 512 * nn:dd * T + 512 * (nn + 1)],
                                   start=(dd == 0), stop=(dd == 7)),
                          sT if d_ == 7 else None)
                m_gmm = sT.n
                te_bank(bu)
                for d_ in range(8):
                    TE.op(lambda e, dd=d_, cc=c, nn=n, bb=bu:
                          e.matmul(ps[bb],
                                   kuT[0:128, I_FF * dd + 128 * cc:I_FF * dd + 128 * (cc + 1)],
                                   xq_fm[0:128, dd * T + 512 * nn:dd * T + 512 * (nn + 1)],
                                   start=(dd == 0), stop=(dd == 7)),
                          sT if d_ == 7 else None)
                m_umm = sT.n
                SC.wait(sT, m_gmm)
                if n == 0:
                    SC.wait(sV, m_scl[2 * c])
                if vhe >= 3:
                    SC.wait(sV, m_vhmul[vhe - 3])
                SC.wait(sS, sS.n)
                m_gev[vhe] = SC.op(lambda e, bb=bg, s2=vhe % 3, cc=c:
                                   e.activation(out=g_tmp[s2][:], in_=ps[bb], func=ACT.Silu,
                                                scale=sgx[:, cc:cc + 1]), sS)
                bankfree[bg] = (sS, m_gev[vhe])
                SC.wait(sT, m_umm)
                if n == 0:
                    SC.wait(sV, m_scl[2 * c + 1])
                if vhe >= 3:
                    SC.wait(sV, m_vhmul[vhe - 3])
                SC.wait(sS, sS.n)
                m_upev[vhe] = SC.op(lambda e, bb=bu, s2=vhe % 3, cc=c:
                                    e.activation(out=up_tmp[s2][:], in_=ps[bb], func=ACT.Copy,
                                                 scale=sux[:, cc:cc + 1]), sS)
                bankfree[bu] = (sS, m_upev[vhe])
                VE.wait(sS, m_gev[vhe])
                VE.wait(sS, m_upev[vhe])
                if vhe >= 3:
                    VE.wait(sVH[vhe % 3], sVH[vhe % 3].n)
                VE.wait(sV, sV.n)
                m_vhmul[vhe] = VE.op(lambda e, s2=vhe % 3:
                                     e.tensor_mul(out=vh_t[s2][:], in0=g_tmp[s2][:], in1=up_tmp[s2][:]), sV)
                VE.wait(sV, sV.n)
                VE.op(lambda e, s2=vhe % 3: e.reduce_max(out=stats[:, 20:21], in_=vh_t[s2][:], axis=AX,
                                                         apply_absolute_value=True), sV)
                VE.wait(sV, sV.n)
                if vhe == 0:
                    m_mh_ = VE.op(lambda e: e.tensor_copy(out=cols[:, C_MH:C_MH + 1],
                                                          in_=stats[:, 20:21]), sV)
                else:
                    m_mh_ = VE.op(lambda e: e.tensor_tensor(out=cols[:, C_MH:C_MH + 1],
                                                            in0=cols[:, C_MH:C_MH + 1],
                                                            in1=stats[:, 20:21], op=OP.max), sV)
                GP.wait(sV, m_mh_)
                m_vhst[vhe] = dma(GP, vh_hbm[128 * c:128 * (c + 1), 512 * n:512 * (n + 1)],
                                  vh_t[vhe % 3][:], sVH[vhe % 3])
                vhe_box[0] = vhe + 1
            return m_mh_

        issue_kw_load(0)
        issue_kw_load(1)
        issue_kw_load(2)
        for c in range(NI):
            emit_kw_prep(c)
            issue_kw_load(c + 3)
            if c >= 1:
                m_mh = emit_gateup_mm(c - 1)
        m_mh = emit_gateup_mm(NI - 1)

        GP.wait(sV, m_mh)
        dma(GP, mh_in, cols[:, C_MH:C_MH + 1], sSCB)
        GP.wait(sSCB, sSCB.n)
        GP.op(lambda e: e.collective_compute("AllReduce", OP.max, replica_groups=RG,
                                             ins=[mh_in], outs=[mh_out]), sCC)
        m_ccmh = sCC.n

        stB1.close()
        stW.close()
        barrier(exclude=(sCC,))
        stB2 = ExitStack()
        kdT = sb(stB2, "kdT", [128, NI * D], BF16)
        h_fm = sb(stB2, "h_fm", [128, NI * T], BF16)
        wd_st = sb(stB2, "wd_st", [128, I_FF], F32)
        wmx2 = sb(stB2, "wmx2", [128, 32], F32)
        qi2 = [sb(stB2, f"qi2{i}", [128, 1024], I32) for i in range(2)]
        qf2 = [sb(stB2, f"qf2{i}", [128, 1024], F32) for i in range(2)]
        vh_ld = [sb(stB2, f"vhld{i}", [128, 1024], F32) for i in range(3)]
        sdh_bc = sb(stB2, "sdh_bc", [128, D], F32)
        x2r_t = [sb(stB2, f"x2rt{i}", [128, D], F32) for i in range(3)]
        a2_t = [sb(stB2, f"a2t{i}", [128, D], F32) for i in range(2)]
        ep2 = [sb(stB2, f"ep2{i}", [128, 512], F32) for i in range(2)]

        fill(50, bank=4)
        m_kdq = {}
        m_kdtr = {}
        m_sdc = {}
        HW2 = 704
        for c in range(8):
            if c >= 1:
                SY.wait(sS, m_kdq[c - 1])
                SY.wait(sT, m_kdtr[c - 1])
            dma(SY, wd_st[:], d_wd[128 * c:128 * (c + 1), :], sWD)
            VE.wait(sWD, sWD.n)
            VE.wait(sV, sV.n)
            VE.op(lambda e, cc=c: e.reduce_max(out=wmx2[:, cc:cc + 1], in_=wd_st[:],
                                               axis=AX, apply_absolute_value=True), sV)
            VE.wait(sV, sV.n)
            VE.op(lambda e, cc=c: e.reciprocal(out=wmx2[:, 16:17], in_=wmx2[:, cc:cc + 1]), sV)
            VE.wait(sV, sV.n)
            VE.op(lambda e, cc=c: e.tensor_tensor(out=wmx2[:, 17:18], in0=wmx2[:, cc:cc + 1],
                                                  in1=wmx2[:, 16:17], op=OP.mult), sV)
            VE.wait(sV, sV.n)
            VE.op(lambda e: e.tensor_scalar(out=wmx2[:, 17:18], in0=wmx2[:, 17:18],
                                            scalar1=2.0, scalar2=-1.0,
                                            op0=OP.subtract, op1=OP.mult), sV)
            VE.wait(sV, sV.n)
            VE.op(lambda e: e.tensor_tensor(out=wmx2[:, 16:17], in0=wmx2[:, 16:17],
                                            in1=wmx2[:, 17:18], op=OP.mult), sV)
            VE.wait(sV, sV.n)
            VE.op(lambda e, cc=c: e.tensor_scalar_mul(wmx2[:, 24 + cc:25 + cc],
                                                      wmx2[:, 16:17], 127.0), sV)
            VE.wait(sV, sV.n)
            m_sdc[c] = VE.op(lambda e, cc=c: e.tensor_scalar_mul(wmx2[:, cc:cc + 1],
                                                                 wmx2[:, cc:cc + 1], 1.0 / 127.0), sV)
            SY.wait(sV, m_sdc[c])
            dma(SY, sd_hbm[0:1, 128 * c:128 * (c + 1)], wmx2[:, c:c + 1], sSD)
            for hf in range(4):
                sl = hf % 2
                pre = [] if (c < 1 or hf > 0) else [(sTRKD, sTRKD.n)]
                for w_ in pre:
                    SC.wait(*w_)
                VE.wait(sV, sV.n)
                VE.op(lambda e, cc=c, h2=hf, s2=sl:
                      e.tensor_scalar(out=qf2[s2][0:128, 0:HW2], in0=wd_st[0:128, HW2 * h2:HW2 * (h2 + 1)],
                                      scalar1=wmx2[:, 24 + cc:25 + cc], scalar2=-128.0,
                                      op0=OP.mult, op1=OP.max), sV)
                VE.wait(sV, sV.n)
                md2 = VE.op(lambda e, s2=sl: e.tensor_scalar(out=qi2[s2][0:128, 0:HW2],
                                                             in0=qf2[s2][0:128, 0:HW2],
                                                             scalar1=127.0, scalar2=None,
                                                             op0=OP.min), sV)
                SC.wait(sV, md2)
                if c >= 1 and hf == 0:
                    SC.wait(sT, m_kdtr[c - 1])
                SC.wait(sS, sS.n)
                m_kdq[c] = SC.op(lambda e, s2=sl, h2=hf: e.copy(out=wd_st[0:128, HW2 * h2:HW2 * (h2 + 1)],
                                                                in_=qi2[s2][0:128, 0:HW2]), sS)
            TE.wait(sS, m_kdq[c])
            te_transpose(kdT[:, 128 * c:], wd_st[:], NI, src_step=128, dst_step=D, banks=(4, 5, 6, 7))
            m_kdtr[c] = sT.n
            m_kdtr_ev = sS.n
            fill(20, bank=0)
        m_kd_tr = m_kdtr_ev

        GP.wait(sCC, m_ccmh)
        dma(GP, row128[0:1, :], mh_out.rearrange("p o -> o p"), sSCB)
        VE.wait(sSCB, sSCB.n)
        VE.wait(sV, sV.n)
        VE.op(lambda e: e.reduce_max(out=one1[0:1, O_M:O_M + 1], in_=row128[0:1, :], axis=AX), sV)
        SC.wait(sS, sS.n)
        m_sigd = SC.op(lambda e: e.activation(out=one1[0:1, O_SIG:O_SIG + 1],
                                              in_=one1[0:1, O_CLD:O_CLD + 1], func=ACT.Sigmoid), sS)
        VE.wait(sS, m_sigd)
        VE.wait(sV, sV.n)
        VE.op(lambda e: e.tensor_tensor(out=one1[0:1, O_SH:O_SH + 1], in0=one1[0:1, O_SIG:O_SIG + 1],
                                        in1=one1[0:1, O_M:O_M + 1], op=OP.mult), sV)
        VE.wait(sV, sV.n)
        VE.op(lambda e: e.tensor_scalar_mul(one1[0:1, O_SH:O_SH + 1], one1[0:1, O_SH:O_SH + 1],
                                            1.0 / 127.0), sV)
        m_rsh = recip_nr(one1[0:1, O_SH:O_SH + 1], one1[0:1, O_RSH:O_RSH + 1])
        GP.wait(sV, m_rsh)
        dma(GP, sc_hbm[0:1, 2:4], one1[0:1, O_SH:O_RSH + 1], sSCB)
        GP.wait(sSCB, sSCB.n)
        dma(GP, cols[:, C_SH:C_SH + 1], sc_hbm[0:1, 2:3].partition_broadcast(128), sSCB)
        dma(GP, cols[:, C_RSH:C_RSH + 1], sc_hbm[0:1, 3:4].partition_broadcast(128), sSCB)
        m_shc = sSCB.n

        SY.wait(sSD, sSD.n)
        dma(SY, sdh_bc[:], sd_hbm.partition_broadcast(128), sBC)
        VE.wait(sBC, sBC.n)
        VE.wait(sSCB, m_shc)
        VE.wait(sV, sV.n)
        m_sdh = VE.op(lambda e: e.tensor_scalar_mul(sdh_bc[:], sdh_bc[:], cols[:, C_SH:C_SH + 1]), sV)

        m_hq = {}
        for j0 in range(2):
            rq = SY if j0 % 2 == 0 else GP
            dma(rq, vh_ld[j0][:], vh_hbm[128 * (j0 // 2):128 * (j0 // 2) + 128,
                                         1024 * (j0 % 2):1024 * (j0 % 2) + 1024], sVHL[j0])
        for j in range(2 * NI):
            c, hf = divmod(j, 2)
            sl = j % 3
            if j + 2 < 2 * NI:
                s2 = (j + 2) % 3
                c3, h3 = divmod(j + 2, 2)
                rq = SY if (j + 2) % 2 == 0 else GP
                if j >= 1:
                    rq.wait(sS, m_hq[j - 1])
                dma(rq, vh_ld[s2][:], vh_hbm[128 * c3:128 * (c3 + 1), 1024 * h3:1024 * (h3 + 1)],
                    sVHL[s2])
            VE.wait(sVHL[sl], sVHL[sl].n)
            if j < 2:
                VE.wait(sSCB, m_shc)
            pre_ve = [(sS, m_hq[j - 2])] if j >= 2 else []
            m_hq[j] = quant(vh_ld[sl][:], cols[:, C_RSH:C_RSH + 1],
                            qi2[j % 2][0:128, 0:1024], qf2[j % 2][0:128, 0:1024],
                            h_fm[0:128, T * c + 1024 * hf:T * c + 1024 * (hf + 1)],
                            pre_ve_waits=pre_ve)
            TE.wait(sS, m_hq[j])
            fill(9, bank=6)
        m_hq_all = m_hq[2 * NI - 1]
        dbg_tap(13, h_fm[:, 0:T], T, sS, m_hq_all)
        dbg_tap(14, kdT[:, 0:2048], 2048, sS, m_kd_tr)
        dbg_tap(15, sdh_bc[:], D, sV, m_sdh)

        m_oev = {}
        m_ost = {}
        TE.wait(sS, m_hq_all)
        TE.wait(sS, m_kd_tr)
        for t in range(2):
            dma(SY, x2r_t[t][:], x2_hbm[128 * t:128 * (t + 1), :], sX2R[t])
        for t in range(NT):
            sl = t % 3
            gidx = t
            if t + 2 < NT:
                s2 = (t + 2) % 3
                if t >= 1:
                    SY.wait(sV, m_oev[t - 1])
                    SY.wait(sOUTs[s2], sOUTs[s2].n)
                dma(SY, x2r_t[s2][:], x2_hbm[128 * (t + 2):128 * (t + 3), :], sX2R[s2])
            VE.wait(sX2R[sl], sX2R[sl].n)
            VE.wait(sV, sV.n)
            m_x2n = VE.op(lambda e, s2=sl, gi=gidx:
                          e.tensor_scalar(out=x2r_t[s2][:], in0=x2r_t[s2][:],
                                          scalar1=cols2[:, gi:gi + 1], scalar2=cols2[:, 16 + gi:17 + gi],
                                          op0=OP.subtract, op1=OP.mult), sV)
            for hn in range(2):
                b = 2 * sl + hn
                te_bank(b)
                for cc in range(NI):
                    TE.op(lambda e, c2=cc, tt=t, hh=hn, bb=b:
                          e.matmul(ps[bb],
                                   h_fm[0:128, T * c2 + 128 * tt:T * c2 + 128 * (tt + 1)],
                                   kdT[0:128, D * c2 + 512 * hh:D * c2 + 512 * (hh + 1)],
                                   start=(c2 == 0), stop=(c2 == NI - 1)),
                          sT if cc == NI - 1 else None)
                m_dmm = sT.n
                VE.wait(sT, m_dmm)
                if t == 0 and hn == 0:
                    VE.wait(sV, m_sdh)
                VE.wait(sV, sV.n)
                VE.op(lambda e, bb=b, hh=hn:
                      e.tensor_mul(out=ep2[hh][:], in0=ps[bb],
                                   in1=sdh_bc[:, 512 * hh:512 * (hh + 1)]), sV)
                VE.wait(sV, max(sV.n, m_x2n))
                mep = VE.op(lambda e, s2=sl, a2=t % 2, hh=hn:
                            e.tensor_add(out=a2_t[a2][:, 512 * hh:512 * (hh + 1)],
                                         in0=x2r_t[s2][:, 512 * hh:512 * (hh + 1)], in1=ep2[hh][:]), sV)
                bankfree[b] = (sV, mep)
            ln_stats(a2_t[t % 2][:], stats[:, 14:15], stats[:, 15:16])
            VE.wait(sV, sV.n)
            m_oev[t] = VE.op(lambda e, s2=sl, a2=t % 2:
                             e.tensor_scalar(out=x2r_t[s2][:], in0=a2_t[a2][:],
                                             scalar1=stats[:, 14:15],
                                             scalar2=stats[:, 15:16],
                                             op0=OP.subtract, op1=OP.mult), sV)
            GP.wait(sV, m_oev[t])
            m_ost[t] = dma(GP, d_out[128 * t:128 * (t + 1), :], x2r_t[sl][:], sOUTs[sl])

        GP.wait(sOUTs[0], sOUTs[0].n)
        GP.wait(sOUTs[1], sOUTs[1].n)
        GP.wait(sOUTs[2], sOUTs[2].n)
        stB2.close()

        blk = st0.enter_context(nc.Block())

        def realize(rec):
            def runner(eng):
                for it in rec.items:
                    if it[0] == "w":
                        eng.wait_ge(it[1].h, it[2])
                    else:
                        instr = it[1](eng)
                        if it[2] is not None:
                            instr.then_inc(it[2].h, it[3])
            return runner

        blk.sync(realize(SY))
        blk.vector(realize(VE))
        blk.scalar(realize(SC))
        blk.tensor(realize(TE))
        blk.gpsimd(realize(GP))

    return nc


_NC_CACHE = [None]


def kernel(L0_attn_x, L0_attn_ln_g, L0_attn_ln_b, L0_attn_W_in, L0_attn_b_in,
           L0_attn_W_out, L0_attn_b_out, L0_attn_gamma_q, L0_attn_beta_q,
           L0_attn_gamma_k, L0_attn_beta_k, L0_ffn_clip_x, L0_ffn_clip_down,
           L0_ffn_w_gate, L0_ffn_w_up, L0_ffn_w_down):
    f32 = np.float32
    xr = np.ascontiguousarray(np.asarray(L0_attn_x, dtype=f32).reshape(L, D))
    shared = {
        "ln_g": np.asarray(L0_attn_ln_g, f32).reshape(1, D),
        "ln_b": np.asarray(L0_attn_ln_b, f32).reshape(1, D),
        "W_in": np.ascontiguousarray(np.asarray(L0_attn_W_in, f32)),
        "b_in": np.asarray(L0_attn_b_in, f32).reshape(1, F_IN),
        "W_out": np.ascontiguousarray(np.asarray(L0_attn_W_out, f32)),
        "b_out": np.asarray(L0_attn_b_out, f32).reshape(1, D),
        "gq": np.ascontiguousarray(np.asarray(L0_attn_gamma_q, f32).reshape(S, 1)),
        "bq": np.ascontiguousarray(np.asarray(L0_attn_beta_q, f32).reshape(S, 1)),
        "gk": np.ascontiguousarray(np.asarray(L0_attn_gamma_k, f32).reshape(S, 1)),
        "bk": np.ascontiguousarray(np.asarray(L0_attn_beta_k, f32).reshape(S, 1)),
        "clip_x": np.asarray(L0_ffn_clip_x, f32).reshape(1, 1),
        "clip_d": np.asarray(L0_ffn_clip_down, f32).reshape(1, 1),
        "w_gate": np.ascontiguousarray(np.asarray(L0_ffn_w_gate, f32)),
        "w_up": np.ascontiguousarray(np.asarray(L0_ffn_w_up, f32)),
        "w_down": np.ascontiguousarray(np.asarray(L0_ffn_w_down, f32)),
        "ident": np.eye(128, dtype=f32),
    }
    if _NC_CACHE[0] is None:
        _NC_CACHE[0] = build_program()
    nc = _NC_CACHE[0]
    in_maps = []
    for c in range(NCORES):
        m = dict(shared)
        m["x"] = np.ascontiguousarray(xr[T * c:T * (c + 1)])
        in_maps.append(m)
    res = run_bass_kernel_spmd(nc, in_maps, core_ids=list(range(NCORES)))
    out = np.concatenate([res.results[c]["out"] for c in range(NCORES)], axis=0)
    return out.reshape(1, L, D).astype(np.float32)


# version 15
# speedup vs baseline: 1.0388x; 1.0039x over previous
import numpy as np
from contextlib import ExitStack
import concourse.bass as bass
import concourse.mybir as mybir
from concourse.bass_utils import run_bass_kernel_spmd

F32 = mybir.dt.float32
BF16 = mybir.dt.bfloat16
I32 = mybir.dt.int32
AX = mybir.AxisListType.X
OP = mybir.AluOpType
ACT = mybir.ActivationFunctionType

NCORES = 8
L = 16384
T = L // NCORES
D = 1024
E = 2048
S = 128
F_IN = 2 * E + S
I_FF = 2816
NI = I_FF // 128
LN_EPS = 1e-5
NT = T // 128
KS = 512
NSUP = L // KS
NPASS = 4
QP = T // NPASS
INV_L = 1.0 / L


class Rec:
    def __init__(self, name):
        self.name = name
        self.items = []

    def wait(self, sem, val=None):
        v = sem.n if val is None else val
        assert v <= sem.n, f"{self.name}: wait {sem.name}>={v} > recorded {sem.n}"
        if v > 0:
            self.items.append(("w", sem, v))

    def op(self, fn, sem=None, amt=1):
        self.items.append(("o", fn, sem, amt if sem else 0))
        if sem is not None:
            sem.n += amt
            return sem.n
        return None


class Sem:
    def __init__(self, stack, nc, name):
        self.h = stack.enter_context(nc.semaphore(name))
        self.name = name
        self.n = 0


def build_program(debug=False):
    nc = bass.Bass("TRN2", target_bir_lowering=False, debug=False)

    dx = nc.dram_tensor("x", [T, D], F32, kind="ExternalInput").ap()
    d_ln_g = nc.dram_tensor("ln_g", [1, D], F32, kind="ExternalInput").ap()
    d_ln_b = nc.dram_tensor("ln_b", [1, D], F32, kind="ExternalInput").ap()
    d_Win = nc.dram_tensor("W_in", [D, F_IN], F32, kind="ExternalInput").ap()
    d_bin = nc.dram_tensor("b_in", [1, F_IN], F32, kind="ExternalInput").ap()
    d_Wout = nc.dram_tensor("W_out", [E, D], F32, kind="ExternalInput").ap()
    d_bout = nc.dram_tensor("b_out", [1, D], F32, kind="ExternalInput").ap()
    d_gq = nc.dram_tensor("gq", [S, 1], F32, kind="ExternalInput").ap()
    d_bq = nc.dram_tensor("bq", [S, 1], F32, kind="ExternalInput").ap()
    d_gk = nc.dram_tensor("gk", [S, 1], F32, kind="ExternalInput").ap()
    d_bk = nc.dram_tensor("bk", [S, 1], F32, kind="ExternalInput").ap()
    d_clip_x = nc.dram_tensor("clip_x", [1, 1], F32, kind="ExternalInput").ap()
    d_clip_d = nc.dram_tensor("clip_d", [1, 1], F32, kind="ExternalInput").ap()
    d_wg = nc.dram_tensor("w_gate", [I_FF, D], F32, kind="ExternalInput").ap()
    d_wu = nc.dram_tensor("w_up", [I_FF, D], F32, kind="ExternalInput").ap()
    d_wd = nc.dram_tensor("w_down", [D, I_FF], F32, kind="ExternalInput").ap()
    d_id = nc.dram_tensor("ident", [128, 128], F32, kind="ExternalInput").ap()
    d_out = nc.dram_tensor("out", [T, D], F32, kind="ExternalOutput").ap()
    d_dbg = nc.dram_tensor("dbg", [16 * 128, 2048], F32, kind="ExternalOutput").ap() if debug else None
    sDBG = [None]

    d_uavT = nc.dram_tensor("uavT", [T, E], BF16).ap()
    zg_in = nc.dram_tensor("zg_in", [S, T], BF16).ap()
    zg_out = nc.dram_tensor("zg_out", [S * NCORES, T], BF16, addr_space="Shared").ap()
    vg_in = nc.dram_tensor("vg_in", [T, E], BF16).ap()
    vg_outA = nc.dram_tensor("vg_outA", [L // 2, E], BF16, addr_space="Shared").ap()
    vg_outB = nc.dram_tensor("vg_outB", [L // 2, E], BF16, addr_space="Shared").ap()
    u_hbm = nc.dram_tensor("u_hbm", [T, E], BF16).ap()
    uav_hbm = nc.dram_tensor("uav_hbm", [T, E], F32).ap()
    x2_hbm = nc.dram_tensor("x2_hbm", [T, D], F32).ap()
    vh_hbm = nc.dram_tensor("vh_hbm", [I_FF, T], F32).ap()
    mx_in = nc.dram_tensor("mx_in", [128, 1], F32).ap()
    mx_out = nc.dram_tensor("mx_out", [128, 1], F32).ap()
    mh_in = nc.dram_tensor("mh_in", [128, 1], F32).ap()
    mh_out = nc.dram_tensor("mh_out", [128, 1], F32).ap()
    sc_hbm = nc.dram_tensor("sc_hbm", [1, 8], F32).ap()
    sd_hbm = nc.dram_tensor("sd_hbm", [1, D], F32).ap()

    RG = [list(range(NCORES))]
    SY, VE, SC, TE, GP = Rec("sync"), Rec("vector"), Rec("scalar"), Rec("tensor"), Rec("gpsimd")
    RECS = [SY, VE, SC, TE, GP]

    with ExitStack() as st0:
        def sem(name):
            return Sem(st0, nc, name)

        def sb(stk, name, shape, dtype):
            return stk.enter_context(nc.sbuf_tensor(name, shape, dtype))

        sT = sem("sT"); sV = sem("sV"); sS = sem("sS"); sCC = sem("sCC")
        sBC = sem("sBC")
        sXT = [sem("sXT0"), sem("sXT1"), sem("sXT2")]
        sWI = sem("sWI")
        sTRXN = sem("sTRXN")
        sUAVT = [sem("sUAVT0"), sem("sUAVT1")]
        sTRXQ = sem("sTRXQ")
        sTRKW = [sem("sTRKW0"), sem("sTRKW1")]
        sTRKD = sem("sTRKD")
        sSTZ = sem("sSTZ")
        sSTV = sem("sSTV")
        sSTU = sem("sSTU")
        sX2ST = [sem("sX2ST0"), sem("sX2ST1")]
        sSD = sem("sSD")
        sVL = [sem("sVL0"), sem("sVL1")]
        sGZ = sem("sGZ")
        sUL = [sem("sUL0"), sem("sUL1")]
        sWO = [sem("sWO0"), sem("sWO1")]
        sUT = [sem("sUT0"), sem("sUT1"), sem("sUT2")]
        sUL3 = [sem("sUL3_0"), sem("sUL3_1"), sem("sUL3_2")]
        sXR = [sem("sXR0"), sem("sXR1"), sem("sXR2")]
        sXQ = [sem("sXQ0"), sem("sXQ1"), sem("sXQ2")]
        sWS = [sem(f"sWS{i}") for i in range(6)]
        sWD = sem("sWD")
        sVH = [sem("sVH0"), sem("sVH1"), sem("sVH2")]
        sVHL = [sem("sVHL0"), sem("sVHL1"), sem("sVHL2"), sem("sVHL3")]
        sX2R = [sem("sX2R0"), sem("sX2R1"), sem("sX2R2")]
        sSCB = sem("sSCB")
        sOUTs = [sem("sOUT0"), sem("sOUT1"), sem("sOUT2")]
        sDBG[0] = sem("sDBG")
        ALL_SEMS = [sT, sV, sS, sCC, sBC, sWI, sTRXN, sTRXQ, sTRKD, sSTZ, sSTV,
                    sSTU, sSD, sGZ, sSCB, sWD] + sXT + sUAVT + sTRKW + sX2ST + \
                   sVL + sUL + sWO + sUT + sUL3 + sXR + sXQ + sWS + sVH + sVHL + sX2R + sOUTs + [sDBG[0]]

        ps_all = st0.enter_context(nc.psum_tensor("ps_all", [128, 4096], F32))
        ps = [ps_all[0:128, 512 * b:512 * (b + 1)] for b in range(8)]
        bankfree = {}

        def te_bank(b):
            if b in bankfree:
                TE.wait(*bankfree[b])

        ident = sb(st0, "ident_sb", [128, 128], F32)
        identb = sb(st0, "identb", [128, 128], BF16)
        cols2 = sb(st0, "cols2", [128, 48], F32)
        tr_state = {"n": 0}
        stats = sb(st0, "stats", [128, 64], F32)
        cols = sb(st0, "cols", [128, 16], F32)
        one1 = sb(st0, "one1", [1, 16], F32)
        row128 = sb(st0, "row128", [1, 128], F32)
        scrS = sb(st0, "scrS", [128, 1024], F32)

        C_GQ, C_BQ, C_GK, C_BK, C_BZ, C_MX, C_MH, C_RSX, C_RSH, C_SX, C_SH = range(11)
        O_CLX, O_CLD, O_M, O_SIG, O_SX, O_RSX, O_T0, O_SH, O_RSH = range(9)

        def dma(rec, out_ap, in_ap, semx, transpose=False):
            rec.op(lambda e, o=out_ap, i=in_ap, tr=transpose:
                   e.dma_start(out=o, in_=i, transpose=tr), semx, 16)
            return semx.n

        def fill(n, bank=0):
            te_bank(bank)
            for _ in range(n):
                TE.op(lambda e, b=bank:
                      e.matmul(ps_all[0:128, 512 * b:512 * b + 128],
                               ident[:], ident[:], start=True, stop=True), None)

        def dbg_tap(k, src_ap, width, wait_sem=None, wait_val=None):
            if d_dbg is None:
                return
            if wait_sem is not None:
                GP.wait(wait_sem, wait_val)
            dma(GP, d_dbg[128 * k:128 * (k + 1), 0:width], src_ap, sDBG[0])

        def te_transpose(dst_bf, src_f32, nblk, src_step=128, dst_step=128, banks=(6, 7),
                         idn=None):
            m = None
            idn = ident if idn is None else idn
            isbf = (idn is not ident)
            for j in range(nblk):
                b = banks[tr_state["n"] % len(banks)]
                tr_state["n"] += 1
                te_bank(b)
                if isbf:
                    pview = ps_all[0:128, 512 * b:512 * b + 64].bitcast(BF16)
                else:
                    pview = ps_all[0:128, 512 * b:512 * b + 128]
                TE.op(lambda e, jj=j, s=src_f32, st_=src_step, id2=idn, pv=pview:
                      e.transpose(pv, s[0:128, st_ * jj:st_ * jj + 128], id2[:]), sT)
                mt = sT.n
                SC.wait(sT, mt)
                SC.wait(sS, sS.n)
                m = SC.op(lambda e, jj=j, d=dst_bf, st_=dst_step, pv=pview:
                          e.copy(out=d[0:128, st_ * jj:st_ * jj + 128], in_=pv), sS)
                bankfree[b] = (sS, m)
            return m

        def barrier(exclude=()):
            snap = [(s, s.n) for s in ALL_SEMS if s.n > 0 and s not in exclude]
            for rec in RECS:
                for s, v in snap:
                    rec.wait(s, v)

        def ln_rstd(src_col, dst_col):
            t0, t1, t2, t3 = (stats[:, c:c + 1] for c in (60, 61, 62, 63))
            VE.wait(sV, sV.n)
            mvar = VE.op(lambda e, s=src_col: e.tensor_copy(out=t0, in_=s), sV)
            SC.wait(sV, mvar)
            SC.wait(sS, sS.n)
            msq = SC.op(lambda e: e.activation(out=t1, in_=t0, func=ACT.Sqrt), sS)
            VE.wait(sS, msq)
            VE.op(lambda e: e.reciprocal(out=t2, in_=t1), sV)
            VE.wait(sV, sV.n)
            VE.op(lambda e: e.tensor_tensor(out=t3, in0=t2, in1=t2, op=OP.mult), sV)
            VE.wait(sV, sV.n)
            VE.op(lambda e, s=src_col: e.tensor_tensor(out=t1, in0=t3, in1=s, op=OP.mult), sV)
            VE.wait(sV, sV.n)
            VE.op(lambda e: e.tensor_scalar(out=t1, in0=t1, scalar1=3.0, scalar2=-0.5,
                                            op0=OP.subtract, op1=OP.mult), sV)
            VE.wait(sV, sV.n)
            return VE.op(lambda e, d=dst_col: e.tensor_tensor(out=d, in0=t2, in1=t1, op=OP.mult), sV)

        def ln_stats(a_ap, mu_col, rstd_col):
            s0, s1, s2, s3 = (stats[:, c:c + 1] for c in (8, 9, 10, 11))
            VE.wait(sV, sV.n)
            VE.op(lambda e, a=a_ap: e.reduce_sum(out=s0, in_=a, axis=AX), sV)
            SC.wait(sV, sV.n)
            SC.wait(sS, sS.n)
            msq = SC.op(lambda e, a=a_ap: e.activation(out=scrS[:], in_=a, func=ACT.Square,
                                                       accum_out=s1), sS)
            VE.wait(sV, sV.n)
            VE.op(lambda e, m=mu_col: e.tensor_scalar_mul(m, s0, 1.0 / D), sV)
            VE.wait(sS, msq)
            VE.wait(sV, sV.n)
            VE.op(lambda e: e.tensor_scalar_mul(s2, s1, 1.0 / D), sV)
            VE.wait(sV, sV.n)
            VE.op(lambda e, m=mu_col: e.tensor_tensor(out=s3, in0=m, in1=m, op=OP.mult), sV)
            VE.wait(sV, sV.n)
            VE.op(lambda e: e.tensor_scalar(out=s2, in0=s2, scalar1=s3, scalar2=LN_EPS,
                                            op0=OP.subtract, op1=OP.add), sV)
            return ln_rstd(s2, rstd_col)

        def recip_nr(src, dst):
            t = one1[0:1, O_T0:O_T0 + 1]
            VE.wait(sV, sV.n)
            VE.op(lambda e, s=src, d=dst: e.reciprocal(out=d, in_=s), sV)
            VE.wait(sV, sV.n)
            VE.op(lambda e, s=src, d=dst: e.tensor_tensor(out=t, in0=s, in1=d, op=OP.mult), sV)
            VE.wait(sV, sV.n)
            VE.op(lambda e: e.tensor_scalar(out=t, in0=t, scalar1=2.0, scalar2=-1.0,
                                            op0=OP.subtract, op1=OP.mult), sV)
            VE.wait(sV, sV.n)
            return VE.op(lambda e, d=dst: e.tensor_tensor(out=d, in0=d, in1=t, op=OP.mult), sV)

        def rw_from_max(mx_col, rw_col):
            r0, tt = stats[:, 17:18], stats[:, 18:19]
            VE.wait(sV, sV.n)
            VE.op(lambda e, m=mx_col: e.reciprocal(out=r0, in_=m), sV)
            VE.wait(sV, sV.n)
            VE.op(lambda e, m=mx_col: e.tensor_tensor(out=tt, in0=m, in1=r0, op=OP.mult), sV)
            VE.wait(sV, sV.n)
            VE.op(lambda e: e.tensor_scalar(out=tt, in0=tt, scalar1=2.0, scalar2=-1.0,
                                            op0=OP.subtract, op1=OP.mult), sV)
            VE.wait(sV, sV.n)
            VE.op(lambda e: e.tensor_tensor(out=r0, in0=r0, in1=tt, op=OP.mult), sV)
            VE.wait(sV, sV.n)
            return VE.op(lambda e, d=rw_col: e.tensor_scalar_mul(d, r0, 127.0), sV)

        def quant(in_f32, rcol, itile, f32tmp, out_bf, pre_sc_waits=(), pre_ve_waits=()):
            for w_ in pre_ve_waits:
                VE.wait(*w_)
            VE.wait(sV, sV.n)
            m1 = VE.op(lambda e, s=in_f32, d=f32tmp, r=rcol:
                       e.tensor_scalar(out=d, in0=s, scalar1=r, scalar2=-128.0,
                                       op0=OP.mult, op1=OP.max), sV)
            VE.wait(sV, sV.n)
            m2 = VE.op(lambda e, s=f32tmp, d=itile:
                       e.tensor_scalar(out=d, in0=s, scalar1=127.0, scalar2=None,
                                       op0=OP.min), sV)
            for w_ in pre_sc_waits:
                SC.wait(*w_)
            SC.wait(sV, m2)
            SC.wait(sS, sS.n)
            return SC.op(lambda e, s=itile, d=out_bf: e.copy(out=d, in_=s), sS)

        stA = ExitStack()
        scr1 = sb(stA, "scr1", [128, 1024], F32)
        q_fm = sb(stA, "q_fm", [128, T], BF16)
        k_fm = sb(stA, "k_fm", [128, L], BF16)

        stP12 = ExitStack()
        xt = [sb(stP12, f"xt{i}", [128, D], F32) for i in range(3)]
        xnb16 = [sb(stP12, f"xnb16_{i}", [128, D], BF16) for i in range(2)]
        lng_bc = sb(stP12, "lng_bc", [128, D], F32)
        lnb_bc = sb(stP12, "lnb_bc", [128, D], F32)
        xn_fm = sb(stP12, "xn_fm", [128, 8 * T], BF16)
        wi_st = sb(stP12, "wi_st", [128, F_IN], F32)
        wi_bf = sb(stP12, "wi_bf", [128, 8 * F_IN], BF16)
        z_bf = sb(stP12, "z_bf", [128, T], BF16)
        uv_sub = [sb(stP12, f"uvsub{i}", [128, 512], BF16) for i in range(2)]
        zch = sb(stP12, "zch", [128, T], BF16)
        ones_r = sb(stP12, "ones_r", [1, T], BF16)
        binr = sb(stP12, "binr", [1, F_IN], BF16)

        dma(SY, lng_bc[:], d_ln_g.partition_broadcast(128), sBC)
        dma(SY, lnb_bc[:], d_ln_b.partition_broadcast(128), sBC)
        dma(GP, binr[0:1, :], d_bin, sBC)
        dma(SY, cols[:, C_GQ:C_GQ + 1], d_gq, sBC)
        dma(SY, cols[:, C_BQ:C_BQ + 1], d_bq, sBC)
        dma(SY, cols[:, C_GK:C_GK + 1], d_gk, sBC)
        dma(SY, cols[:, C_BK:C_BK + 1], d_bk, sBC)
        dma(SY, cols[:, C_BZ:C_BZ + 1], d_bin[0:1, 2 * E:F_IN].rearrange("o p -> p o"), sBC)
        dma(SY, ident[:], d_id, sBC)
        dma(SY, one1[0:1, O_CLX:O_CLX + 1], d_clip_x, sBC)
        dma(SY, one1[0:1, O_CLD:O_CLD + 1], d_clip_d, sBC)
        m_bc = sBC.n
        VE.wait(sV, sV.n)
        m_ones = VE.op(lambda e: e.memset(ones_r[0:1, :], 1.0), sV)
        SC.wait(sBC, m_bc)
        SC.wait(sS, sS.n)
        m_idb = SC.op(lambda e: e.copy(out=identb[:], in_=ident[:]), sS)
        TE.wait(sBC, m_bc)
        TE.wait(sS, m_idb)

        m_sumA = {}
        m_sqA = {}
        m_wicv = {}
        dma(GP, wi_st[:], d_Win[0:128, :], sWI)
        for i in range(2):
            dma(SY, xt[i][:], dx[128 * i:128 * (i + 1), :], sXT[i])
        for i in range(NT):
            sl = i % 3
            if i + 2 < NT:
                s2 = (i + 2) % 3
                if i >= 1:
                    SY.wait(sV, m_sumA[i - 1])
                    SY.wait(sS, m_sqA[i - 1])
                dma(SY, xt[s2][:], dx[128 * (i + 2):128 * (i + 3), :], sXT[s2])
            VE.wait(sXT[sl], sXT[sl].n)
            VE.wait(sV, sV.n)
            m_sumA[i] = VE.op(lambda e, s=xt[sl], ii=i:
                              e.reduce_sum(out=stats[:, ii:ii + 1], in_=s[:], axis=AX), sV)
            SC.wait(sXT[sl], sXT[sl].n)
            SC.wait(sS, sS.n)
            m_sqA[i] = SC.op(lambda e, s=xt[sl], ii=i:
                             e.activation(out=scrS[:], in_=s[:], func=ACT.Square,
                                          accum_out=stats[:, 16 + ii:17 + ii]), sS)
            if i % 2 == 1 and i // 2 < 8:
                d_ = i // 2
                SC.wait(sWI, sWI.n)
                SC.wait(sS, sS.n)
                m_wicv[d_] = SC.op(lambda e, dd=d_:
                                   e.copy(out=wi_bf[0:128, F_IN * dd:F_IN * (dd + 1)], in_=wi_st[:]), sS)
                if d_ + 1 < 8:
                    GP.wait(sS, m_wicv[d_])
                    dma(GP, wi_st[:], d_Win[128 * (d_ + 1):128 * (d_ + 2), :], sWI)
            TE.wait(sS, m_sqA[i])
            fill(14, bank=0)
        VE.wait(sS, m_sqA[NT - 1])
        VE.wait(sV, sV.n)
        VE.op(lambda e: e.tensor_scalar_mul(stats[:, 32:48], stats[:, 0:16], 1.0 / D), sV)
        VE.wait(sV, sV.n)
        VE.op(lambda e: e.tensor_scalar_mul(stats[:, 16:32], stats[:, 16:32], 1.0 / D), sV)
        VE.wait(sV, sV.n)
        VE.op(lambda e: e.tensor_tensor(out=stats[:, 0:16], in0=stats[:, 32:48], in1=stats[:, 32:48], op=OP.mult), sV)
        VE.wait(sV, sV.n)
        VE.op(lambda e: e.tensor_tensor(out=stats[:, 16:32], in0=stats[:, 16:32], in1=stats[:, 0:16], op=OP.subtract), sV)
        VE.wait(sV, sV.n)
        mvB = VE.op(lambda e: e.tensor_scalar_add(stats[:, 16:32], stats[:, 16:32], LN_EPS), sV)
        SC.wait(sV, mvB)
        SC.wait(sS, sS.n)
        msB = SC.op(lambda e: e.activation(out=stats[:, 0:16], in_=stats[:, 16:32], func=ACT.Sqrt), sS)
        VE.wait(sS, msB)
        VE.op(lambda e: e.reciprocal(out=stats[:, 48:64], in_=stats[:, 0:16]), sV)
        VE.wait(sV, sV.n)
        VE.op(lambda e: e.tensor_tensor(out=stats[:, 0:16], in0=stats[:, 48:64], in1=stats[:, 48:64], op=OP.mult), sV)
        VE.wait(sV, sV.n)
        VE.op(lambda e: e.tensor_tensor(out=stats[:, 0:16], in0=stats[:, 0:16], in1=stats[:, 16:32], op=OP.mult), sV)
        VE.wait(sV, sV.n)
        VE.op(lambda e: e.tensor_scalar(out=stats[:, 0:16], in0=stats[:, 0:16], scalar1=3.0, scalar2=-0.5,
                                        op0=OP.subtract, op1=OP.mult), sV)
        VE.wait(sV, sV.n)
        VE.op(lambda e: e.tensor_tensor(out=stats[:, 16:32], in0=stats[:, 48:64], in1=stats[:, 0:16], op=OP.mult), sV)
        m_rstdB = sV.n
        m_xn = {}
        m_scr1 = {}
        m_xntr = {}
        m_xntr_te = {}
        for i in range(2):
            SY.wait(sV, m_sumA[NT - 1])
            SY.wait(sS, m_sqA[NT - 1])
            dma(SY, xt[i][:], dx[128 * i:128 * (i + 1), :], sXT[i])
        for i in range(NT):
            sl = i % 3
            if i + 2 < NT:
                s2 = (i + 2) % 3
                if i >= 1:
                    SY.wait(sV, m_scr1[i - 1])
                SY.wait(sS, m_sqA[NT - 1])
                dma(SY, xt[s2][:], dx[128 * (i + 2):128 * (i + 3), :], sXT[s2])
            VE.wait(sXT[sl], sXT[sl].n)
            VE.wait(sBC, m_bc)
            VE.wait(sV, max(sV.n, m_rstdB))
            m_scr1[i] = VE.op(lambda e, s=xt[sl], ii=i:
                              e.scalar_tensor_tensor(out=scr1[:], in0=s[:], scalar=stats[:, 32 + ii:33 + ii],
                                                     in1=lng_bc[:], op0=OP.subtract, op1=OP.mult), sV)
            VE.wait(sV, sV.n)
            if i >= 2:
                VE.wait(sT, m_xntr_te[i - 2])
            xnbuf = xnb16[i % 2]
            m_xn[i] = VE.op(lambda e, d=xnbuf, ii=i:
                            e.scalar_tensor_tensor(out=d[:], in0=scr1[:], scalar=stats[:, 16 + ii:17 + ii],
                                                   in1=lnb_bc[:], op0=OP.mult, op1=OP.add), sV)
            TE.wait(sV, m_xn[i])
            m_xntr[i] = te_transpose(xn_fm[:, 128 * i:], xnbuf[:], 8,
                                     src_step=128, dst_step=T, banks=(0, 1, 2, 3, 4, 5, 6, 7),
                                     idn=identb)
            m_xntr_te[i] = sT.n
            fill(6, bank=(i % 2) * 4)
        m_xn_tr = m_xntr[NT - 1]
        dbg_tap(0, xnb16[1][:], D, sV, m_xn[NT - 1])

        m_zev = {}
        for n in range(4):
            TE.wait(sS, m_xntr[4 * n + 3])
            TE.wait(sS, m_wicv[7])
            te_bank(n)
            for d_ in range(8):
                TE.op(lambda e, dd=d_, nn=n:
                      e.matmul(ps[nn],
                               wi_bf[0:128, F_IN * dd + 2 * E:F_IN * dd + 2 * E + 128],
                               xn_fm[0:128, dd * T + 512 * nn:dd * T + 512 * (nn + 1)],
                               start=(dd == 0), stop=(dd == 7)),
                      sT if d_ == 7 else None)
            m_zmm = sT.n
            SC.wait(sT, m_zmm)
            SC.wait(sBC, m_bc)
            SC.wait(sS, sS.n)
            m_zev[n] = SC.op(lambda e, nn=n:
                             e.activation(out=z_bf[:, 512 * nn:512 * (nn + 1)], in_=ps[nn],
                                          func=ACT.Silu, bias=cols[:, C_BZ:C_BZ + 1]), sS)
            bankfree[n] = (sS, m_zev[n])
        VE.wait(sS, m_zev[3])
        VE.wait(sBC, m_bc)
        VE.wait(sV, sV.n)
        m_q = VE.op(lambda e: e.tensor_scalar(out=q_fm[:], in0=z_bf[:], scalar1=cols[:, C_GQ:C_GQ + 1],
                                              scalar2=cols[:, C_BQ:C_BQ + 1], op0=OP.mult, op1=OP.add), sV)
        SY.wait(sS, m_zev[3])
        dma(SY, zg_in, z_bf[:], sSTZ)
        dbg_tap(1, z_bf[:], T, sS, m_zev[3])
        dbg_tap(2, q_fm[:], T, sV, m_q)

        GP.wait(sSTZ, sSTZ.n)
        GP.op(lambda e: e.collective_compute("AllGather", OP.bypass, replica_groups=RG,
                                             ins=[zg_in], outs=[zg_out]), sCC)
        GP.wait(sCC, sCC.n)

        ev = 0
        m_uvadd = {}
        m_uvev = {}
        m_substore = {}
        for w in range(2):
            col0 = E if w == 0 else 0
            for t in range(NT):
                if w == 0 and t == NT // 2:
                    GP.wait(sSTV, sSTV.n)
                    GP.op(lambda e: e.collective_compute("AllGather", OP.bypass, replica_groups=RG,
                                                         ins=[vg_in[0:T // 2, :]], outs=[vg_outA]), sCC)
                    m_vgA = sCC.n
                for n in range(4):
                    b = ev % 8
                    te_bank(b)
                    if ev == 0:
                        TE.wait(sV, m_ones)
                        TE.wait(sBC, m_bc)
                    for d_ in range(9):
                        if d_ < 8:
                            TE.op(lambda e, dd=d_, tt=t, c0=col0, nn=n, bb=b:
                                  e.matmul(ps[bb],
                                           xn_fm[0:128, dd * T + 128 * tt:dd * T + 128 * (tt + 1)],
                                           wi_bf[0:128, F_IN * dd + c0 + 512 * nn:F_IN * dd + c0 + 512 * (nn + 1)],
                                           start=(dd == 0), stop=False),
                                  None)
                        else:
                            TE.op(lambda e, tt=t, c0=col0, nn=n, bb=b:
                                  e.matmul(ps[bb],
                                           ones_r[0:1, 128 * tt:128 * (tt + 1)],
                                           binr[0:1, c0 + 512 * nn:c0 + 512 * (nn + 1)],
                                           start=False, stop=True),
                                  sT)
                    m_mm = sT.n
                    SC.wait(sT, m_mm)
                    if ev >= 2:
                        SC.wait(*m_substore[ev - 2])
                    SC.wait(sS, sS.n)
                    m_uvev[ev] = SC.op(
                        lambda e, esl=ev % 2, bb=b:
                        e.activation(out=uv_sub[esl][:], in_=ps[bb], func=ACT.Silu), sS)
                    bankfree[b] = (sS, m_uvev[ev])
                    SY.wait(sS, m_uvev[ev])
                    dst, stx = (vg_in, sSTV) if w == 0 else (u_hbm, sSTU)
                    dma(SY, dst[128 * t:128 * (t + 1), 512 * n:512 * (n + 1)], uv_sub[ev % 2][:], stx)
                    m_substore[ev] = (stx, stx.n)
                    ev += 1
            if w == 0:
                GP.wait(sSTV, sSTV.n)
                GP.op(lambda e: e.collective_compute("AllGather", OP.bypass, replica_groups=RG,
                                                     ins=[vg_in[T // 2:T, :]], outs=[vg_outB]), sCC)
                m_vg = sCC.n

        m_kch = {}
        for r in range(NCORES):
            if r >= 1:
                GP.wait(sV, m_kch[r - 1])
            dma(GP, zch[:], zg_out[S * r:S * (r + 1), :], sGZ)
            VE.wait(sGZ, sGZ.n)
            VE.wait(sBC, m_bc)
            VE.wait(sV, sV.n)
            m_kch[r] = VE.op(lambda e, rr=r:
                             e.tensor_scalar(out=k_fm[:, T * rr:T * (rr + 1)], in0=zch[:],
                                             scalar1=cols[:, C_GK:C_GK + 1],
                                             scalar2=cols[:, C_BK:C_BK + 1],
                                             op0=OP.mult, op1=OP.add), sV)
            TE.wait(sV, m_kch[r])
            fill(14, bank=0)
        m_k = m_kch[NCORES - 1]
        dbg_tap(3, k_fm[:, 0:T], T, sV, m_k)
        fill(70, bank=0)
        TE.wait(sCC, m_vgA)
        fill(70, bank=0)

        stP12.close()
        barrier(exclude=(sCC,))
        stP3 = ExitStack()
        v_sup = [sb(stP3, f"vsup{i}", [128, 4 * E], BF16) for i in range(4)]
        sc_sb = [sb(stP3, f"scsb{i}", [128, 4 * 512], BF16) for i in range(4)]
        rl_tmp = sb(stP3, "rltmp", [128, 512], F32)
        av_acc = sb(stP3, "av_acc", [128, 4 * E], F32)
        u_t = [sb(stP3, f"ut{i}", [128, E], BF16) for i in range(3)]
        uav_f = [sb(stP3, f"uavf{i}", [128, E], BF16) for i in range(2)]
        uavT_st = [sb(stP3, f"uavT{i}", [128, E], BF16) for i in range(2)]

        NPAIR = NSUP // 2
        m_avpair = {}
        m_sq = {}
        m_vld = {}
        m_uav = {}
        grp_box = [0]

        def emit_scores_kb(p, s_, kb):
            sl4 = s_ % 4
            idx = p * NSUP + s_
            if kb == 0:
                if idx >= 4:
                    pp, ss = (p, s_ - 4) if s_ >= 4 else (p - 1, NSUP - 4 + s_)
                    GP.wait(sT, m_avpair[(pp, ss // 2)])
                r_, o_ = divmod(KS * s_, T)
                if o_ < T // 2:
                    vsrc = vg_outA[(T // 2) * r_ + o_:(T // 2) * r_ + o_ + KS, :]
                    GP.wait(sCC, m_vgA)
                else:
                    vsrc = vg_outB[(T // 2) * r_ + o_ - T // 2:(T // 2) * r_ + o_ - T // 2 + KS, :]
                    GP.wait(sCC, m_vg)
                dma(GP, v_sup[s_ % 4][:], vsrc, sVL[s_ % 2])
                m_vld[(p, s_)] = sVL[s_ % 2].n
            if idx == 0 and kb == 0:
                TE.wait(sV, m_k)
            b = kb % 2
            te_bank(b)
            TE.op(lambda e, pp=p, ss=s_, kk=kb, bb=b:
                  e.matmul(ps[bb],
                           k_fm[0:128, KS * ss + 128 * kk:KS * ss + 128 * (kk + 1)],
                           q_fm[0:128, QP * pp:QP * (pp + 1)],
                           start=True, stop=True), sT)
            m_scmm = sT.n
            SC.wait(sT, m_scmm)
            SC.wait(sS, sS.n)
            m_rl = SC.op(lambda e, bb=b:
                         e.activation(out=rl_tmp[:], in_=ps[bb], func=ACT.Relu, scale=INV_L), sS)
            bankfree[b] = (sS, m_rl)
            if kb == 0 and idx >= 4:
                pp, ss = (p, s_ - 4) if s_ >= 4 else (p - 1, NSUP - 4 + s_)
                SC.wait(sT, m_avpair[(pp, ss // 2)])
            SC.wait(sS, sS.n)
            m_sq[(p, s_, kb)] = SC.op(
                lambda e, so=sl4, kk=kb:
                e.activation(out=sc_sb[so][0:128, 512 * kk:512 * (kk + 1)],
                             in_=rl_tmp[:], func=ACT.Square), sS)

        def emit_avpair(p, pr):
            s0, s1 = 2 * pr, 2 * pr + 1
            nxt = []
            if pr + 1 < NPAIR:
                nxt = [(2 * (pr + 1) + i // 4, i % 4) for i in range(8)]
            elif p + 1 < NPASS:
                nxt = [(0, None)] * 0
            TE.wait(sS, m_sq[(p, s1, 3)])
            TE.wait(sVL[s0 % 2], m_vld[(p, s0)])
            TE.wait(sVL[s1 % 2], m_vld[(p, s1)])
            gi = 0
            for qb in range(4):
                for g in range(2):
                    r = grp_box[0] % 2
                    grp_box[0] += 1
                    b0, b1 = 2 + 2 * r, 3 + 2 * r
                    te_bank(b0)
                    te_bank(b1)
                    for s_i in (s0, s1):
                        sl4 = s_i % 4
                        for kb in range(4):
                            for j in range(2):
                                TE.op(lambda e, so=sl4, kk=kb, qq=qb, gg=g, jj=j,
                                      bb=(b0 if j == 0 else b1), st=(s_i == s0 and kb == 0),
                                      sp=(s_i == s1 and kb == 3):
                                      e.matmul(ps[bb],
                                               sc_sb[so][0:128, 512 * kk + 128 * qq:512 * kk + 128 * (qq + 1)],
                                               v_sup[s_i % 4][0:128, E * kk + 1024 * gg + 512 * jj:
                                                              E * kk + 1024 * gg + 512 * (jj + 1)],
                                               start=st, stop=sp),
                                      sT if (s_i == s1 and kb == 3 and j == 1) else None)
                    m_grp = sT.n
                    if gi < len(nxt):
                        emit_scores_kb(p, nxt[gi][0], nxt[gi][1])
                        gi += 1
                    VE.wait(sT, m_grp)
                    VE.wait(sV, sV.n)
                    dsl = av_acc[:, E * qb + 1024 * g:E * qb + 1024 * (g + 1)]
                    pss = ps_all[0:128, 512 * b0:512 * b0 + 1024]
                    if pr == 0:
                        VE.op(lambda e, d=dsl, b_=pss: e.tensor_copy(out=d, in_=b_), sV)
                    else:
                        VE.op(lambda e, d=dsl, b_=pss: e.tensor_add(out=d, in0=b_, in1=d), sV)
                    bankfree[b0] = (sV, sV.n)
                    bankfree[b1] = (sV, sV.n)
            m_avpair[(p, pr)] = sT.n

        m_uavst = {}
        m_uavtrA = {}
        for p in range(NPASS):
            if p == 0:
                for kb in range(4):
                    emit_scores_kb(p, 0, kb)
                for kb in range(4):
                    emit_scores_kb(p, 1, kb)
            for pr in range(NPAIR):
                emit_avpair(p, pr)
            if p + 1 < NPASS:
                for kb in range(4):
                    emit_scores_kb(p + 1, 0, kb)
                for kb in range(4):
                    emit_scores_kb(p + 1, 1, kb)
            if p == 0:
                SY.wait(sSTU, sSTU.n)
                for g0 in range(2):
                    dma(SY, u_t[g0][:], u_hbm[128 * g0:128 * (g0 + 1), :], sUT[g0])
            for qb in range(4):
                sl = qb % 2
                gq_ = 4 * p + qb
                usl = gq_ % 3
                if gq_ + 2 < 4 * NPASS:
                    s2 = (gq_ + 2) % 3
                    if gq_ >= 1:
                        SY.wait(sV, m_uav[gq_ - 1])
                    dma(SY, u_t[s2][:], u_hbm[128 * (gq_ + 2):128 * (gq_ + 3), :], sUT[s2])
                VE.wait(sUT[usl], sUT[usl].n)
                if gq_ >= 2:
                    VE.wait(sT, m_uavtrA[gq_ - 2])
                VE.wait(sV, sV.n)
                m_uav[gq_] = VE.op(
                    lambda e, s2=sl, u2=usl, qq=qb:
                    e.tensor_mul(out=uav_f[s2][:], in0=u_t[u2][:], in1=av_acc[:, E * qq:E * (qq + 1)]), sV)
                TE.wait(sV, m_uav[gq_])
                if gq_ >= 2:
                    TE.wait(sUAVT[sl], sUAVT[sl].n)
                te_transpose(uavT_st[sl], uav_f[sl][:], 16, src_step=128, dst_step=128,
                             banks=(6, 7), idn=identb)
                m_uavtrA[gq_] = sT.n
                m_trevA = sS.n
                GP.wait(sS, m_trevA)
                m_uavst[gq_] = dma(GP, d_uavT[128 * gq_:128 * (gq_ + 1), :], uavT_st[sl][:],
                                   sUAVT[sl])

        stP3.close()
        stA.close()
        barrier()
        stW = ExitStack()
        w_st = [sb(stW, f"wst{i}", [128, D], F32) for i in range(6)]
        wmx = sb(stW, "wmx", [128, 96], F32)
        stP35 = ExitStack()
        uavld = [sb(stP35, f"uavld{i}", [128, E], BF16) for i in range(3)]
        Wout_bf = sb(stP35, "Wout_bf", [128, 16 * D], BF16)
        w35 = [sb(stP35, f"w35{i}", [128, D], F32) for i in range(2)]
        bout_bc = sb(stP35, "bout_bc", [128, D], F32)
        xr_t = [sb(stP35, f"xrt{i}", [128, D], F32) for i in range(3)]
        a1_t = [sb(stP35, f"a1t{i}", [128, D], F32) for i in range(2)]
        scr35 = sb(stP35, "scr35", [128, D], F32)

        fill(25, bank=0)
        dma(SY, bout_bc[:], d_bout.partition_broadcast(128), sBC)
        m_bout = sBC.n
        m_wocv = {}
        for j in range(16):
            wsl = j % 2
            rq = GP if wsl == 0 else SY
            if j >= 2:
                rq.wait(sS, m_wocv[j - 2])
            dma(rq, w35[wsl][:], d_Wout[128 * j:128 * (j + 1), :], sWO[wsl])
            SC.wait(sWO[wsl], sWO[wsl].n)
            SC.wait(sS, sS.n)
            m_wocv[j] = SC.op(lambda e, jj=j, s2=wsl:
                              e.copy(out=Wout_bf[0:128, D * jj:D * (jj + 1)], in_=w35[s2][:]), sS)
            TE.wait(sS, m_wocv[j])
            fill(10, bank=0)
        m_wo_all = m_wocv[15]

        m_uavtr = {}
        m_wo = {}
        m_sq2q = {}
        m_mxq = {}
        m_a1st = {}
        wo_box = [0]

        for t in range(2):
            dma(SY, uavld[t][:], d_uavT[128 * t:128 * (t + 1), :], sUL3[t])
            dma(GP, xr_t[t][:], dx[128 * t:128 * (t + 1), :], sXR[t])
        for t in range(NT):
            sl = t % 2
            ul = t % 3
            if t + 2 < NT:
                s2 = (t + 2) % 3
                if t >= 1:
                    SY.wait(sT, m_wo[t - 1])
                dma(SY, uavld[s2][:], d_uavT[128 * (t + 2):128 * (t + 3), :], sUL3[s2])
                if t >= 1:
                    GP.wait(sV, m_mxq[t - 1])
                dma(GP, xr_t[s2][:], dx[128 * (t + 2):128 * (t + 3), :], sXR[s2])

            TE.wait(sUL3[ul], sUL3[ul].n)
            for dn in range(2):
                b = wo_box[0] % 4
                wo_box[0] += 1
                te_bank(b)
                for j in range(16):
                    if t == 0 and dn == 0:
                        TE.wait(sS, m_wocv[j])
                    TE.op(lambda e, jj=j, dd=dn, bb=b, u2=ul:
                          e.matmul(ps[bb],
                                   uavld[u2][0:128, 128 * jj:128 * (jj + 1)],
                                   Wout_bf[0:128, D * jj + 512 * dd:D * jj + 512 * (dd + 1)],
                                   start=(jj == 0), stop=(jj == 15)),
                          sT if j == 15 else None)
                m_wodn = sT.n
                VE.wait(sT, m_wodn)
                VE.wait(sXR[ul], sXR[ul].n)
                VE.wait(sBC, m_bout)
                if t >= 2 and dn == 0:
                    VE.wait(sS, m_sq2q[t - 2])
                    VE.wait(sX2ST[sl], sX2ST[sl].n)
                VE.wait(sV, sV.n)
                VE.op(lambda e, bb=b, dd=dn, s2=sl:
                      e.tensor_add(out=a1_t[s2][:, 512 * dd:512 * (dd + 1)], in0=ps[bb],
                                   in1=bout_bc[:, 512 * dd:512 * (dd + 1)]), sV)
                VE.wait(sV, sV.n)
                mep = VE.op(lambda e, s2=sl, u2=ul, dd=dn:
                            e.scalar_tensor_tensor(out=a1_t[s2][:, 512 * dd:512 * (dd + 1)],
                                                   in0=xr_t[u2][:, 512 * dd:512 * (dd + 1)],
                                                   scalar=2.0, in1=a1_t[s2][:, 512 * dd:512 * (dd + 1)],
                                                   op0=OP.mult, op1=OP.add), sV)
                bankfree[b] = (sV, mep)
            m_wo[t] = sT.n
            m_a1 = sV.n
            VE.wait(sV, m_a1)
            VE.op(lambda e, gi=t, s2=sl: e.reduce_sum(out=stats[:, gi:gi + 1], in_=a1_t[s2][:], axis=AX), sV)
            SC.wait(sV, sV.n)
            SC.wait(sS, sS.n)
            m_sq2q[t] = SC.op(lambda e, gi=t, s2=sl:
                              e.activation(out=scrS[:], in_=a1_t[s2][:], func=ACT.Square,
                                           accum_out=stats[:, 16 + gi:17 + gi]), sS)
            VE.wait(sV, sV.n)
            VE.op(lambda e, gi=t: e.tensor_scalar_mul(stats[:, 60:61], stats[:, gi:gi + 1], 1.0 / D), sV)
            VE.wait(sV, sV.n)
            VE.op(lambda e, s2=sl: e.tensor_single_scalar(out=scr35[:], in_=a1_t[s2][:],
                                                          scalar=stats[:, 60:61], op=OP.subtract), sV)
            VE.wait(sV, sV.n)
            m_mxq[t] = VE.op(lambda e, gi=t:
                             e.reduce_max(out=stats[:, 32 + gi:33 + gi], in_=scr35[:], axis=AX,
                                          apply_absolute_value=True), sV)
            GP.wait(sV, m_mxq[t])
            GP.wait(sS, m_sq2q[t])
            m_a1st[t] = dma(GP, x2_hbm[128 * t:128 * (t + 1), :], a1_t[sl][:], sX2ST[sl])

        VE.wait(sS, sS.n)
        VE.wait(sV, sV.n)
        VE.op(lambda e: e.tensor_scalar_mul(cols2[:, 0:16], stats[:, 0:16], 1.0 / D), sV)
        VE.wait(sV, sV.n)
        VE.op(lambda e: e.tensor_scalar_mul(cols2[:, 32:48], stats[:, 16:32], 1.0 / D), sV)
        VE.wait(sV, sV.n)
        VE.op(lambda e: e.tensor_tensor(out=stats[:, 16:32], in0=cols2[:, 0:16], in1=cols2[:, 0:16], op=OP.mult), sV)
        VE.wait(sV, sV.n)
        VE.op(lambda e: e.tensor_tensor(out=cols2[:, 32:48], in0=cols2[:, 32:48], in1=stats[:, 16:32], op=OP.subtract), sV)
        VE.wait(sV, sV.n)
        mv2 = VE.op(lambda e: e.tensor_scalar_add(cols2[:, 32:48], cols2[:, 32:48], LN_EPS), sV)
        SC.wait(sV, mv2)
        SC.wait(sS, sS.n)
        ms2 = SC.op(lambda e: e.activation(out=stats[:, 16:32], in_=cols2[:, 32:48], func=ACT.Sqrt), sS)
        VE.wait(sS, ms2)
        VE.op(lambda e: e.reciprocal(out=cols2[:, 16:32], in_=stats[:, 16:32]), sV)
        VE.wait(sV, sV.n)
        VE.op(lambda e: e.tensor_tensor(out=stats[:, 16:32], in0=cols2[:, 16:32], in1=cols2[:, 16:32], op=OP.mult), sV)
        VE.wait(sV, sV.n)
        VE.op(lambda e: e.tensor_tensor(out=stats[:, 16:32], in0=stats[:, 16:32], in1=cols2[:, 32:48], op=OP.mult), sV)
        VE.wait(sV, sV.n)
        VE.op(lambda e: e.tensor_scalar(out=stats[:, 16:32], in0=stats[:, 16:32], scalar1=3.0, scalar2=-0.5,
                                        op0=OP.subtract, op1=OP.mult), sV)
        VE.wait(sV, sV.n)
        VE.op(lambda e: e.tensor_tensor(out=cols2[:, 16:32], in0=cols2[:, 16:32], in1=stats[:, 16:32], op=OP.mult), sV)
        VE.wait(sV, sV.n)
        VE.op(lambda e: e.tensor_tensor(out=stats[:, 32:48], in0=stats[:, 32:48], in1=cols2[:, 16:32], op=OP.mult), sV)
        VE.wait(sV, sV.n)
        VE.op(lambda e: e.tensor_reduce(out=cols[:, C_MX:C_MX + 1], in_=stats[:, 32:48], axis=AX, op=OP.max), sV)
        m_mx = sV.n

        stP35.close()
        barrier()
        stB1 = ExitStack()
        xq_fm = sb(stB1, "xq_fm", [128, 8 * T], BF16)
        kgT = sb(stB1, "kgT", [128, 8 * I_FF], BF16)
        kuT = sb(stB1, "kuT", [128, 8 * I_FF], BF16)
        qi_t = [sb(stB1, f"qit{i}", [128, D], I32) for i in range(2)]
        qf_t = [sb(stB1, f"qft{i}", [128, D], F32) for i in range(2)]
        qb16 = [sb(stB1, f"qb16_{i}", [128, D], BF16) for i in range(2)]
        x2q_t = [sb(stB1, f"x2q{i}", [128, D], F32) for i in range(3)]
        sgx = sb(stB1, "sgx", [128, NI], F32)
        sux = sb(stB1, "sux", [128, NI], F32)
        g_tmp = [sb(stB1, f"gtmp{i}", [128, 512], F32) for i in range(3)]
        up_tmp = [sb(stB1, f"uptmp{i}", [128, 512], F32) for i in range(3)]
        vh_t = [sb(stB1, f"vht{i}", [128, 512], F32) for i in range(3)]

        fill(230, bank=0)
        GP.wait(sV, m_mx)
        dma(GP, mx_in, cols[:, C_MX:C_MX + 1], sSCB)
        GP.wait(sSCB, sSCB.n)
        GP.op(lambda e: e.collective_compute("AllReduce", OP.max, replica_groups=RG,
                                             ins=[mx_in], outs=[mx_out]), sCC)
        m_cc_mx = sCC.n
        TE.wait(sCC, m_cc_mx)
        fill(40, bank=0)
        GP.wait(sCC, sCC.n)
        dma(GP, row128[0:1, :], mx_out.rearrange("p o -> o p"), sSCB)
        VE.wait(sSCB, sSCB.n)
        VE.wait(sV, sV.n)
        VE.op(lambda e: e.reduce_max(out=one1[0:1, O_M:O_M + 1], in_=row128[0:1, :], axis=AX), sV)
        SC.wait(sBC, m_bc)
        SC.wait(sS, sS.n)
        m_sig = SC.op(lambda e: e.activation(out=one1[0:1, O_SIG:O_SIG + 1],
                                             in_=one1[0:1, O_CLX:O_CLX + 1], func=ACT.Sigmoid), sS)
        VE.wait(sS, m_sig)
        VE.wait(sV, sV.n)
        VE.op(lambda e: e.tensor_tensor(out=one1[0:1, O_SX:O_SX + 1], in0=one1[0:1, O_SIG:O_SIG + 1],
                                        in1=one1[0:1, O_M:O_M + 1], op=OP.mult), sV)
        VE.wait(sV, sV.n)
        VE.op(lambda e: e.tensor_scalar_mul(one1[0:1, O_SX:O_SX + 1], one1[0:1, O_SX:O_SX + 1],
                                            1.0 / 127.0), sV)
        m_rsx = recip_nr(one1[0:1, O_SX:O_SX + 1], one1[0:1, O_RSX:O_RSX + 1])
        GP.wait(sV, m_rsx)
        dma(GP, sc_hbm[0:1, 0:2], one1[0:1, O_SX:O_RSX + 1], sSCB)
        GP.wait(sSCB, sSCB.n)
        dma(GP, cols[:, C_SX:C_SX + 1], sc_hbm[0:1, 0:1].partition_broadcast(128), sSCB)
        dma(GP, cols[:, C_RSX:C_RSX + 1], sc_hbm[0:1, 1:2].partition_broadcast(128), sSCB)
        m_sxc = sSCB.n
        VE.wait(sSCB, m_sxc)
        VE.wait(sV, sV.n)
        m_rs2 = VE.op(lambda e: e.tensor_scalar_mul(cols2[:, 32:48], cols2[:, 16:32],
                                                    cols[:, C_RSX:C_RSX + 1]), sV)

        m_kx = {}
        m_kxtr = {}
        m_kxtr_s = {}
        for i in range(2):
            dma(SY, x2q_t[i][:], x2_hbm[128 * i:128 * (i + 1), :], sXQ[i])
        for i in range(NT):
            sl = i % 2
            xs = i % 3
            if i + 2 < NT:
                s2 = (i + 2) % 3
                if i >= 1:
                    SY.wait(sV, m_kx[i - 1][0])
                dma(SY, x2q_t[s2][:], x2_hbm[128 * (i + 2):128 * (i + 3), :], sXQ[s2])
            VE.wait(sXQ[xs], sXQ[xs].n)
            VE.wait(sV, max(sV.n, m_rs2))
            if i >= 2:
                VE.wait(sT, m_kxtr[i - 2])
            mq1 = VE.op(lambda e, s2=sl, x3=xs, ii=i:
                        e.tensor_scalar(out=qf_t[s2][:], in0=x2q_t[x3][:],
                                        scalar1=cols2[:, ii:ii + 1], scalar2=cols2[:, 32 + ii:33 + ii],
                                        op0=OP.subtract, op1=OP.mult), sV)
            VE.wait(sV, sV.n)
            mq2 = VE.op(lambda e, s2=sl: e.tensor_scalar(out=qi_t[s2][:], in0=qf_t[s2][:],
                                                         scalar1=-128.0, scalar2=127.0,
                                                         op0=OP.max, op1=OP.min), sV)
            SC.wait(sV, mq2)
            if i >= 2:
                SC.wait(sT, m_kxtr[i - 2])
            SC.wait(sS, sS.n)
            mq3 = SC.op(lambda e, s2=sl: e.copy(out=qb16[s2][:], in_=qi_t[s2][:]), sS)
            m_kx[i] = (mq1, mq3)
            TE.wait(sS, mq3)
            te_transpose(xq_fm[:, 128 * i:], qb16[sl][:], 8, src_step=128, dst_step=T,
                         banks=(6, 7), idn=identb)
            m_kxtr[i] = sT.n
            m_kxtr_s[i] = sS.n
            fill(6, bank=0)
        m_xq_tr = m_kxtr_s[NT - 1]
        dbg_tap(10, xq_fm[:, 0:T], T, sS, m_xq_tr)

        m_kwq = {}
        m_kwtr = {}
        m_kwtr_s = {}
        m_gev = {}
        m_upev = {}
        m_vhmul = {}
        m_vhst = {}
        vhe_box = [0]

        m_kwq1 = {}

        def issue_kw_load(c):
            if c >= NI:
                return
            for wsel, dsrc in enumerate((d_wg, d_wu)):
                idx = 2 * c + wsel
                sl = idx % 6
                if idx >= 6:
                    SY.wait(sV, m_kwq1[idx - 6])
                dma(SY, w_st[sl][:], dsrc[128 * c:128 * (c + 1), :], sWS[sl])

        m_scl = {}

        def emit_kw_prep(c):
            for wsel, (dsrc, kT) in enumerate(((d_wg, kgT), (d_wu, kuT))):
                idx = 2 * c + wsel
                cx = wsel * NI + c
                sl = idx % 2
                ws4 = idx % 6
                scol = sgx if wsel == 0 else sux
                VE.wait(sWS[ws4], sWS[ws4].n)
                if idx == 0:
                    VE.wait(sSCB, m_sxc)
                VE.wait(sV, sV.n)
                VE.op(lambda e, w4=ws4, ix=cx:
                      e.reduce_max(out=wmx[:, ix:ix + 1], in_=w_st[w4][:],
                                   axis=AX, apply_absolute_value=True), sV)
                VE.wait(sV, sV.n)
                VE.op(lambda e, ix=cx: e.reciprocal(out=wmx[:, 44:45], in_=wmx[:, ix:ix + 1]), sV)
                VE.wait(sV, sV.n)
                VE.op(lambda e, ix=cx: e.tensor_tensor(out=wmx[:, 45:46], in0=wmx[:, ix:ix + 1],
                                                       in1=wmx[:, 44:45], op=OP.mult), sV)
                VE.wait(sV, sV.n)
                VE.op(lambda e: e.tensor_scalar(out=wmx[:, 45:46], in0=wmx[:, 45:46],
                                                scalar1=2.0, scalar2=-1.0,
                                                op0=OP.subtract, op1=OP.mult), sV)
                VE.wait(sV, sV.n)
                VE.op(lambda e: e.tensor_tensor(out=wmx[:, 44:45], in0=wmx[:, 44:45],
                                                in1=wmx[:, 45:46], op=OP.mult), sV)
                VE.wait(sV, sV.n)
                VE.op(lambda e, ix=cx: e.tensor_scalar_mul(wmx[:, 48 + ix:49 + ix],
                                                           wmx[:, 44:45], 127.0), sV)
                VE.wait(sV, sV.n)
                m_scl[idx] = VE.op(lambda e, sc2=scol, cc=c, ix=cx:
                                   e.tensor_scalar(out=sc2[:, cc:cc + 1], in0=wmx[:, ix:ix + 1],
                                                   scalar1=cols[:, C_SX:C_SX + 1],
                                                   scalar2=1.0 / 127.0,
                                                   op0=OP.mult, op1=OP.mult), sV)
                VE.wait(sV, sV.n)
                mm1 = VE.op(lambda e, s2=sl, w4=ws4, ix=cx:
                            e.tensor_scalar(out=qf_t[s2][:], in0=w_st[w4][:],
                                            scalar1=wmx[:, 48 + ix:49 + ix], scalar2=-128.0,
                                            op0=OP.mult, op1=OP.max), sV)
                m_kwq1[idx] = mm1
                if idx >= 2:
                    VE.wait(sS, m_kwq[idx - 2])
                else:
                    VE.wait(sS, m_kx[NT - 2 + idx][1])
                VE.wait(sV, sV.n)
                mm2 = VE.op(lambda e, s2=sl: e.tensor_scalar(out=qi_t[s2][:], in0=qf_t[s2][:],
                                                             scalar1=127.0, scalar2=None,
                                                             op0=OP.min), sV)
                SC.wait(sV, mm2)
                SC.wait(sT, m_kwtr[idx - 2] if idx >= 2 else m_kxtr[NT - 2 + idx])
                SC.wait(sS, sS.n)
                m_kwq[idx] = SC.op(lambda e, s2=sl: e.copy(out=qb16[s2][:], in_=qi_t[s2][:]), sS)
                TE.wait(sS, m_kwq[idx])
                te_transpose(kT[:, 128 * c:], qb16[sl][:], 8, src_step=128, dst_step=I_FF,
                             banks=(6, 7), idn=identb)
                m_kwtr[idx] = sT.n
                m_kwtr_s[idx] = sS.n

        def emit_gateup_mm(c):
            TE.wait(sS, m_kwtr_s[2 * c + 1])
            for n in range(4):
                vhe = vhe_box[0]
                if c == 0:
                    TE.wait(sS, m_kxtr_s[4 * n + 3])
                bg = vhe % 3
                bu = 3 + vhe % 3
                te_bank(bg)
                for d_ in range(8):
                    TE.op(lambda e, dd=d_, cc=c, nn=n, bb=bg:
                          e.matmul(ps[bb],
                                   kgT[0:128, I_FF * dd + 128 * cc:I_FF * dd + 128 * (cc + 1)],
                                   xq_fm[0:128, dd * T + 512 * nn:dd * T + 512 * (nn + 1)],
                                   start=(dd == 0), stop=(dd == 7)),
                          sT if d_ == 7 else None)
                m_gmm = sT.n
                te_bank(bu)
                for d_ in range(8):
                    TE.op(lambda e, dd=d_, cc=c, nn=n, bb=bu:
                          e.matmul(ps[bb],
                                   kuT[0:128, I_FF * dd + 128 * cc:I_FF * dd + 128 * (cc + 1)],
                                   xq_fm[0:128, dd * T + 512 * nn:dd * T + 512 * (nn + 1)],
                                   start=(dd == 0), stop=(dd == 7)),
                          sT if d_ == 7 else None)
                m_umm = sT.n
                SC.wait(sT, m_gmm)
                if n == 0:
                    SC.wait(sV, m_scl[2 * c])
                if vhe >= 3:
                    SC.wait(sV, m_vhmul[vhe - 3])
                SC.wait(sS, sS.n)
                m_gev[vhe] = SC.op(lambda e, bb=bg, s2=vhe % 3, cc=c:
                                   e.activation(out=g_tmp[s2][:], in_=ps[bb], func=ACT.Silu,
                                                scale=sgx[:, cc:cc + 1]), sS)
                bankfree[bg] = (sS, m_gev[vhe])
                SC.wait(sT, m_umm)
                if n == 0:
                    SC.wait(sV, m_scl[2 * c + 1])
                if vhe >= 3:
                    SC.wait(sV, m_vhmul[vhe - 3])
                SC.wait(sS, sS.n)
                m_upev[vhe] = SC.op(lambda e, bb=bu, s2=vhe % 3, cc=c:
                                    e.activation(out=up_tmp[s2][:], in_=ps[bb], func=ACT.Copy,
                                                 scale=sux[:, cc:cc + 1]), sS)
                bankfree[bu] = (sS, m_upev[vhe])
                VE.wait(sS, m_gev[vhe])
                VE.wait(sS, m_upev[vhe])
                if vhe >= 3:
                    VE.wait(sVH[vhe % 3], sVH[vhe % 3].n)
                VE.wait(sV, sV.n)
                m_vhmul[vhe] = VE.op(lambda e, s2=vhe % 3:
                                     e.tensor_mul(out=vh_t[s2][:], in0=g_tmp[s2][:], in1=up_tmp[s2][:]), sV)
                VE.wait(sV, sV.n)
                VE.op(lambda e, s2=vhe % 3: e.reduce_max(out=stats[:, 20:21], in_=vh_t[s2][:], axis=AX,
                                                         apply_absolute_value=True), sV)
                VE.wait(sV, sV.n)
                if vhe == 0:
                    m_mh_ = VE.op(lambda e: e.tensor_copy(out=cols[:, C_MH:C_MH + 1],
                                                          in_=stats[:, 20:21]), sV)
                else:
                    m_mh_ = VE.op(lambda e: e.tensor_tensor(out=cols[:, C_MH:C_MH + 1],
                                                            in0=cols[:, C_MH:C_MH + 1],
                                                            in1=stats[:, 20:21], op=OP.max), sV)
                GP.wait(sV, m_mh_)
                m_vhst[vhe] = dma(GP, vh_hbm[128 * c:128 * (c + 1), 512 * n:512 * (n + 1)],
                                  vh_t[vhe % 3][:], sVH[vhe % 3])
                vhe_box[0] = vhe + 1
            return m_mh_

        issue_kw_load(0)
        issue_kw_load(1)
        issue_kw_load(2)
        emit_kw_prep(0)
        emit_kw_prep(1)
        for c in range(NI):
            issue_kw_load(c + 3)
            if c + 2 < NI:
                emit_kw_prep(c + 2)
            m_mh = emit_gateup_mm(c)

        GP.wait(sV, m_mh)
        dma(GP, mh_in, cols[:, C_MH:C_MH + 1], sSCB)
        GP.wait(sSCB, sSCB.n)
        GP.op(lambda e: e.collective_compute("AllReduce", OP.max, replica_groups=RG,
                                             ins=[mh_in], outs=[mh_out]), sCC)
        m_ccmh = sCC.n

        stB1.close()
        stW.close()
        barrier(exclude=(sCC,))
        stB2 = ExitStack()
        kdT = sb(stB2, "kdT", [128, NI * D], BF16)
        h_fm = sb(stB2, "h_fm", [128, NI * T], BF16)
        wd_st = sb(stB2, "wd_st", [128, I_FF], F32)
        wmx2 = sb(stB2, "wmx2", [128, 32], F32)
        qi2 = [sb(stB2, f"qi2{i}", [128, 1024], I32) for i in range(2)]
        qf2 = [sb(stB2, f"qf2{i}", [128, 1024], F32) for i in range(2)]
        vh_ld = [sb(stB2, f"vhld{i}", [128, 1024], F32) for i in range(3)]
        sdh_bc = sb(stB2, "sdh_bc", [128, D], F32)
        x2r_t = [sb(stB2, f"x2rt{i}", [128, D], F32) for i in range(3)]
        a2_t = [sb(stB2, f"a2t{i}", [128, D], F32) for i in range(2)]
        ep2 = [sb(stB2, f"ep2{i}", [128, 512], F32) for i in range(2)]

        fill(50, bank=4)
        m_kdq = {}
        m_kdtr = {}
        m_sdc = {}
        HW2 = 704
        for c in range(8):
            if c >= 1:
                SY.wait(sS, m_kdq[c - 1])
                SY.wait(sT, m_kdtr[c - 1])
            dma(SY, wd_st[:], d_wd[128 * c:128 * (c + 1), :], sWD)
            VE.wait(sWD, sWD.n)
            VE.wait(sV, sV.n)
            VE.op(lambda e, cc=c: e.reduce_max(out=wmx2[:, cc:cc + 1], in_=wd_st[:],
                                               axis=AX, apply_absolute_value=True), sV)
            VE.wait(sV, sV.n)
            VE.op(lambda e, cc=c: e.reciprocal(out=wmx2[:, 16:17], in_=wmx2[:, cc:cc + 1]), sV)
            VE.wait(sV, sV.n)
            VE.op(lambda e, cc=c: e.tensor_tensor(out=wmx2[:, 17:18], in0=wmx2[:, cc:cc + 1],
                                                  in1=wmx2[:, 16:17], op=OP.mult), sV)
            VE.wait(sV, sV.n)
            VE.op(lambda e: e.tensor_scalar(out=wmx2[:, 17:18], in0=wmx2[:, 17:18],
                                            scalar1=2.0, scalar2=-1.0,
                                            op0=OP.subtract, op1=OP.mult), sV)
            VE.wait(sV, sV.n)
            VE.op(lambda e: e.tensor_tensor(out=wmx2[:, 16:17], in0=wmx2[:, 16:17],
                                            in1=wmx2[:, 17:18], op=OP.mult), sV)
            VE.wait(sV, sV.n)
            VE.op(lambda e, cc=c: e.tensor_scalar_mul(wmx2[:, 24 + cc:25 + cc],
                                                      wmx2[:, 16:17], 127.0), sV)
            VE.wait(sV, sV.n)
            m_sdc[c] = VE.op(lambda e, cc=c: e.tensor_scalar_mul(wmx2[:, cc:cc + 1],
                                                                 wmx2[:, cc:cc + 1], 1.0 / 127.0), sV)
            SY.wait(sV, m_sdc[c])
            dma(SY, sd_hbm[0:1, 128 * c:128 * (c + 1)], wmx2[:, c:c + 1], sSD)
            for hf in range(4):
                sl = hf % 2
                pre = [] if (c < 1 or hf > 0) else [(sTRKD, sTRKD.n)]
                for w_ in pre:
                    SC.wait(*w_)
                VE.wait(sV, sV.n)
                VE.op(lambda e, cc=c, h2=hf, s2=sl:
                      e.tensor_scalar(out=qf2[s2][0:128, 0:HW2], in0=wd_st[0:128, HW2 * h2:HW2 * (h2 + 1)],
                                      scalar1=wmx2[:, 24 + cc:25 + cc], scalar2=-128.0,
                                      op0=OP.mult, op1=OP.max), sV)
                VE.wait(sV, sV.n)
                md2 = VE.op(lambda e, s2=sl: e.tensor_scalar(out=qi2[s2][0:128, 0:HW2],
                                                             in0=qf2[s2][0:128, 0:HW2],
                                                             scalar1=127.0, scalar2=None,
                                                             op0=OP.min), sV)
                SC.wait(sV, md2)
                if c >= 1 and hf == 0:
                    SC.wait(sT, m_kdtr[c - 1])
                SC.wait(sS, sS.n)
                m_kdq[c] = SC.op(lambda e, s2=sl, h2=hf: e.copy(out=wd_st[0:128, HW2 * h2:HW2 * (h2 + 1)],
                                                                in_=qi2[s2][0:128, 0:HW2]), sS)
            TE.wait(sS, m_kdq[c])
            te_transpose(kdT[:, 128 * c:], wd_st[:], NI, src_step=128, dst_step=D, banks=(4, 5, 6, 7))
            m_kdtr[c] = sT.n
            m_kdtr_ev = sS.n
            fill(20, bank=0)
        m_kd_tr = m_kdtr_ev

        GP.wait(sCC, m_ccmh)
        dma(GP, row128[0:1, :], mh_out.rearrange("p o -> o p"), sSCB)
        VE.wait(sSCB, sSCB.n)
        VE.wait(sV, sV.n)
        VE.op(lambda e: e.reduce_max(out=one1[0:1, O_M:O_M + 1], in_=row128[0:1, :], axis=AX), sV)
        SC.wait(sS, sS.n)
        m_sigd = SC.op(lambda e: e.activation(out=one1[0:1, O_SIG:O_SIG + 1],
                                              in_=one1[0:1, O_CLD:O_CLD + 1], func=ACT.Sigmoid), sS)
        VE.wait(sS, m_sigd)
        VE.wait(sV, sV.n)
        VE.op(lambda e: e.tensor_tensor(out=one1[0:1, O_SH:O_SH + 1], in0=one1[0:1, O_SIG:O_SIG + 1],
                                        in1=one1[0:1, O_M:O_M + 1], op=OP.mult), sV)
        VE.wait(sV, sV.n)
        VE.op(lambda e: e.tensor_scalar_mul(one1[0:1, O_SH:O_SH + 1], one1[0:1, O_SH:O_SH + 1],
                                            1.0 / 127.0), sV)
        m_rsh = recip_nr(one1[0:1, O_SH:O_SH + 1], one1[0:1, O_RSH:O_RSH + 1])
        GP.wait(sV, m_rsh)
        dma(GP, sc_hbm[0:1, 2:4], one1[0:1, O_SH:O_RSH + 1], sSCB)
        GP.wait(sSCB, sSCB.n)
        dma(GP, cols[:, C_SH:C_SH + 1], sc_hbm[0:1, 2:3].partition_broadcast(128), sSCB)
        dma(GP, cols[:, C_RSH:C_RSH + 1], sc_hbm[0:1, 3:4].partition_broadcast(128), sSCB)
        m_shc = sSCB.n

        SY.wait(sSD, sSD.n)
        dma(SY, sdh_bc[:], sd_hbm.partition_broadcast(128), sBC)
        VE.wait(sBC, sBC.n)
        VE.wait(sSCB, m_shc)
        VE.wait(sV, sV.n)
        m_sdh = VE.op(lambda e: e.tensor_scalar_mul(sdh_bc[:], sdh_bc[:], cols[:, C_SH:C_SH + 1]), sV)

        m_hq = {}
        for j0 in range(2):
            rq = SY if j0 % 2 == 0 else GP
            dma(rq, vh_ld[j0][:], vh_hbm[128 * (j0 // 2):128 * (j0 // 2) + 128,
                                         1024 * (j0 % 2):1024 * (j0 % 2) + 1024], sVHL[j0])
        for j in range(2 * NI):
            c, hf = divmod(j, 2)
            sl = j % 3
            if j + 2 < 2 * NI:
                s2 = (j + 2) % 3
                c3, h3 = divmod(j + 2, 2)
                rq = SY if (j + 2) % 2 == 0 else GP
                if j >= 1:
                    rq.wait(sS, m_hq[j - 1])
                dma(rq, vh_ld[s2][:], vh_hbm[128 * c3:128 * (c3 + 1), 1024 * h3:1024 * (h3 + 1)],
                    sVHL[s2])
            VE.wait(sVHL[sl], sVHL[sl].n)
            if j < 2:
                VE.wait(sSCB, m_shc)
            pre_ve = [(sS, m_hq[j - 2])] if j >= 2 else []
            m_hq[j] = quant(vh_ld[sl][:], cols[:, C_RSH:C_RSH + 1],
                            qi2[j % 2][0:128, 0:1024], qf2[j % 2][0:128, 0:1024],
                            h_fm[0:128, T * c + 1024 * hf:T * c + 1024 * (hf + 1)],
                            pre_ve_waits=pre_ve)
            TE.wait(sS, m_hq[j])
            fill(9, bank=6)
        m_hq_all = m_hq[2 * NI - 1]
        dbg_tap(13, h_fm[:, 0:T], T, sS, m_hq_all)
        dbg_tap(14, kdT[:, 0:2048], 2048, sS, m_kd_tr)
        dbg_tap(15, sdh_bc[:], D, sV, m_sdh)

        m_oev = {}
        m_ost = {}
        TE.wait(sS, m_hq_all)
        TE.wait(sS, m_kd_tr)
        for t in range(2):
            dma(SY, x2r_t[t][:], x2_hbm[128 * t:128 * (t + 1), :], sX2R[t])
        for t in range(NT):
            sl = t % 3
            gidx = t
            if t + 2 < NT:
                s2 = (t + 2) % 3
                if t >= 1:
                    SY.wait(sV, m_oev[t - 1])
                    SY.wait(sOUTs[s2], sOUTs[s2].n)
                dma(SY, x2r_t[s2][:], x2_hbm[128 * (t + 2):128 * (t + 3), :], sX2R[s2])
            VE.wait(sX2R[sl], sX2R[sl].n)
            VE.wait(sV, sV.n)
            m_x2n = VE.op(lambda e, s2=sl, gi=gidx:
                          e.tensor_scalar(out=x2r_t[s2][:], in0=x2r_t[s2][:],
                                          scalar1=cols2[:, gi:gi + 1], scalar2=cols2[:, 16 + gi:17 + gi],
                                          op0=OP.subtract, op1=OP.mult), sV)
            for hn in range(2):
                b = 2 * sl + hn
                te_bank(b)
                for cc in range(NI):
                    TE.op(lambda e, c2=cc, tt=t, hh=hn, bb=b:
                          e.matmul(ps[bb],
                                   h_fm[0:128, T * c2 + 128 * tt:T * c2 + 128 * (tt + 1)],
                                   kdT[0:128, D * c2 + 512 * hh:D * c2 + 512 * (hh + 1)],
                                   start=(c2 == 0), stop=(c2 == NI - 1)),
                          sT if cc == NI - 1 else None)
                m_dmm = sT.n
                VE.wait(sT, m_dmm)
                if t == 0 and hn == 0:
                    VE.wait(sV, m_sdh)
                VE.wait(sV, sV.n)
                VE.op(lambda e, bb=b, hh=hn:
                      e.tensor_mul(out=ep2[hh][:], in0=ps[bb],
                                   in1=sdh_bc[:, 512 * hh:512 * (hh + 1)]), sV)
                VE.wait(sV, max(sV.n, m_x2n))
                mep = VE.op(lambda e, s2=sl, a2=t % 2, hh=hn:
                            e.tensor_add(out=a2_t[a2][:, 512 * hh:512 * (hh + 1)],
                                         in0=x2r_t[s2][:, 512 * hh:512 * (hh + 1)], in1=ep2[hh][:]), sV)
                bankfree[b] = (sV, mep)
            ln_stats(a2_t[t % 2][:], stats[:, 14:15], stats[:, 15:16])
            VE.wait(sV, sV.n)
            m_oev[t] = VE.op(lambda e, s2=sl, a2=t % 2:
                             e.tensor_scalar(out=x2r_t[s2][:], in0=a2_t[a2][:],
                                             scalar1=stats[:, 14:15],
                                             scalar2=stats[:, 15:16],
                                             op0=OP.subtract, op1=OP.mult), sV)
            GP.wait(sV, m_oev[t])
            m_ost[t] = dma(GP, d_out[128 * t:128 * (t + 1), :], x2r_t[sl][:], sOUTs[sl])

        GP.wait(sOUTs[0], sOUTs[0].n)
        GP.wait(sOUTs[1], sOUTs[1].n)
        GP.wait(sOUTs[2], sOUTs[2].n)
        stB2.close()

        blk = st0.enter_context(nc.Block())

        def realize(rec):
            def runner(eng):
                for it in rec.items:
                    if it[0] == "w":
                        eng.wait_ge(it[1].h, it[2])
                    else:
                        instr = it[1](eng)
                        if it[2] is not None:
                            instr.then_inc(it[2].h, it[3])
            return runner

        blk.sync(realize(SY))
        blk.vector(realize(VE))
        blk.scalar(realize(SC))
        blk.tensor(realize(TE))
        blk.gpsimd(realize(GP))

    return nc


_NC_CACHE = [None]


def kernel(L0_attn_x, L0_attn_ln_g, L0_attn_ln_b, L0_attn_W_in, L0_attn_b_in,
           L0_attn_W_out, L0_attn_b_out, L0_attn_gamma_q, L0_attn_beta_q,
           L0_attn_gamma_k, L0_attn_beta_k, L0_ffn_clip_x, L0_ffn_clip_down,
           L0_ffn_w_gate, L0_ffn_w_up, L0_ffn_w_down):
    f32 = np.float32
    xr = np.ascontiguousarray(np.asarray(L0_attn_x, dtype=f32).reshape(L, D))
    shared = {
        "ln_g": np.asarray(L0_attn_ln_g, f32).reshape(1, D),
        "ln_b": np.asarray(L0_attn_ln_b, f32).reshape(1, D),
        "W_in": np.ascontiguousarray(np.asarray(L0_attn_W_in, f32)),
        "b_in": np.asarray(L0_attn_b_in, f32).reshape(1, F_IN),
        "W_out": np.ascontiguousarray(np.asarray(L0_attn_W_out, f32)),
        "b_out": np.asarray(L0_attn_b_out, f32).reshape(1, D),
        "gq": np.ascontiguousarray(np.asarray(L0_attn_gamma_q, f32).reshape(S, 1)),
        "bq": np.ascontiguousarray(np.asarray(L0_attn_beta_q, f32).reshape(S, 1)),
        "gk": np.ascontiguousarray(np.asarray(L0_attn_gamma_k, f32).reshape(S, 1)),
        "bk": np.ascontiguousarray(np.asarray(L0_attn_beta_k, f32).reshape(S, 1)),
        "clip_x": np.asarray(L0_ffn_clip_x, f32).reshape(1, 1),
        "clip_d": np.asarray(L0_ffn_clip_down, f32).reshape(1, 1),
        "w_gate": np.ascontiguousarray(np.asarray(L0_ffn_w_gate, f32)),
        "w_up": np.ascontiguousarray(np.asarray(L0_ffn_w_up, f32)),
        "w_down": np.ascontiguousarray(np.asarray(L0_ffn_w_down, f32)),
        "ident": np.eye(128, dtype=f32),
    }
    if _NC_CACHE[0] is None:
        _NC_CACHE[0] = build_program()
    nc = _NC_CACHE[0]
    in_maps = []
    for c in range(NCORES):
        m = dict(shared)
        m["x"] = np.ascontiguousarray(xr[T * c:T * (c + 1)])
        in_maps.append(m)
    res = run_bass_kernel_spmd(nc, in_maps, core_ids=list(range(NCORES)))
    out = np.concatenate([res.results[c]["out"] for c in range(NCORES)], axis=0)
    return out.reshape(1, L, D).astype(np.float32)
